# version 09
# baseline (speedup 1.0000x reference)
import numpy as np
import ml_dtypes
from contextlib import ExitStack
import concourse.bass as bass
import concourse.mybir as mybir
from concourse.bass_utils import run_bass_kernel_spmd

F32 = mybir.dt.float32
BF16 = mybir.dt.bfloat16
AF = mybir.ActivationFunctionType
OP = mybir.AluOpType
AX = mybir.AxisListType.X

T = 4096
D = 1024
NT = T // 128
NSL = T // 512
NE = 8
FD = 4096
HID = 2730
NEG = -1.0e30


def host_constants():
    c = {}
    c["identbf"] = np.eye(128, dtype=ml_dtypes.bfloat16)
    c["identf"] = np.eye(128, dtype=np.float32)
    m = np.zeros((4, 128, 512), np.float32)
    for i in range(4):
        p = np.arange(128)[:, None]
        cc = np.arange(512)[None, :]
        m[i] = np.where(cc >= p + 128 * i, 0.0, NEG)
    c["causal4"] = m
    fm = np.zeros((2, 128, 2048), np.float32)
    for g in range(2):
        for kk in range(4):
            kt = 4 * g + kk
            jp = kt * 128 + np.arange(128)[:, None]
            tp = np.arange(512)[None, :]
            valid = (tp <= jp) & (jp <= tp + 512)
            fm[g][:, kk * 512:(kk + 1) * 512] = np.where(valid, 0.0, NEG)
    c["ffnmask"] = fm
    half = np.arange(0, 64, 2, dtype=np.float64)
    invf = 1.0 / 10000.0 ** (half / 64)
    t = np.arange(T, dtype=np.float64)
    fr = t[:, None] * invf[None, :]
    base = (half + 0.4 * 64) / (1.4 * 64)
    power = (t - T // 2) / 256.0
    sc = base[None, :] ** power[:, None]
    cos, sin = np.cos(fr), np.sin(fr)
    def two(a):
        return np.concatenate([a, a], 1).T.astype(np.float32).copy()
    c["xq_c"] = two(cos * sc)
    c["xq_s"] = two(sin * sc)
    c["xk_c"] = two(cos / sc)
    c["xk_s"] = two(sin / sc)
    gam = np.stack([base ** (-(w * 512 + 256 - T // 2) / 256.0)
                    for w in range(NSL)], 1)
    gam = np.concatenate([gam, gam], 0)
    c["gamq8"] = (gam * 8.0).astype(np.float32)
    c["gamki"] = (1.0 / gam).astype(np.float32)
    b2 = np.concatenate([base ** 2, base ** 2], 0)[:, None]
    c["gamrr"] = b2.astype(np.float32)
    return c

CONSTS = None
DEBUG_A0 = False
OB_INTERNAL = True



SCRATCH = {
    "XNF":   ([T, D], F32, "ln0", False),
    "XNBP":  ([T + 8, D], BF16, "ln0", False),
    "ATT0":  ([T, D], F32, "l0_attn", False),
    "X1":    ([T, D], F32, "l0_ffn", False),
    "XN1F":  ([T, D], F32, "l0_ffn", False),
    "XN1BP": ([T + 8, D], BF16, "l0_ffn", False),
    "XHATB": ([T, D], BF16, "l0_ffn", False),
    "Y1F":   ([T, D], F32, "l0_ffn", True),
    "Y1BP":  ([T + 8, D], BF16, "l0_ffn", True),
    "YMBP":  ([T + 8, D], BF16, "l0_ffn", True),
    "ATT1":  ([T, D], F32, "l1_attn", False),
    "MGATE": ([T, 8], F32, "l1_attn", False),
    "OB":    ([D, T], BF16, "l1_attn", True),
    "W1B":   ([NE, FD, D], BF16, "wprep", False),
    "W2B":   ([NE, D, FD], BF16, "wprep", False),
    "CENB":  ([16, 1024], BF16, "wprep", False),
    "FQKB":  ([2, 1024, 1536], BF16, "wprep", False),
    "FWOB":  ([2, 512, 1024], BF16, "wprep", False),
    "FW1B":  ([2, 1024, 5460], BF16, "wprep", False),
    "FW2B":  ([2, 2730, 1024], BF16, "wprep", False),
    "ACCD":  ([T, D], F32, "moe", False),
    "DBG0":  ([128, 8192], F32, "DBGPH", False),
    "DBG1":  ([64, T], F32, "DBGPH", False),
    "DBG2":  ([64, T], F32, "DBGPH", False),
    "DBG3":  ([128, 4096], F32, "DBGPH", False),
}
CONSUMES = {
    "wprep":   [],
    "ln0":     [],
    "l0_attn": ["XNF", "XNBP"],
    "l0_ffn":  ["XNF", "XNBP", "ATT0", "FQKB", "FWOB", "FW1B", "FW2B"],
    "l1_attn": ["XN1BP", "CENB"],
    "moe":     ["XHATB", "W1B", "W2B", "MGATE"],
    "comb":    ["ACCD", "X1", "ATT1", "XN1F"],
}
INPUT_SPECS = {
    "x":             ([T, D], F32),
    "L0_attn_WQ":    ([16, 1024, 64], F32),
    "L0_attn_WK":    ([16, 1024, 64], F32),
    "L0_attn_WV_down": ([16, 1024, 64], F32),
    "L0_attn_WV_up": ([16, 64, 1024], F32),
    "L0_ffn_wqkv":   ([2, 1024, 1536], F32),
    "L0_ffn_q_scale": ([2, 64], F32),
    "L0_ffn_k_scale": ([2, 64], F32),
    "L0_ffn_wo":     ([2, 512, 1024], F32),
    "L0_ffn_ln_g":   ([2, 1024], F32),
    "L0_ffn_ln_b":   ([2, 1024], F32),
    "L0_ffn_w1":     ([2, 1024, 5460], F32),
    "L0_ffn_b1":     ([2, 5460], F32),
    "L0_ffn_w2":     ([2, 2730, 1024], F32),
    "L0_ffn_b2":     ([2, 1024], F32),
    "L1_attn_w_qkv": ([1024, 3072], F32),
    "L1_attn_b_qkv": ([3072], F32),
    "L1_attn_w_out": ([1024, 1024], F32),
    "L1_attn_b_out": ([1024], F32),
    "L1_ffn_centroids": ([8, 1024], F32),
    "L1_ffn_ln_g":   ([8, 1024], F32),
    "L1_ffn_ln_b":   ([8, 1024], F32),
    "L1_ffn_w1":     ([8, 4096, 1024], F32),
    "L1_ffn_b1":     ([8, 4096], F32),
    "L1_ffn_w2":     ([8, 1024, 4096], F32),
    "L1_ffn_b2":     ([8, 1024], F32),
}
CONST_SPECS = {
    "identbf": ([128, 128], BF16),
    "identf":  ([128, 128], F32),
    "causal4": ([4, 128, 512], BF16),
    "ffnmask": ([2, 128, 2048], BF16),
    "xq_c":    ([64, T], F32),
    "xq_s":    ([64, T], F32),
    "xk_c":    ([64, T], F32),
    "xk_s":    ([64, T], F32),
    "gamq8":   ([64, NSL], F32),
    "gamki":   ([64, NSL], F32),
    "gamrr":   ([64, 1], F32),
}
WANTS = {
    "wprep":   ["L1_ffn_w1", "L1_ffn_w2", "L1_ffn_centroids",
                "L0_ffn_wqkv", "L0_ffn_wo", "L0_ffn_w1", "L0_ffn_w2"],
    "ln0":     ["x"],
    "l0_attn": ["L0_attn_WQ", "L0_attn_WK", "L0_attn_WV_down", "L0_attn_WV_up",
                "causal4", "identf", "identbf"],
    "l0_ffn":  ["x", "L0_ffn_wqkv", "L0_ffn_q_scale", "L0_ffn_k_scale",
                "L0_ffn_wo", "L0_ffn_ln_g", "L0_ffn_ln_b", "L0_ffn_w1",
                "L0_ffn_b1", "L0_ffn_w2", "L0_ffn_b2", "identf",
                "ffnmask", "xq_c", "xq_s", "xk_c", "xk_s", "gamq8", "gamki",
                "gamrr"],
    "l1_attn": ["L1_attn_w_qkv", "L1_attn_b_qkv", "L1_attn_w_out",
                "L1_attn_b_out", "causal4", "identbf"],
    "moe":     ["L1_ffn_ln_g", "L1_ffn_ln_b", "L1_ffn_b1", "L1_ffn_b2",
                "identf"],
    "comb":    [],
}


class IO:
    def __init__(self, nc, phases, fused):
        self.t = {}
        wanted = set()
        for ph in phases:
            wanted |= set(WANTS[ph])
        self.wanted_inputs = set()
        for n, (shape, dt) in {**INPUT_SPECS, **CONST_SPECS}.items():
            if n in wanted:
                self.t[n] = nc.dram_tensor(n, shape, dt, kind="ExternalInput")
                self.wanted_inputs.add(n)
        produced = {n for n, (_, _, p, _) in SCRATCH.items()
                    if p in phases or (p == "DBGPH" and DEBUG_A0)}
        consumed = set()
        for ph in phases:
            consumed |= set(CONSUMES[ph])
        self.outputs = []
        for n, (shape, dt, prod, internal) in SCRATCH.items():
            if n == "OB":
                internal = OB_INTERNAL
            if n.startswith("DBG") and not DEBUG_A0:
                continue
            if n in produced:
                if fused or internal:
                    kind = "Internal"
                else:
                    kind = "ExternalOutput"
                    self.outputs.append(n)
            elif n in consumed:
                kind = "ExternalInput"
                self.wanted_inputs.add(n)
            else:
                continue
            self.t[n] = nc.dram_tensor(n, shape, dt, kind=kind)
        if "comb" in phases:
            self.t["OUT"] = nc.dram_tensor("OUT", [T, D], F32, kind="ExternalOutput")
            self.outputs.append("OUT")

    def __getattr__(self, n):
        try:
            return self.t[n].ap()
        except KeyError:
            raise AttributeError(n)


class Sems:
    def __init__(self, nc, stack, tag):
        self.nc, self.stack, self.tag = nc, stack, tag
        self.h = {}
        self.c = {}
    def hdl(self, name):
        if name not in self.h:
            self.h[name] = self.stack.enter_context(
                self.nc.semaphore(f"{self.tag}_{name}"))
            self.c[name] = 0
        return self.h[name]


ENGINES = ["sync", "gpsimd", "tensor", "vector", "scalar"]


class Bld:
    def __init__(self, nc, s):
        self.nc, self.s = nc, s
        self.q = {e: [] for e in ENGINES}
        self.seen = {e: {} for e in ENGINES}

    def op(self, eng, fn, inc=None, incby=1, waits=()):
        s = self.s
        tok = None
        if inc is not None:
            s.hdl(inc)
            s.c[inc] += incby
            tok = (inc, s.c[inc])
        wl = [w for w in waits if w is not None]
        seen = self.seen[eng]
        need = []
        for wn, wv in wl:
            if seen.get(wn, 0) < wv:
                need.append((wn, wv))
                seen[wn] = wv
        def emit(e, need=need, fn=fn, inc=inc, incby=incby):
            for wn, wv in need:
                e.wait_ge(s.hdl(wn), wv)
            ins = fn(e)
            if inc is not None:
                assert ins is not None
                ins.then_inc(s.hdl(inc), incby)
        self.q[eng].append(emit)
        return tok

    def dma(self, eng, out, in_, stream, waits=(), **kw):
        return self.op(eng, lambda e: e.dma_start(out=out, in_=in_, **kw),
                       inc=stream, incby=16, waits=waits)

    def barrier_wait(self, eng, toks):
        self.op(eng, lambda e: None, waits=toks)

    def finish(self, block):
        for name in ENGINES:
            ops = self.q[name]
            if not ops:
                continue
            def body(eng, ops=ops):
                for f in ops:
                    f(eng)
            getattr(block, name)(body)



def ph_wprep(nc, io, stack):
    s = Sems(nc, stack, "wp")
    NW1 = NE * (FD // 128)
    NW2 = NE * (D // 128)
    JOBS = []
    for e in range(NE):
        for r in range(FD // 128):
            JOBS.append((lambda io, e=e, r=r: (
                io.L1_ffn_w1[e, r * 128:(r + 1) * 128, :],
                io.W1B[e, r * 128:(r + 1) * 128, :], 128, D)))
    for e in range(NE):
        for r in range(D // 128):
            JOBS.append((lambda io, e=e, r=r: (
                io.L1_ffn_w2[e, r * 128:(r + 1) * 128, :],
                io.W2B[e, r * 128:(r + 1) * 128, :], 128, FD)))
    JOBS.append(lambda io: (io.L1_ffn_centroids, io.CENB[0:8, :], 8, 1024))
    for l in range(2):
        for r in range(8):
            JOBS.append((lambda io, l=l, r=r: (
                io.L0_ffn_wqkv[l, r * 128:(r + 1) * 128, :],
                io.FQKB[l, r * 128:(r + 1) * 128, :], 128, 1536)))
        for r in range(4):
            JOBS.append((lambda io, l=l, r=r: (
                io.L0_ffn_wo[l, r * 128:(r + 1) * 128, :],
                io.FWOB[l, r * 128:(r + 1) * 128, :], 128, 1024)))
        for r in range(8):
            JOBS.append((lambda io, l=l, r=r: (
                io.L0_ffn_w1[l, r * 128:(r + 1) * 128, :],
                io.FW1B[l, r * 128:(r + 1) * 128, :], 128, 5460)))
        for r in range(22):
            rows = 128 if r < 21 else 42
            JOBS.append((lambda io, l=l, r=r, rows=rows: (
                io.L0_ffn_w2[l, r * 128:r * 128 + rows, :],
                io.FW2B[l, r * 128:r * 128 + rows, :], rows, 1024)))
    NLD = len(JOBS)
    NBW = 4
    with (
        nc.sbuf_tensor("wp_a", [128, NBW * 5460], BF16) as wa,
        nc.Block() as block,
    ):
        @block.gpsimd
        def _(g):
            for i in range(NLD):
                k = i % NBW
                a, _, rows, w = JOBS[i](io)
                if i >= NBW:
                    g.wait_ge(s.hdl(f"st{k}"), 16 * (i // NBW))
                g.dma_start(out=wa[0:rows, k * 5460:k * 5460 + w], in_=a) \
                    .then_inc(s.hdl(f"ld{k}"), 16)

        @block.sync
        def _(sy):
            for i in range(NLD):
                k = i % NBW
                _, d, rows, w = JOBS[i](io)
                sy.wait_ge(s.hdl(f"ld{k}"), 16 * (i // NBW + 1))
                sy.dma_start(out=d, in_=wa[0:rows, k * 5460:k * 5460 + w]) \
                    .then_inc(s.hdl(f"st{k}"), 16)
            for k in range(NBW):
                sy.wait_ge(s.hdl(f"st{k}"),
                           16 * ((NLD - k + NBW - 1) // NBW))


def ph_ln0(nc, io, stack):
    s = Sems(nc, stack, "l0")
    nb = 2
    with (
        nc.sbuf_tensor("ln_x", [128, nb * D], F32) as xb,
        nc.sbuf_tensor("ln_o", [128, nb * D], F32) as ob,
        nc.sbuf_tensor("ln_sq", [128, nb * D], F32) as sqb,
        nc.sbuf_tensor("ln_st", [128, nb * 8], F32) as st,
        nc.sbuf_tensor("ln_eps", [128, 1], F32) as eps,
        nc.sbuf_tensor("ln_z", [1, D], BF16) as zrow,
        nc.Block() as block,
    ):
        VC = 5
        AC = 3
        @block.vector
        def _(v):
            v.memset(eps[:, :], 1e-5)
            v.memset(zrow[:, :], 0.0).then_inc(s.hdl("aux"), 1)
            for i in range(NT):
                k = i % nb
                X = xb[:, k * D:(k + 1) * D]
                O = ob[:, k * D:(k + 1) * D]
                SQ = sqb[:, k * D:(k + 1) * D]
                ns = st[:, k * 8 + 0:k * 8 + 1]
                mn = st[:, k * 8 + 1:k * 8 + 2]
                rstd = st[:, k * 8 + 3:k * 8 + 4]
                c = VC * i
                v.wait_ge(s.hdl(f"ld{k}"), 16 * (i // nb + 1))
                if i >= nb:
                    v.wait_ge(s.hdl(f"wr{k}"), 32 * (i // nb))
                v.tensor_reduce(ns, X, AX, OP.add, negate=True) \
                    .then_inc(s.hdl("vc"), 1)
                v.wait_ge(s.hdl("vc"), c + 1)
                v.tensor_scalar(mn, ns, 1.0 / D, None, op0=OP.mult) \
                    .then_inc(s.hdl("vc"), 1)
                v.wait_ge(s.hdl("vc"), c + 2)
                v.tensor_scalar(O, X, mn, None, op0=OP.add) \
                    .then_inc(s.hdl("vc"), 1)
                v.wait_ge(s.hdl("ac"), AC * i + 1)
                v.tensor_reduce(rstd, SQ, AX, OP.add) \
                    .then_inc(s.hdl("vc"), 1)
                v.wait_ge(s.hdl("ac"), AC * i + 3)
                v.tensor_scalar(O, O, rstd, None, op0=OP.mult) \
                    .then_inc(s.hdl("vc"), 1)

        @block.scalar
        def _(sc):
            for i in range(NT):
                k = i % nb
                O = ob[:, k * D:(k + 1) * D]
                SQ = sqb[:, k * D:(k + 1) * D]
                rstd = st[:, k * 8 + 3:k * 8 + 4]
                c = VC * i
                sc.wait_ge(s.hdl("vc"), c + 3)
                sc.activation(SQ, O, AF.Square).then_inc(s.hdl("ac"), 1)
                sc.wait_ge(s.hdl("vc"), c + 4)
                sc.activation(rstd, rstd, AF.Ln, bias=eps[:, 0:1], scale=1.0 / D) \
                    .then_inc(s.hdl("ac"), 1)
                sc.wait_ge(s.hdl("ac"), AC * i + 2)
                sc.activation(rstd, rstd, AF.Exp, scale=-0.5) \
                    .then_inc(s.hdl("ac"), 1)

        @block.sync
        def _(sy):
            for i in range(NT):
                k = i % nb
                X = xb[:, k * D:(k + 1) * D]
                if i >= nb:
                    sy.wait_ge(s.hdl("vc"), VC * (i - nb) + 3)
                sy.dma_start(out=X, in_=io.x[i * 128:(i + 1) * 128, :]) \
                    .then_inc(s.hdl(f"ld{k}"), 16)

        @block.gpsimd
        def _(g):
            g.wait_ge(s.hdl("aux"), 1)
            g.dma_start(out=io.XNBP[0:1, :], in_=zrow[:, :]).then_inc(s.hdl("zr"), 16)
            for i in range(NT):
                k = i % nb
                O = ob[:, k * D:(k + 1) * D]
                g.wait_ge(s.hdl("vc"), VC * i + 5)
                g.dma_start(out=io.XNF[i * 128:(i + 1) * 128, :], in_=O) \
                    .then_inc(s.hdl(f"wr{k}"), 16)
                g.dma_start(out=io.XNBP[1 + i * 128:1 + (i + 1) * 128, :], in_=O) \
                    .then_inc(s.hdl(f"wr{k}"), 16)
            g.wait_ge(s.hdl("wr0"), 32 * ((NT + 1) // 2))
            g.wait_ge(s.hdl("wr1"), 32 * (NT // 2))
            g.wait_ge(s.hdl("zr"), 16)


PHASE_FN = {
    "wprep": ph_wprep,
    "ln0": ph_ln0,
}


def build_program(phases, fused):
    nc = bass.Bass("TRN2", target_bir_lowering=False)
    with ExitStack() as stack:
        io = IO(nc, phases, fused)
        for ph in phases:
            PHASE_FN[ph](nc, io, stack)
    return nc, io



def kernel_run(inputs, groups, core_ids=None, trace=False, fused=None):
    global CONSTS
    if CONSTS is None:
        CONSTS = host_constants()
    if core_ids is None:
        core_ids = list(range(8))
    ncore = len(core_ids)
    if fused is None:
        fused = len(groups) == 1 and len(groups[0]) == len(PHASE_FN)
    x = np.ascontiguousarray(np.asarray(inputs["L0_attn_inputs"], np.float32))
    base = {}
    for n in INPUT_SPECS:
        if n == "x":
            continue
        base[n] = np.ascontiguousarray(np.asarray(inputs[n], np.float32))
    np_dt = {F32: np.float32, BF16: ml_dtypes.bfloat16}
    for n, (shape, dt) in CONST_SPECS.items():
        base[n] = np.ascontiguousarray(CONSTS[n].astype(np_dt[dt]))
    pool = [dict() for _ in range(ncore)]
    res = None
    for phases in groups:
        nc, io = build_program(phases, fused)
        in_maps = []
        for ci in range(ncore):
            m = {}
            for k in io.wanted_inputs:
                if k == "x":
                    m["x"] = x[ci]
                elif k in base:
                    m[k] = base[k]
                else:
                    m[k] = pool[ci][k]
            in_maps.append(m)
        res = run_bass_kernel_spmd(nc, in_maps, core_ids=core_ids, trace=trace)
        for ci in range(ncore):
            for k, v in res.results[ci].items():
                pool[ci][k] = v
    return pool, res


def ph_l0_attn(nc, io, stack):
    s = Sems(nc, stack, "a0")
    NH = 16
    with ExitStack() as ls:
        b = Bld(nc, s)
        sb = lambda name, shape, dt: ls.enter_context(nc.sbuf_tensor(name, shape, dt))
        psa = lambda name, shape, dt: ls.enter_context(nc.psum_tensor(name, shape, dt))

        XN = sb("a0_xn", [128, 8 * T], BF16)
        WQK = sb("a0_wqk", [128, 2 * 1024], BF16)
        Q = sb("a0_q", [64, T], BF16)
        K = sb("a0_k", [64, T], BF16)
        DT = sb("a0_dt", [64, T], BF16)
        ONE64B = sb("a0_one64b", [64, 1], BF16)
        DG = sb("a0_dg", [1, T], F32)
        PS = sb("a0_pscr", [128, 2048], BF16)
        LP = sb("a0_lp", [128, 2], F32)
        LC = sb("a0_lc", [128, 32], F32)
        DE = sb("a0_de", [128, 32], F32)
        RL = sb("a0_rl", [128, 32], F32)
        DCOL = sb("a0_dcol", [128, NH * 32], F32)
        NRM = sb("a0_nrm", [64, 4], F32)
        SC0 = sb("a0_sc0", [1, 8], F32)
        NC0 = sb("a0_nc0", [128, 2], F32)
        ONE64 = sb("a0_one64", [64, 1], F32)
        ONE128 = sb("a0_one128", [1, 128], F32)
        CAUS = sb("a0_caus", [128, 2048], BF16)
        IDF = sb("a0_idf", [128, 128], F32)
        IDB = sb("a0_idb", [128, 128], BF16)
        WVD = sb("a0_wvd", [128, 8 * 1024], BF16)
        WVU = sb("a0_wvu", [128, 8 * 1024], BF16)
        DS = sb("a0_ds", [128, 512], BF16)
        DP = sb("a0_dp", [128, 8 * 512], BF16)
        XT = sb("a0_xt", [128, 4 * 1024], F32)
        AT = sb("a0_at", [128, 2 * 1024], F32)

        pj = [psa("a0_pj0", [128, 512], F32), psa("a0_pj1", [128, 512], F32)]
        scp = psa("a0_scp", [128, 2048], F32)
        msc = psa("a0_msc", [128, 512], F32)
        tpp = psa("a0_tp", [64, 1024], BF16)

        with nc.Block() as block:
            t_xn = None
            for c in range(8):
                t_xn = b.dma("sync", XN[:, c * T:(c + 1) * T],
                             io.XNBP[1:T + 1, c * 128:(c + 1) * 128],
                             "xn", transpose=True)
            t_caus = None
            for i in range(4):
                t_caus = b.dma("sync", CAUS[:, i * 512:(i + 1) * 512],
                               io.causal4[i], "xn")
            t_idf = b.dma("sync", IDF[:, :], io.identf, "xn")
            t_idb = b.dma("sync", IDB[:, :], io.identbf, "xn")
            t_ones = b.op("vector", lambda e: (e.memset(ONE64[:, :], 1.0),
                                               e.memset(ONE64B[:, :], 1.0),
                                               e.memset(ONE128[:, :], 1.0))[-1],
                          inc="vc")
            t_wvd = None
            for n in range(NH):
                t_wvd = b.dma(
                    "gpsimd",
                    WVD.ap().rearrange("p (c n d) -> p c n d", c=8, n=NH)[:, :, n, :],
                    io.L0_attn_WV_down[n].rearrange("(c p) d -> p c d", p=128), "wv")
            t_wvu = b.dma(
                "gpsimd",
                WVU.ap().rearrange("p (c e) -> p c e", c=8),
                io.L0_attn_WV_up.rearrange("n dd e -> (n dd) e")
                                .rearrange("(c p) e -> p c e", p=128), "wv")

            headA_done = {}
            t_wqk = {}

            def load_wqk(n):
                k2 = n % 2
                waits = []
                if n >= 2:
                    waits.append(headA_done[n - 2])
                base = WQK.ap()[:, k2 * 1024:(k2 + 1) * 1024] \
                    .rearrange("p (c w) -> p c w", c=8)
                b.dma("gpsimd", base[:, :, 0:64],
                      io.L0_attn_WQ[n].rearrange("(c p) d -> p c d", p=128),
                      f"wq{k2}", waits=waits)
                return b.dma("gpsimd", base[:, :, 64:128],
                             io.L0_attn_WK[n].rearrange("(c p) d -> p c d", p=128),
                             f"wq{k2}")

            t_prev_expfree = None
            t_prev_mscfree = None
            t_prev_diag_evac = None
            t_qk_free = []
            t_prev_t17 = None
            t_prev_t18 = None
            t_wqk[0] = load_wqk(0)
            for n in range(NH):
                k2 = n % 2
                t_evac = {}
                t_mm = None
                for sl in range(NSL):
                    p = pj[sl % 2]
                    waits = [t_wqk[n], t_xn]
                    if sl >= 2:
                        waits.append(t_evac[sl - 2])
                    if t_prev_diag_evac is not None:
                        waits.append(t_prev_diag_evac)
                    def mmA(e, p=p, sl=sl, k2=k2):
                        ins = None
                        for c in range(8):
                            ins = e.matmul(
                                p[:, :],
                                WQK[:, k2 * 1024 + c * 128:k2 * 1024 + (c + 1) * 128],
                                XN[:, c * T + sl * 512:c * T + (sl + 1) * 512],
                                start=(c == 0), stop=(c == 7))
                        return ins
                    t_mm = b.op("tensor", mmA, inc="tn", waits=waits)
                    ew = [t_mm] + (t_qk_free if sl == 0 else [])
                    b.op("scalar", lambda e, p=p, sl=sl: e.activation(
                        Q[:, sl * 512:(sl + 1) * 512], p[0:64, :],
                        AF.Copy, scale=0.125), waits=ew)
                    t_evac[sl] = b.op("scalar",
                        lambda e, p=p, sl=sl: e.activation(
                            K[:, sl * 512:(sl + 1) * 512],
                            p[64:128, :], AF.Copy),
                        inc="ac", waits=[t_mm])
                headA_done[n] = t_mm
                t_qk = t_evac[NSL - 1]
                if n + 1 < NH:
                    t_wqk[n + 1] = load_wqk(n + 1)

                t1 = b.op("vector", lambda e: e.tensor_reduce(
                    NRM[:, 0:1], Q[:, :], AX, OP.max,
                    apply_absolute_value=True),
                    waits=[t_qk])
                t2 = b.op("vector", lambda e: e.tensor_reduce(
                    NRM[:, 1:2], K[:, :], AX, OP.max,
                    apply_absolute_value=True),
                    inc="vc", waits=[t_qk])
                t3 = b.op("scalar", lambda e: e.activation(
                    NRM[:, 2:4], NRM[:, 0:2], AF.Square), inc="ac", waits=[t2])
                waits = [t3, t_ones]
                if t_prev_mscfree is not None:
                    waits.append(t_prev_mscfree)
                t4 = b.op("tensor", lambda e: e.matmul(
                    msc[0:1, 0:2], ONE64[:, :], NRM[:, 2:4], start=True, stop=True),
                    inc="tn", waits=waits)
                t5 = b.op("vector", lambda e: e.tensor_copy(SC0[0:1, 0:2],
                                                            msc[0:1, 0:2]),
                          inc="vc", waits=[t4])
                t6 = b.op("vector", lambda e: e.tensor_mul(
                    SC0[0:1, 2:3], SC0[0:1, 0:1], SC0[0:1, 1:2]),
                    inc="vc", waits=[t5])
                t7 = b.op("scalar", lambda e: e.activation(
                    SC0[0:1, 3:4], SC0[0:1, 2:3], AF.Ln), inc="ac", waits=[t6])
                t8 = b.op("scalar", lambda e: e.activation(
                    SC0[0:1, 4:5], SC0[0:1, 3:4], AF.Exp, scale=0.5),
                    inc="ac", waits=[t7])
                t9 = b.op("vector", lambda e: e.tensor_scalar(
                    SC0[0:1, 5:6], SC0[0:1, 4:5], -1.0, None, op0=OP.mult),
                    inc="vc", waits=[t8])
                t9b = b.op("vector", lambda e: e.tensor_scalar(
                    SC0[0:1, 6:7], SC0[0:1, 4:5], -1.0, -2.772588722239781,
                    op0=OP.mult, op1=OP.add), inc="vc", waits=[t8])
                t10 = b.op("tensor", lambda e: (
                    e.matmul(msc[:, 4:5], ONE128[:, :], SC0[0:1, 5:6],
                             start=True, stop=True),
                    e.matmul(msc[:, 5:6], ONE128[:, :], SC0[0:1, 6:7],
                             start=True, stop=True))[-1],
                    inc="tn", waits=[t9, t9b, t5])
                t11 = b.op("vector", lambda e: e.tensor_copy(NC0[:, 0:2],
                                                             msc[:, 4:6]),
                           inc="vc", waits=[t10])

                t12 = b.op("vector", lambda e: e.tensor_tensor(
                    DT[:, :], Q[:, :], K[:, :], op=OP.mult),
                    inc="vc", waits=[t_qk])

                t_lc = {}
                tmm = None
                for j in range(NT):
                    g0 = j // 16
                    t_exp = {}
                    for g in range(g0, 2):
                        b0 = max(0, j // 4 - 4 * g)
                        waits = [t_qk]
                        if t_prev_expfree is not None:
                            waits.append(t_prev_expfree)
                        def mmS(e, j=j, g=g, b0=b0):
                            ins = None
                            for bb in range(b0, 4):
                                ins = e.matmul(
                                    scp[:, bb * 512:(bb + 1) * 512],
                                    K[:, j * 128:(j + 1) * 128],
                                    Q[:, (4 * g + bb) * 512:
                                      (4 * g + bb + 1) * 512],
                                    start=True, stop=True)
                            return ins
                        tmm = b.op("tensor", mmS, inc="tn", waits=waits)
                        tmask = None
                        if 4 * g <= j // 4 <= 4 * g + 3:
                            bb = j // 4 - 4 * g
                            tmask = b.op("vector",
                                lambda e, bb=bb, j=j: e.scalar_tensor_tensor(
                                    scp[:, bb * 512:(bb + 1) * 512],
                                    scp[:, bb * 512:(bb + 1) * 512], 1.0,
                                    CAUS[:, (j % 4) * 512:(j % 4 + 1) * 512],
                                    op0=OP.mult, op1=OP.add),
                                inc="vc", waits=[tmm, t_caus])
                        ew = [tmm, t11]
                        if tmask is not None:
                            ew.append(tmask)
                        if j >= 1:
                            ew.append(t_lc.get(j - 1))
                        t_exp[g] = b.op("scalar", lambda e, g=g, b0=b0: e.activation(
                            PS[:, 0:(4 - b0) * 512], scp[:, b0 * 512:2048], AF.Exp,
                            bias=NC0[:, 0:1], accum_out=LP[:, g:g + 1]),
                            inc="ac", waits=ew)
                        t_prev_expfree = t_exp[g]
                    if g0 == 0:
                        t_lc[j] = b.op("vector", lambda e, j=j: e.tensor_add(
                            LC[:, j:j + 1], LP[:, 0:1], LP[:, 1:2]),
                            inc="vc", waits=[t_exp[0], t_exp[1], t_prev_t17])
                    else:
                        t_lc[j] = b.op("vector", lambda e, j=j: e.tensor_copy(
                            LC[:, j:j + 1], LP[:, 1:2]),
                            inc="vc", waits=[t_exp[1], t_prev_t17])
                t_scores_last = tmm

                t_dgc = {}
                for sl in range(NSL):
                    p = pj[sl % 2]
                    waits = [t12, t_ones]
                    if sl >= 2:
                        waits.append(t_dgc[sl - 2])
                    tdm = b.op("tensor", lambda e, p=p, sl=sl: e.matmul(
                        p[0:1, :], ONE64B[:, :], DT[:, sl * 512:(sl + 1) * 512],
                        start=True, stop=True), inc="tn", waits=waits)
                    t_dgc[sl] = b.op("vector", lambda e, p=p, sl=sl: e.tensor_copy(
                        DG[0:1, sl * 512:(sl + 1) * 512], p[0:1, :]),
                        inc="vc", waits=[tdm])
                t_prev_diag_evac = t_dgc[NSL - 1]
                def trd(e):
                    ins = None
                    for j in range(NT):
                        ins = e.matmul(msc[:, j:j + 1],
                                       DG[0:1, j * 128:(j + 1) * 128],
                                       ONE128[0:1, 0:1], start=True, stop=True)
                    return ins
                t15 = b.op("tensor", trd, inc="tn",
                           waits=[t_dgc[NSL - 1], t11, t_idf])
                t16 = b.op("scalar", lambda e: e.activation(
                    DE[:, 0:32], msc[:, 0:32], AF.Exp, bias=NC0[:, 1:2]),
                    inc="ac", waits=[t15, t_prev_t18])
                t17 = b.op("vector", lambda e: e.reciprocal(RL[:, :], LC[:, :]),
                           inc="vc", waits=[t_lc[NT - 1], t16, t_prev_t18])
                t18 = b.op("vector", lambda e, n=n: e.tensor_tensor(
                    DCOL[:, n * 32:(n + 1) * 32], DE[:, 0:32], RL[:, :],
                    op=OP.mult), inc="vc", waits=[t17])
                t_prev_mscfree = t16
                t_qk_free = [t_scores_last, t12]
                t_prev_t17 = t17
                t_prev_t18 = t18

            t_ld = {}
            t_st = {}
            t_ds_free = None
            t_tp_free = None
            t_dp_free = {}
            t_comb = {}
            dcol_ready = t18
            for sl in range(NSL):
                for tt in range(4):
                    ti = sl * 4 + tt
                    slot = ti % 4
                    waits = []
                    if ti >= 4:
                        waits.append(t_comb.get(ti - 4))
                    t_ld[ti] = b.dma("sync", XT[:, slot * 1024:(slot + 1) * 1024],
                                     io.XNF[ti * 128:(ti + 1) * 128, :],
                                     f"ld{slot}", waits=waits)
                for tt in range(4):
                    ti = sl * 4 + tt
                    slot = ti % 2
                    tcp = None
                    for hg in range(2):
                        p = pj[(tt * 2 + hg) % 2]
                        waits = [t_wvd, t_xn]
                        if t_ds_free is not None:
                            waits.append(t_ds_free)
                        else:
                            waits.append(t_prev_diag_evac)
                        def mmD(e, p=p, sl=sl, tt=tt, hg=hg):
                            ins = None
                            for c in range(8):
                                ins = e.matmul(
                                    p[:, :],
                                    XN[:, c * T + sl * 512 + tt * 128:
                                       c * T + sl * 512 + (tt + 1) * 128],
                                    WVD[:, c * 1024 + hg * 512:
                                        c * 1024 + (hg + 1) * 512],
                                    start=(c == 0), stop=(c == 7))
                            return ins
                        tdt = b.op("tensor", mmD, inc="tn", waits=waits)
                        def dsg(e, p=p, sl=sl, tt=tt, hg=hg):
                            ins = None
                            for h in range(8):
                                nh = hg * 8 + h
                                ins = e.tensor_scalar(
                                    DS[:, h * 64:(h + 1) * 64],
                                    p[:, h * 64:(h + 1) * 64],
                                    DCOL[:, nh * 32 + sl * 4 + tt:
                                         nh * 32 + sl * 4 + tt + 1],
                                    None, op0=OP.mult)
                            return ins
                        wds = [tdt, dcol_ready]
                        if t_ds_free is not None:
                            wds.append(t_ds_free)
                        tds = b.op("vector", dsg, inc="vc", waits=wds)
                        def trp(e):
                            ins = None
                            for h in range(8):
                                ins = e.transpose(tpp[:, h * 128:(h + 1) * 128],
                                                  DS[:, h * 64:(h + 1) * 64],
                                                  IDB[:, :])
                            return ins
                        wtr = [tds, t_idb]
                        if t_tp_free is not None:
                            wtr.append(t_tp_free)
                        ttr = b.op("tensor", trp, inc="tn", waits=wtr)
                        t_ds_free = ttr
                        def cpy(e, tt=tt, hg=hg):
                            ins = None
                            for h in range(8):
                                ch = hg * 4 + h // 2
                                r0 = (h % 2) * 64
                                ins = e.tensor_copy(
                                    DP[r0:r0 + 64,
                                       ch * 512 + tt * 128:ch * 512 + (tt + 1) * 128],
                                    tpp[:, h * 128:(h + 1) * 128])
                            return ins
                        waits = [ttr]
                        if sl >= 1 and hg == 0 and tt == 0:
                            waits.append(t_dp_free.get(sl - 1))
                        tcp = b.op("vector", cpy, inc="vc", waits=waits)
                        t_tp_free = tcp
                    def mmE(e, tt=tt):
                        ins = None
                        for eh in range(2):
                            for c in range(8):
                                ins = e.matmul(
                                    scp[:, eh * 512:(eh + 1) * 512],
                                    DP[:, c * 512 + tt * 128:c * 512 + (tt + 1) * 128],
                                    WVU[:, c * 1024 + eh * 512:
                                        c * 1024 + (eh + 1) * 512],
                                    start=(c == 0), stop=(c == 7))
                        return ins
                    waits = [tcp, t_wvu]
                    if t_prev_expfree is not None:
                        waits.append(t_prev_expfree)
                        t_prev_expfree = None
                    if ti >= 1:
                        waits.append(t_comb.get(ti - 1))
                    tdel = b.op("tensor", mmE, inc="tn", waits=waits)
                    t_dp_free[sl] = tdel
                    t_comb[ti] = b.op("vector",
                        lambda e, slot=slot, ls=ti % 4: e.tensor_add(
                            AT[:, slot * 1024:(slot + 1) * 1024],
                            XT[:, ls * 1024:(ls + 1) * 1024], scp[:, 0:1024]),
                        inc="vc", waits=[tdel, t_ld[ti], t_st.get(ti - 2)])
                    t_st[ti] = b.dma("gpsimd",
                                     io.ATT0[ti * 128:(ti + 1) * 128, :],
                                     AT[:, slot * 1024:(slot + 1) * 1024],
                                     f"st{slot}", waits=[t_comb[ti]])
            if DEBUG_A0:
                tb = b.dma("gpsimd", io.DBG3[:, :], DP[:, :], "dbg",
                           waits=[t_comb[NT - 1]])
                tb = b.dma("sync", io.DBG0[:, 0:512], DCOL[:, :], "dbg",
                           waits=[dcol_ready])
                tb = b.dma("sync", io.DBG0[:, 512:544], LC[:, :], "dbg")
                tb = b.dma("sync", io.DBG0[:, 544:546], NC0[:, :], "dbg")
                tb = b.dma("sync", io.DBG0[0:1, 4096:8192], DG[0:1, :], "dbg")
                tb = b.dma("sync", io.DBG0[:, 546:578], DE[:, :], "dbg")
                tb = b.dma("sync", io.DBG0[:, 578:610], RL[:, :], "dbg")
                tb = b.dma("gpsimd", io.DBG1[:, :], Q[:, :], "dbg")
                tb = b.dma("gpsimd", io.DBG2[:, :], K[:, :], "dbg")
                b.barrier_wait("sync", [tb])
            b.barrier_wait("gpsimd", [t_st[NT - 2], t_st[NT - 1]])
            b.finish(block)


PHASE_FN["l0_attn"] = ph_l0_attn

def ph_l1_attn(nc, io, stack):
    s = Sems(nc, stack, "a1")
    NH = 16
    with ExitStack() as ls:
        b = Bld(nc, s)
        sb = lambda name, shape, dt: ls.enter_context(nc.sbuf_tensor(name, shape, dt))
        psa = lambda name, shape, dt: ls.enter_context(nc.psum_tensor(name, shape, dt))

        XN = sb("a1_xn", [128, 8 * T], BF16)
        VH = sb("a1_vh", [128, 32 * 1040], BF16)
        WX = sb("a1_wx", [128, 8 * 1024], BF16)
        WQK = sb("a1_wqk", [128, 2 * 1024], BF16)
        Qb = sb("a1_q", [64, T], BF16)
        Kb = sb("a1_k", [64, T], BF16)
        PT = sb("a1_pt", [128, 2048], BF16)
        OBF = sb("a1_obf", [64, 512], BF16)
        OSB = sb("a1_osb", [64, 512], F32)
        RLR = sb("a1_rlr", [1, 512], F32)
        NRM = sb("a1_nrm", [64, 4], F32)
        SC0 = sb("a1_sc0", [1, 8], F32)
        NC0 = sb("a1_nc0", [128, 1], F32)
        BQR = sb("a1_bqr", [1, 2 * 128], BF16)
        BVO = sb("a1_bvo", [1, 2 * 1024], BF16)
        ONE64 = sb("a1_one64", [64, 1], F32)
        ONE128 = sb("a1_one128", [1, 128], F32)
        ONEB = sb("a1_oneb", [1, 512], BF16)
        CAUS = sb("a1_caus", [128, 2048], BF16)
        IDB = sb("a1_idb", [128, 128], BF16)
        CENT = sb("a1_cent", [128, 128], BF16)
        ALPH = sb("a1_alph", [128, 8], F32)
        RMAX = sb("a1_rmax", [128, 1], F32)
        MSK = sb("a1_msk", [128, 8], F32)
        MGT = sb("a1_mgt", [128, 2 * 8], F32)
        OSL = sb("a1_osl", [128, 2 * 8 * 512], BF16)
        AT = sb("a1_at", [128, 1024], F32)

        pj = [psa("a1_pj0", [128, 512], F32), psa("a1_pj1", [128, 512], F32)]
        big4 = psa("a1_big4", [128, 2048], F32)
        pvp = psa("a1_pv", [65, 512], F32)
        msc = psa("a1_msc", [128, 512], F32)

        with nc.Block() as block:
            t_xn = None
            for c in range(8):
                t_xn = b.dma("sync", XN[:, c * T:(c + 1) * T],
                             io.XN1BP[1:T + 1, c * 128:(c + 1) * 128],
                             "xn", transpose=True)
            t_caus = None
            for i in range(4):
                t_caus = b.dma("sync", CAUS[:, i * 512:(i + 1) * 512],
                               io.causal4[i], "xn")
            t_idb = b.dma("sync", IDB[:, :], io.identbf, "xn")
            t_wv = b.dma("gpsimd",
                         WX.ap().rearrange("p (c v) -> p c v", c=8),
                         io.L1_attn_w_qkv[:, 2048:3072]
                             .rearrange("(c p) v -> p c v", p=128), "wv")
            t_bv = b.dma("gpsimd", BVO[0:1, 0:1024],
                         io.L1_attn_b_qkv[2048:3072].rearrange("(o v) -> o v", o=1),
                         "wv")
            t_bo = b.dma("gpsimd", BVO[0:1, 1024:2048],
                         io.L1_attn_b_out.rearrange("(o v) -> o v", o=1), "wv")
            t_ones = b.op("vector", lambda e: (
                e.memset(ONE64[:, :], 1.0), e.memset(ONE128[:, :], 1.0),
                e.memset(ONEB[:, :], 1.0),
                e.memset(VH.ap().rearrange("p (k h w) -> p k h w", k=32, h=16)
                         [:, :, :, 64:65], 1.0))[-1], inc="vc")

            t_vh = {}
            for ti in range(NT):
                pa = pj[ti % 2]
                pb = big4.ap()[:, (ti % 2) * 512:(ti % 2 + 1) * 512]
                waits = [t_xn, t_wv, t_bv, t_ones]
                if ti >= 2:
                    waits.append(t_vh[ti - 2])
                def mmV(e, pa=pa, pb=pb, ti=ti):
                    ins = None
                    for half, pp in ((0, pa[:, :]), (1, pb)):
                        for c in range(8):
                            e.matmul(pp,
                                     XN[:, c * T + ti * 128:c * T + (ti + 1) * 128],
                                     WX[:, c * 1024 + half * 512:
                                        c * 1024 + (half + 1) * 512],
                                     start=(c == 0), stop=False)
                        ins = e.matmul(pp, ONEB[0:1, 0:128],
                                       BVO[0:1, half * 512:(half + 1) * 512],
                                       start=False, stop=True)
                    return ins
                tmv = b.op("tensor", mmV, inc="tn", waits=waits)
                base = VH.ap()[:, ti * 1040:(ti + 1) * 1040] \
                    .rearrange("p (h w) -> p h w", h=16)
                def evV(e, pa=pa, pb=pb, base=base):
                    e.tensor_copy(base[:, 0:8, 0:64],
                                  pa.ap().rearrange("p (h w) -> p h w", h=8))
                    return e.tensor_copy(base[:, 8:16, 0:64],
                                         pb.rearrange("p (h w) -> p h w", h=8))
                t_vh[ti] = b.op("vector", evV, inc="vc", waits=[tmv, t_ones])
            t_vh_all = t_vh[NT - 1]

            t_wqk = {}
            headA_done = {}
            t_ob_store = {}

            def load_wqk(h):
                k2 = h % 2
                waits = []
                if h >= 2:
                    waits.append(headQK_done.get(h - 2))
                base = WQK.ap()[:, k2 * 1024:(k2 + 1) * 1024] \
                    .rearrange("p (c w) -> p c w", c=8)
                b.dma("gpsimd", base[:, :, 0:64],
                      io.L1_attn_w_qkv[:, h * 64:(h + 1) * 64]
                          .rearrange("(c p) d -> p c d", p=128),
                      f"wq{k2}", waits=waits)
                b.dma("gpsimd", BQR[0:1, k2 * 128:k2 * 128 + 64],
                      io.L1_attn_b_qkv[h * 64:(h + 1) * 64]
                          .rearrange("(o p) -> o p", o=1),
                      f"wq{k2}", waits=waits)
                b.dma("gpsimd", BQR[0:1, k2 * 128 + 64:k2 * 128 + 128],
                      io.L1_attn_b_qkv[1024 + h * 64:1024 + (h + 1) * 64]
                          .rearrange("(o p) -> o p", o=1), f"wq{k2}")
                return b.dma("gpsimd", base[:, :, 64:128],
                             io.L1_attn_w_qkv[:, 1024 + h * 64:1024 + (h + 1) * 64]
                                 .rearrange("(c p) d -> p c d", p=128), f"wq{k2}")

            t_qk_free = []
            t_prev_pt = {}
            headQK_done = {}
            t_onorm_prev = None
            t_ob_last = None
            t_wqk[0] = load_wqk(0)
            t_bq8 = None
            for h in range(NH):
                k2 = h % 2
                t_evac = {}
                t_mm = None
                for sl in range(NSL):
                    p = pj[sl % 2]
                    waits = [t_wqk[h], t_xn, t_vh_all]
                    if sl >= 2:
                        waits.append(t_evac[sl - 2])
                    def mmA(e, p=p, sl=sl, k2=k2):
                        for c in range(8):
                            e.matmul(
                                p[:, :],
                                WQK[:, k2 * 1024 + c * 128:k2 * 1024 + (c + 1) * 128],
                                XN[:, c * T + sl * 512:c * T + (sl + 1) * 512],
                                start=(c == 0), stop=False)
                        return e.matmul(p[:, :],
                                        BQR[0:1, k2 * 128:(k2 + 1) * 128],
                                        ONEB[0:1, :],
                                        start=False, stop=True)
                    t_mm = b.op("tensor", mmA, inc="tn",
                                waits=waits + [t_ones])
                    ew = [t_mm] + (t_qk_free if sl == 0 else [])
                    b.op("scalar", lambda e, p=p, sl=sl: e.activation(
                        Qb[:, sl * 512:(sl + 1) * 512], p[0:64, :], AF.Copy,
                        scale=0.125), waits=ew)
                    t_evac[sl] = b.op("scalar",
                        lambda e, p=p, sl=sl: e.activation(
                            Kb[:, sl * 512:(sl + 1) * 512], p[64:128, :], AF.Copy),
                        inc="ac", waits=[t_mm])
                headA_done[h] = t_mm
                t_qk = t_evac[NSL - 1]
                headQK_done[h] = t_qk
                if h + 1 < NH:
                    t_wqk[h + 1] = load_wqk(h + 1)

                t1 = b.op("vector", lambda e: e.tensor_reduce(
                    NRM[:, 0:1], Qb[:, :], AX, OP.max, apply_absolute_value=True),
                    waits=[t_qk])
                t2 = b.op("vector", lambda e: e.tensor_reduce(
                    NRM[:, 1:2], Kb[:, :], AX, OP.max, apply_absolute_value=True),
                    inc="vc", waits=[t_qk])
                t3 = b.op("scalar", lambda e: e.activation(
                    NRM[:, 2:4], NRM[:, 0:2], AF.Square), inc="ac", waits=[t2])
                t4 = b.op("tensor", lambda e: e.matmul(
                    msc[0:1, 0:2], ONE64[:, :], NRM[:, 2:4], start=True, stop=True),
                    inc="tn", waits=[t3, t_ones, t_onorm_prev])
                t5 = b.op("vector", lambda e: e.tensor_copy(
                    SC0[0:1, 0:2], msc[0:1, 0:2]), inc="vc", waits=[t4])
                t6 = b.op("vector", lambda e: e.tensor_mul(
                    SC0[0:1, 2:3], SC0[0:1, 0:1], SC0[0:1, 1:2]),
                    inc="vc", waits=[t5])
                t7 = b.op("scalar", lambda e: e.activation(
                    SC0[0:1, 3:4], SC0[0:1, 2:3], AF.Ln), inc="ac", waits=[t6])
                t8 = b.op("scalar", lambda e: e.activation(
                    SC0[0:1, 4:5], SC0[0:1, 3:4], AF.Exp, scale=0.5),
                    inc="ac", waits=[t7])
                t9 = b.op("vector", lambda e: e.tensor_scalar(
                    SC0[0:1, 5:6], SC0[0:1, 4:5], -1.0, None, op0=OP.mult),
                    inc="vc", waits=[t8])
                t10 = b.op("tensor", lambda e: e.matmul(
                    msc[:, 4:5], ONE128[:, :], SC0[0:1, 5:6], start=True, stop=True),
                    inc="tn", waits=[t9, t5])
                t11 = b.op("vector", lambda e: e.tensor_copy(NC0[:, 0:1],
                                                             msc[:, 4:5]),
                           inc="vc", waits=[t10])

                t_onorm = {}
                t_pv_free = [None]
                for jq in range(NSL):
                    t_pv = None
                    texp = None
                    for kg in range(jq + 1):
                        waits = [t_qk, t11]
                        tprev = t_prev_pt.get(0)
                        if tprev is not None:
                            waits.append(tprev)
                        def mmS(e, kg=kg, jq=jq):
                            ins = None
                            for j4 in range(4):
                                kt = 4 * kg + j4
                                ins = e.matmul(
                                    big4[:, j4 * 512:(j4 + 1) * 512],
                                    Kb[:, kt * 128:(kt + 1) * 128],
                                    Qb[:, jq * 512:(jq + 1) * 512],
                                    start=True, stop=True)
                            return ins
                        tms = b.op("tensor", mmS, inc="tn", waits=waits)
                        tmask = None
                        if kg == jq:
                            tmask = b.op("vector",
                                lambda e: e.scalar_tensor_tensor(
                                    big4[:, :], big4[:, :], 1.0, CAUS[:, :],
                                    op0=OP.mult, op1=OP.add),
                                inc="vc", waits=[tms, t_caus])
                        ew = [tms]
                        if tmask is not None:
                            ew.append(tmask)
                        if t_pv is not None:
                            ew.append(t_pv)
                        elif t_pv_free[0] is not None:
                            ew.append(t_pv_free[0][0])
                        texp = b.op("scalar", lambda e: e.activation(
                            PT[:, :], big4[:, :], AF.Exp, bias=NC0[:, 0:1]),
                            inc="ac", waits=ew)
                        t_prev_pt[0] = texp
                        pw = [texp, t_vh_all]
                        if kg == 0 and t_pv_free[0] is not None:
                            pw.extend(t_pv_free[0])
                        def mmpv(e, kg=kg, jq=jq, h=h):
                            ins = None
                            for j4 in range(4):
                                kt = 4 * kg + j4
                                ins = e.matmul(
                                    pvp[:, :],
                                    VH[:, kt * 1040 + h * 65:
                                       kt * 1040 + (h + 1) * 65],
                                    PT[:, j4 * 512:(j4 + 1) * 512],
                                    start=(kt == 0),
                                    stop=(kg == jq and j4 == 3))
                            return ins
                        t_pv = b.op("tensor", mmpv, inc="tn", waits=pw)
                    tre2 = b.op("vector", lambda e: e.reciprocal(
                        RLR[0:1, :], pvp[64:65, :]),
                        inc="vc", waits=[t_pv])
                    tbc = b.op("tensor", lambda e: e.matmul(
                        msc[0:64, 0:512], ONE128[0:1, 0:64],
                        RLR[0:1, :], start=True, stop=True),
                        inc="tn", waits=[tre2, t_onorm.get(jq - 1), t_onorm_prev])
                    tcp = b.op("scalar", lambda e: e.activation(
                        OSB[:, :], pvp[0:64, :], AF.Copy), inc="ac", waits=[t_pv])
                    wn = [tbc, tcp, t_ob_last]
                    t_onorm[jq] = b.op("vector", lambda e, jq=jq: e.tensor_tensor(
                        OBF[:, :], OSB[:, :],
                        msc[0:64, 0:512], op=OP.mult), inc="vc", waits=wn)
                    t_pv_free[0] = (tcp, tre2)
                    t_ob_last = b.dma("gpsimd",
                                      io.OB[h * 64:(h + 1) * 64,
                                            jq * 512:(jq + 1) * 512],
                                      OBF[:, :], "ob", waits=[t_onorm[jq]])
                t_ob_store[h] = t_ob_last
                t_onorm_prev = t_onorm[NSL - 1]
                t_qk_free = [t_pv, t2]

            t_wo = b.dma("gpsimd",
                         WX.ap().rearrange("p (c v) -> p c v", c=8),
                         io.L1_attn_w_out.rearrange("(c p) d -> p c d", p=128),
                         "wv", waits=[t_vh_all, headA_done[NH - 1]])
            t_osl = {}
            t_att = {}
            t_st = {}
            for sl in range(NSL):
                slot = sl % 2
                waits = [t_ob_store.get(h) for h in range(NH)]
                if sl >= 2:
                    waits.append(t_att.get(sl * 4 - 8 + 3))
                tld = None
                for c in range(8):
                    tld = b.dma("sync",
                                OSL[:, slot * 4096 + c * 512:
                                    slot * 4096 + (c + 1) * 512],
                                io.OB[c * 128:(c + 1) * 128,
                                      sl * 512:(sl + 1) * 512],
                                f"os{slot}", waits=waits if c == 0 else [])
                for tt in range(4):
                    ti = sl * 4 + tt
                    pa = pj[tt % 2]
                    pb = big4.ap()[:, (tt % 2) * 512:(tt % 2 + 1) * 512]
                    waits = [tld, t_wo, t_bo]
                    if ti >= 2:
                        waits.append(t_att.get(ti - 2))
                    def mmO(e, pa=pa, pb=pb, tt=tt, sl=sl):
                        ins = None
                        for half, pp in ((0, pa[:, :]), (1, pb)):
                            for c in range(8):
                                e.matmul(pp,
                                         OSL[:, sl % 2 * 4096 + c * 512 + tt * 128:
                                             sl % 2 * 4096 + c * 512 + (tt + 1) * 128],
                                         WX[:, c * 1024 + half * 512:
                                            c * 1024 + (half + 1) * 512],
                                         start=(c == 0), stop=False)
                            ins = e.matmul(pp, ONEB[0:1, 0:128],
                                           BVO[0:1, 1024 + half * 512:
                                               1024 + (half + 1) * 512],
                                           start=False, stop=True)
                        return ins
                    tmo = b.op("tensor", mmO, inc="tn", waits=waits)
                    t_att[ti] = b.op("vector", lambda e, pa=pa, pb=pb, tt=tt:
                        (e.tensor_copy(AT[:, 0:512], pa[:, :]),
                         e.tensor_copy(AT[:, 512:1024], pb))[-1],
                        inc="vc", waits=[tmo, t_st.get(ti - 1)])
                    t_st[ti] = b.dma("gpsimd", io.ATT1[ti * 128:(ti + 1) * 128, :],
                                     AT[:, 0:1024],
                                     "st0", waits=[t_att[ti]])

            tce = None
            for c in range(8):
                tce = b.dma("sync", CENT[:, c * 16:(c + 1) * 16],
                            io.CENB[:, c * 128:(c + 1) * 128], "ct",
                            transpose=True)
            t_mg = {}
            t_mgv_prev = None
            for ti in range(NT):
                ap = pj[ti % 2].ap()[:, 0:8]
                waits = [tce, t_xn]
                if ti >= 2:
                    waits.append(t_mg[ti - 2])
                else:
                    waits += [t_att.get(NT - 2), t_att.get(NT - 1)]
                def mmAf(e, ap=ap, ti=ti):
                    ins = None
                    for c in range(8):
                        ins = e.matmul(ap,
                                       XN[:, c * T + ti * 128:c * T + (ti + 1) * 128],
                                       CENT[:, c * 16:c * 16 + 8],
                                       start=(c == 0), stop=(c == 7))
                    return ins
                tma = b.op("tensor", mmAf, inc="tn", waits=waits)
                tsg = b.op("scalar", lambda e, ap=ap: e.activation(
                    ALPH[:, :], ap, AF.Sigmoid), inc="ac",
                    waits=[tma, t_mgv_prev])
                trm = b.op("vector", lambda e, ap=ap: e.tensor_reduce(
                    RMAX[:, 0:1], ap, AX, OP.max), inc="vc", waits=[tma])
                tmk = b.op("vector", lambda e, ap=ap: e.tensor_scalar(
                    MSK[:, :], ap, RMAX[:, 0:1], None, op0=OP.is_equal),
                    inc="vc", waits=[trm])
                tmg = b.op("vector", lambda e, ti=ti: e.tensor_tensor(
                    MGT[:, (ti % 2) * 8:(ti % 2 + 1) * 8], MSK[:, :], ALPH[:, :],
                    op=OP.mult), inc="vc",
                    waits=[tmk, tsg, t_mg.get(ti - 2)])
                t_mgv_prev = tmg
                t_mg[ti] = b.dma("gpsimd", io.MGATE[ti * 128:(ti + 1) * 128, :],
                                 MGT[:, (ti % 2) * 8:(ti % 2 + 1) * 8],
                                 f"mg{ti % 2}", waits=[tmg])
            b.barrier_wait("gpsimd", [t_st[NT - 2], t_st[NT - 1],
                                      t_mg[NT - 2], t_mg[NT - 1]])
            b.finish(block)


PHASE_FN["l1_attn"] = ph_l1_attn


def ph_moe(nc, io, stack):
    s = Sems(nc, stack, "mo")
    with ExitStack() as ls:
        b = Bld(nc, s)
        sb = lambda name, shape, dt: ls.enter_context(nc.sbuf_tensor(name, shape, dt))
        psa = lambda name, shape, dt: ls.enter_context(nc.psum_tensor(name, shape, dt))

        W1T = sb("mo_w1t", [128, 8 * FD], BF16)
        W2T = sb("mo_w2t", [128, 32 * 1024], BF16)
        XH = sb("mo_xh", [128, 2 * 8 * 512], BF16)
        XE = sb("mo_xe", [128, 8 * 512], BF16)
        HS = sb("mo_hs", [128, 32 * 512], BF16)
        MG = sb("mo_mg", [128, 32 * 8], F32)
        LNR = sb("mo_lnr", [32, 2 * 384], F32)
        GBC = sb("mo_gbc", [128, 2 * 48], F32)
        B2R = sb("mo_b2r", [1, 2 * 1024], BF16)
        ONEB = sb("mo_oneb", [1, 128], BF16)
        IDF = sb("mo_idf", [128, 128], F32)
        YS = sb("mo_ys", [128, 2 * 1024], F32)

        hp = [psa("mo_h0", [128, 512], F32), psa("mo_h1", [128, 512], F32)]
        yp = [psa("mo_y0", [128, 1024], F32), psa("mo_y1", [128, 1024], F32)]
        lnp = psa("mo_lnp", [128, 48], F32)

        with nc.Block() as block:
            t_idf = b.dma("sync", IDF[:, :], io.identf, "ix")
            t_ones = b.op("vector", lambda e: e.memset(ONEB[:, :], 1.0), inc="vc")
            t_mg = b.dma("gpsimd",
                         MG.ap().rearrange("p (ti e) -> p ti e", ti=NT),
                         io.MGATE.rearrange("(ti p) e -> p ti e", p=128), "mg")

            def load_wexp(e_, waits):
                tw = None
                for dc in range(8):
                    tw = b.dma("sync", W1T[:, dc * FD:(dc + 1) * FD],
                               io.W1B[e_, :, dc * 128:(dc + 1) * 128], "w1",
                               transpose=True, waits=waits if dc == 0 else [])
                tw2 = None
                for fc in range(32):
                    tw2 = b.dma("sync", W2T[:, fc * 1024:(fc + 1) * 1024],
                                io.W2B[e_, :, fc * 128:(fc + 1) * 128], "w2",
                                transpose=True, waits=waits if fc == 0 else [])
                return tw, tw2

            def load_lns(e_, sl2, waits):
                c0 = sl2 * 384
                b.dma("sync", LNR[0:8, c0:c0 + 128],
                      io.L1_ffn_ln_g[e_].rearrange("(c p) -> c p", p=128),
                      f"ln{sl2}", waits=waits)
                b.dma("sync", LNR[0:8, c0 + 128:c0 + 256],
                      io.L1_ffn_ln_b[e_].rearrange("(c p) -> c p", p=128),
                      f"ln{sl2}")
                b.dma("sync", LNR[0:32, c0 + 256:c0 + 384],
                      io.L1_ffn_b1[e_].rearrange("(c p) -> c p", p=128),
                      f"ln{sl2}")
                return b.dma("gpsimd", B2R[0:1, sl2 * 1024:(sl2 + 1) * 1024],
                             io.L1_ffn_b2[e_].rearrange("(o v) -> o v", o=1),
                             f"ln{sl2}")

            t_w = load_wexp(0, [])
            t_ln = {0: load_lns(0, 0, [])}
            t_acc = {}
            t_gb = {}
            prev_m1_last = None
            prev_m2_last = None
            t_ys_free = {}
            t_xe_prev = {}
            t_hev_prev = {}
            tt1_prev = None
            for e in range(NE):
                sl2 = e % 2
                c0 = sl2 * 384
                waits = [t_ln[e], t_idf]
                if e >= 1:
                    waits.append(t_gb[e - 1])
                def trn(en, c0=c0):
                    en.transpose(lnp[:, 0:8], LNR[0:8, c0:c0 + 128],
                                 IDF[0:8, 0:8])
                    en.transpose(lnp[:, 8:16], LNR[0:8, c0 + 128:c0 + 256],
                                 IDF[0:8, 0:8])
                    return en.transpose(lnp[:, 16:48],
                                        LNR[0:32, c0 + 256:c0 + 384],
                                        IDF[0:32, 0:32])
                tt1 = b.op("tensor", trn, inc="tn", waits=waits)
                t_gb[e] = b.op("vector", lambda e2, sl2=sl2: e2.tensor_copy(
                    GBC[:, sl2 * 48:(sl2 + 1) * 48], lnp[:, 0:48]),
                    inc="vc", waits=[tt1])
                if e + 1 < NE:
                    t_ln[e + 1] = load_lns(e + 1, 1 - sl2,
                                           [tt1_prev, prev_m2_last])
                tt1_prev = tt1

                tw1, tw2 = t_w
                for sl in range(NSL):
                    xsl = sl % 2
                    waits = [t_xe_prev.get(xsl)]
                    tx = None
                    for c in range(8):
                        tx = b.dma("sync",
                                   XH[:, xsl * 4096 + c * 512:
                                      xsl * 4096 + (c + 1) * 512],
                                   io.XHATB[sl * 512:(sl + 1) * 512,
                                            c * 128:(c + 1) * 128],
                                   f"xh{xsl}", transpose=True,
                                   waits=waits if c == 0 else [])
                    def xe_f(en, xsl=xsl, sl2=sl2):
                        ins = None
                        for c in range(8):
                            ins = en.tensor_scalar(
                                XE[:, c * 512:(c + 1) * 512],
                                XH[:, xsl * 4096 + c * 512:
                                   xsl * 4096 + (c + 1) * 512],
                                GBC[:, sl2 * 48 + c:sl2 * 48 + c + 1],
                                GBC[:, sl2 * 48 + 8 + c:sl2 * 48 + 8 + c + 1],
                                op0=OP.mult, op1=OP.add)
                        return ins
                    waits = [tx, t_gb[e], prev_m1_last]
                    t_xe = b.op("vector", xe_f, inc="vc", waits=waits)
                    t_xe_prev[xsl] = t_xe
                    t_hev = {}
                    for fc in range(32):
                        p = hp[fc % 2]
                        waits = [t_xe, tw1]
                        if fc >= 2:
                            waits.append(t_hev[fc - 2])
                        else:
                            waits.append(t_hev_prev.get(fc % 2))
                        def m1(en, p=p, fc=fc):
                            ins = None
                            for dc in range(8):
                                ins = en.matmul(
                                    p[:, :],
                                    W1T[:, dc * FD + fc * 128:
                                        dc * FD + (fc + 1) * 128],
                                    XE[:, dc * 512:(dc + 1) * 512],
                                    start=(dc == 0), stop=(dc == 7))
                            return ins
                        tm1 = b.op("tensor", m1, inc="tn", waits=waits)
                        hsw = [tm1, prev_m2_last, t_gb[e]]
                        if fc % 2 == 0:
                            t_hev[fc] = b.op("scalar",
                                lambda en, p=p, fc=fc, c0=c0: en.activation(
                                    HS[:, fc * 512:(fc + 1) * 512], p[:, :],
                                    AF.Relu,
                                    bias=GBC[:, (c0 // 8) + 16 + fc:
                                             (c0 // 8) + 17 + fc]),
                                inc="ac", waits=hsw)
                        else:
                            t_hev[fc] = b.op("vector",
                                lambda en, p=p, fc=fc, c0=c0: en.tensor_scalar(
                                    HS[:, fc * 512:(fc + 1) * 512], p[:, :],
                                    GBC[:, (c0 // 8) + 16 + fc:
                                        (c0 // 8) + 17 + fc], 0.0,
                                    op0=OP.add, op1=OP.max),
                                inc="vc", waits=hsw)
                    prev_m1_last = tm1
                    t_hev_prev = {0: t_hev[30], 1: t_hev[31]}
                    t_h_all = [t_hev[30], t_hev[31]]
                    for tt in range(4):
                        ti = sl * 4 + tt
                        p = yp[tt % 2]
                        waits = list(t_h_all) + [tw2, t_ones, t_ln[e],
                                                 t_ys_free.get(tt % 2)]
                        def m2(en, p=p, tt=tt, sl2=sl2):
                            ins = None
                            for half in range(2):
                                for fc in range(32):
                                    en.matmul(
                                        p[:, half * 512:(half + 1) * 512],
                                        HS[:, fc * 512 + tt * 128:
                                           fc * 512 + (tt + 1) * 128],
                                        W2T[:, fc * 1024 + half * 512:
                                            fc * 1024 + (half + 1) * 512],
                                        start=(fc == 0), stop=False)
                                ins = en.matmul(
                                    p[:, half * 512:(half + 1) * 512],
                                    ONEB[:, :],
                                    B2R[0:1, sl2 * 1024 + half * 512:
                                        sl2 * 1024 + (half + 1) * 512],
                                    start=False, stop=True)
                            return ins
                        tm2 = b.op("tensor", m2, inc="tn", waits=waits)
                        prev_m2_last = tm2
                        tys = b.op("vector", lambda en, p=p, tt=tt, ti=ti, e=e:
                            en.tensor_scalar(
                                YS[:, (tt % 2) * 1024:(tt % 2 + 1) * 1024],
                                p[:, :],
                                MG[:, ti * 8 + e:ti * 8 + e + 1], None,
                                op0=OP.mult),
                            inc="vc", waits=[tm2, t_mg, t_acc.get(ti)])
                        t_acc[ti] = b.dma(
                            "gpsimd", io.ACCD[ti * 128:(ti + 1) * 128, :],
                            YS[:, (tt % 2) * 1024:(tt % 2 + 1) * 1024],
                            f"ya{tt % 2}", waits=[tys],
                            accum_op=(OP.add if e > 0 else OP.bypass))
                        t_ys_free[tt % 2] = t_acc[ti]
                if e + 1 < NE:
                    t_w = load_wexp(e + 1, [prev_m1_last, prev_m2_last])
            b.barrier_wait("gpsimd", [t_acc[NT - 2], t_acc[NT - 1]])
            b.finish(block)


def ph_comb(nc, io, stack):
    s = Sems(nc, stack, "cb")
    with ExitStack() as ls:
        b = Bld(nc, s)
        sb = lambda name, shape, dt: ls.enter_context(nc.sbuf_tensor(name, shape, dt))
        CX = sb("cb_cx", [128, 4 * 1024], F32)
        CO = sb("cb_co", [128, 2 * 1024], F32)
        CA = sb("cb_ca", [128, 2 * 1024], F32)
        with nc.Block() as block:
            t_co = {}
            t_cst = {}
            for ti in range(NT):
                ks = ti % 2
                waits = [t_co.get(ti - 2)]
                tl1 = b.dma("sync", CX[:, (ks * 2) * 1024:(ks * 2 + 1) * 1024],
                            io.X1[ti * 128:(ti + 1) * 128, :], f"c{ks}",
                            waits=waits)
                tl2 = b.dma("sync", CX[:, (ks * 2 + 1) * 1024:(ks * 2 + 2) * 1024],
                            io.ATT1[ti * 128:(ti + 1) * 128, :], f"c{ks}")
                tl3 = b.dma("gpsimd", CO[:, ks * 1024:(ks + 1) * 1024],
                            io.XN1F[ti * 128:(ti + 1) * 128, :], f"d{ks}",
                            waits=waits)
                tl4 = b.dma("gpsimd", CA[:, ks * 1024:(ks + 1) * 1024],
                            io.ACCD[ti * 128:(ti + 1) * 128, :], f"d{ks}")
                tadd = b.op("vector", lambda en, ks=ks: (
                    en.tensor_add(CO[:, ks * 1024:(ks + 1) * 1024],
                                  CO[:, ks * 1024:(ks + 1) * 1024],
                                  CX[:, (ks * 2) * 1024:(ks * 2 + 1) * 1024]),
                    en.tensor_add(CO[:, ks * 1024:(ks + 1) * 1024],
                                  CO[:, ks * 1024:(ks + 1) * 1024],
                                  CX[:, (ks * 2 + 1) * 1024:(ks * 2 + 2) * 1024]),
                    en.tensor_add(CO[:, ks * 1024:(ks + 1) * 1024],
                                  CO[:, ks * 1024:(ks + 1) * 1024],
                                  CA[:, ks * 1024:(ks + 1) * 1024]))[-1],
                    inc="vc", waits=[tl1, tl2, tl3, tl4, t_cst.get(ti - 2)])
                t_co[ti] = tadd
                t_cst[ti] = b.dma("gpsimd", io.OUT[ti * 128:(ti + 1) * 128, :],
                                  CO[:, ks * 1024:(ks + 1) * 1024],
                                  f"o{ks}", waits=[tadd])
            b.barrier_wait("gpsimd", [t_cst[NT - 2], t_cst[NT - 1]])
            b.finish(block)


PHASE_FN["moe"] = ph_moe
PHASE_FN["comb"] = ph_comb



def ph_l0_ffn(nc, io, stack):
    s = Sems(nc, stack, "f0")
    with ExitStack() as ls:
        b = Bld(nc, s)
        sb = lambda name, shape, dt: ls.enter_context(nc.sbuf_tensor(name, shape, dt))
        psa = lambda name, shape, dt: ls.enter_context(nc.psum_tensor(name, shape, dt))

        HS = sb("f0_hs", [128, 8 * 512], BF16)
        HS2 = sb("f0_hs2", [128, 8 * 512], BF16)
        XLN = HS2
        WQK = sb("f0_wqk", [128, 8 * 1024], BF16)
        WV = sb("f0_wv", [128, 8 * 512], BF16)
        WO = sb("f0_wo", [128, 4 * 1024], BF16)
        W1P = sb("f0_w1p", [128, 2 * 2048], BF16)
        W2S = sb("f0_w2s", [128, 2 * 1024], BF16)
        YT = sb("f0_yt", [128, 2 * 1024], F32)
        Y1T = sb("f0_y1t", [128, 4 * 1024], F32)
        QS = sb("f0_qsb", [64, 512], BF16)
        KR = sb("f0_kr", [64, 8 * 1024], BF16)
        VR = sb("f0_vr", [128, 8 * 520], BF16)
        OALL = sb("f0_oall", [128, 4 * 512], BF16)
        H3 = sb("f0_h3", [128, 22 * 512], BF16)
        GELB = sb("f0_gel", [128, 512], F32)
        SQ = sb("f0_sq", [128, 512], F32)
        TBL = sb("f0_tbl", [64, 4 * 512], F32)
        QF = sb("f0_qf", [64, 8 * 512], F32)
        PT4 = sb("f0_pt4", [128, 2048], BF16)
        FMK = sb("f0_fmk", [128, 2 * 2048], BF16)
        RW = sb("f0_rw", [1, 1792], F32)
        B1C = sb("f0_b1c", [128, 44], F32)
        LGC = sb("f0_lgc", [128, 16], F32)
        B2R = sb("f0_b2r", [1, 1024], BF16)
        B1T = sb("f0_b1t", [21, 256], F32)
        B1TT = sb("f0_b1tt", [1, 128], F32)
        LNR = sb("f0_lnr", [8, 256], F32)
        QKC = sb("f0_qkc", [64, 2], F32)
        GAM = sb("f0_gam", [64, 17], F32)
        NC0F = sb("f0_nc0", [128, 1], F32)
        EPSR = sb("f0_eps", [128, 1], F32)
        ONEF = sb("f0_onef", [1, 128], F32)
        ONCB = sb("f0_oncb", [128, 1], BF16)
        ONCF = sb("f0_oncf", [128, 1], F32)
        ONEB = sb("f0_oneb", [1, 128], BF16)
        IDF = sb("f0_idf", [128, 128], F32)
        XT = sb("f0_xtt", [128, 1024], F32)
        A0T = sb("f0_a0t", [128, 1024], F32)
        ST = sb("f0_stt", [128, 16], F32)
        O1 = sb("f0_o1", [128, 2 * 1024], F32)

        pa = [psa("f0_pa0", [128, 512], F32), psa("f0_pa1", [128, 512], F32)]
        pb = psa("f0_pb", [128, 2048], F32)
        pv = psa("f0_pv", [65, 512], F32)
        pm = psa("f0_pm", [128, 512], F32)

        HIDC = [128] * 21 + [42]
        P = {}

        with nc.Block() as block:
            t_ones = b.op("vector", lambda e: (
                e.memset(ONEF[:, :], 1.0), e.memset(ONCB[:, :], 1.0),
                e.memset(ONCF[:, :], 1.0), e.memset(ONEB[:, :], 1.0),
                e.memset(EPSR[:, :], 1e-5), e.memset(B2R[:, :], 0.0),
                e.memset(VR.ap().rearrange("p (k h w) -> p k h w", k=8, h=8)
                         [:, :, :, 64:65], 1.0))[-1], inc="vc")
            t_fmk = None
            for g in range(2):
                t_fmk = b.dma("sync", FMK[:, g * 2048:(g + 1) * 2048],
                              io.ffnmask[g], "ms")
            t_idf = b.dma("sync", IDF[:, :], io.identf, "ms")
            t_gam = b.dma("sync", GAM[:, 0:8], io.gamq8, "ms")
            b.dma("sync", GAM[:, 8:16], io.gamki, "ms")
            t_gam = b.dma("sync", GAM[:, 16:17], io.gamrr, "ms")
            ZR = B2R
            t_zr = None
            for nm in ("YMBP", "Y1BP", "XN1BP"):
                t_zr = b.dma("gpsimd", getattr(io, nm)[0:1, :], ZR[0:1, :],
                             "zr", waits=[t_ones])

            for lyr in range(2):
                YinF = io.XNF if lyr == 0 else io.Y1F
                YinB = io.XNBP if lyr == 0 else io.Y1BP
                lw = [P.get("wfree")]
                wqv = io.FQKB[lyr]
                t_wqk = None
                for c in range(8):
                    wqc = WQK.ap()[:, c * 1024:(c + 1) * 1024] \
                        .rearrange("p (h w) -> p h w", h=8)
                    b.dma("sync", wqc[:, :, 0:64],
                          wqv[c * 128:(c + 1) * 128, 0:512]
                              .rearrange("p (h d) -> p h d", h=8), "wl",
                          waits=lw if c == 0 else [])
                    t_wqk = b.dma("sync", wqc[:, :, 64:128],
                                  wqv[c * 128:(c + 1) * 128, 512:1024]
                                      .rearrange("p (h d) -> p h d", h=8),
                                  "wl")
                t_wv = b.dma("sync",
                             WV.ap().rearrange("p (c v) -> p c v", c=8),
                             wqv[:, 1024:1536].rearrange("(c p) v -> p c v",
                                                         p=128), "wl")
                t_wo = b.dma("sync",
                             WO.ap().rearrange("p (c v) -> p c v", c=4),
                             io.FWOB[lyr].rearrange("(c p) v -> p c v",
                                                    p=128), "wl")
                t_b2 = b.dma("gpsimd", B2R[0:1, :],
                             io.L0_ffn_b2[lyr].rearrange("(o v) -> o v", o=1),
                             "wl", waits=[t_zr])
                b.dma("sync", B1T[0:21, 0:128],
                      io.L0_ffn_b1[lyr, 0:2688].rearrange("(c p) -> c p",
                                                          p=128), "bl",
                      waits=lw)
                b.dma("sync", B1T[0:21, 128:256],
                      io.L0_ffn_b1[lyr, 2730:2730 + 2688]
                          .rearrange("(c p) -> c p", p=128), "bl")
                b.dma("sync", B1TT[0:1, 0:42],
                      io.L0_ffn_b1[lyr, 2688:2730].rearrange("(o p) -> o p",
                                                             o=1), "bl")
                t_b1 = b.dma("sync", B1TT[0:1, 64:106],
                             io.L0_ffn_b1[lyr, 2730 + 2688:5460]
                                 .rearrange("(o p) -> o p", o=1), "bl")
                b.dma("sync", LNR[0:8, 0:128],
                      io.L0_ffn_ln_g[lyr].rearrange("(c p) -> c p", p=128),
                      "bl")
                t_lnr = b.dma("sync", LNR[0:8, 128:256],
                              io.L0_ffn_ln_b[lyr].rearrange("(c p) -> c p",
                                                            p=128), "bl")
                b.dma("sync", QKC[:, 0:1],
                      io.L0_ffn_q_scale[lyr].rearrange("(p o) -> p o", o=1),
                      "bl")
                t_qkc = b.dma("sync", QKC[:, 1:2],
                              io.L0_ffn_k_scale[lyr].rearrange("(p o) -> p o",
                                                               o=1), "bl")
                b.dma("sync", RW[0:1, 1536:1600],
                      io.L0_ffn_q_scale[lyr].rearrange("(o p) -> o p", o=1),
                      "bl")
                t_qkr = b.dma("sync", RW[0:1, 1600:1664],
                              io.L0_ffn_k_scale[lyr].rearrange("(o p) -> o p",
                                                               o=1), "bl")

                waits = [t_b1, t_lnr, t_idf, t_ones, P.get("pm_users"),
                         P.get("gel_users"), P.get("xln_users")]
                def trb(e):
                    e.transpose(pm[:, 0:21], B1T[0:21, 0:128], IDF[0:21, 0:21])
                    e.matmul(pm[0:42, 21:22], B1TT[0:1, 0:42], ONEF[0:1, 0:1],
                             start=True, stop=True)
                    e.transpose(pm[:, 22:43], B1T[0:21, 128:256],
                                IDF[0:21, 0:21])
                    e.matmul(pm[0:42, 43:44], B1TT[0:1, 64:106],
                             ONEF[0:1, 0:1], start=True, stop=True)
                    e.transpose(pm[:, 44:52], LNR[0:8, 0:128], IDF[0:8, 0:8])
                    return e.transpose(pm[:, 52:60], LNR[0:8, 128:256],
                                       IDF[0:8, 0:8])
                ttb = b.op("tensor", trb, inc="tn", waits=waits)
                tbc = b.op("vector", lambda e: (
                    e.tensor_copy(B1C[:, 0:44], pm[:, 0:44]),
                    e.tensor_copy(LGC[:, 0:16], pm[:, 44:60]))[-1],
                    inc="vc", waits=[ttb])
                t1 = b.op("vector", lambda e: e.tensor_mul(
                    RW[0:1, 1664:1728], RW[0:1, 1536:1600],
                    RW[0:1, 1600:1664]), inc="vc", waits=[t_qkr])
                t2 = b.op("vector", lambda e: e.tensor_reduce(
                    RW[0:1, 1728:1729], RW[0:1, 1664:1728], AX, OP.max,
                    apply_absolute_value=True), inc="vc", waits=[t1])
                t3 = b.op("vector", lambda e: e.tensor_scalar(
                    RW[0:1, 1729:1730], RW[0:1, 1728:1729], -8.0, None,
                    op0=OP.mult), inc="vc", waits=[t2])
                t4 = b.op("tensor", lambda e: e.matmul(
                    pm[:, 60:61], ONEF[:, :], RW[0:1, 1729:1730],
                    start=True, stop=True), inc="tn", waits=[t3, ttb])
                t_nc0 = b.op("vector", lambda e: e.tensor_copy(
                    NC0F[:, :], pm[:, 60:61]), inc="vc", waits=[t4])
                P["pm_users"] = t_nc0

                def load_w1(fc, waits):
                    sl2 = fc % 2
                    wslc = HIDC[fc]
                    base = W1P.ap()[:, sl2 * 2048:(sl2 + 1) * 2048] \
                        .rearrange("p (c f) -> p c f", c=8)
                    b.dma("sync", base[:, :, 0:wslc],
                          io.FW1B[lyr][:, fc * 128:fc * 128 + wslc]
                              .rearrange("(c p) f -> p c f", p=128),
                          f"w1{sl2}", waits=waits)
                    return b.dma("sync", base[:, :, 128:128 + wslc],
                                 io.FW1B[lyr][:, 2730 + fc * 128:
                                              2730 + fc * 128 + wslc]
                                     .rearrange("(c p) f -> p c f", p=128),
                                 f"w1{sl2}")

                def load_w2(fc, waits):
                    sl2 = fc % 2
                    return b.dma("sync",
                                 W2S[0:HIDC[fc], sl2 * 1024:(sl2 + 1) * 1024],
                                 io.FW2B[lyr][fc * 128:
                                              fc * 128 + HIDC[fc], :],
                                 f"w2{sl2}", waits=waits)

                for w in range(NSL):
                    th = None
                    for c in range(8):
                        r0 = (1 + w * 512) if c < 4 else (w * 512)
                        hw = [P.get("hs_users")]
                        if lyr == 1 and w == 0:
                            hw += [P.get("ost0"), P.get("ost1")]
                        th = b.dma("sync", HS[:, c * 512:(c + 1) * 512],
                                   YinB[r0:r0 + 512, c * 128:(c + 1) * 128],
                                   "hsl", transpose=True,
                                   waits=hw if c == 0 else [])
                    t_ytl = {}
                    t_vr_last = None
                    for tt in range(4):
                        p = pa[tt % 2]
                        waits = [th, t_wv, t_ones, P.get(f"pa{tt % 2}")]
                        def mmv(e, p=p, tt=tt):
                            ins = None
                            for c in range(8):
                                ins = e.matmul(
                                    p[:, :],
                                    HS[:, c * 512 + tt * 128:
                                       c * 512 + (tt + 1) * 128],
                                    WV[:, c * 512:(c + 1) * 512],
                                    start=(c == 0), stop=(c == 7))
                            return ins
                        tv = b.op("tensor", mmv, inc="tn", waits=waits)
                        slot = (w * 4 + tt) % 8
                        t_vr_last = b.op("vector", lambda e, p=p, slot=slot:
                            e.tensor_copy(
                                VR.ap().rearrange("p (k h z) -> p k h z",
                                                  k=8, h=8)[:, slot, :, 0:64],
                                p.ap().rearrange("p (h z) -> p h z", h=8)),
                            inc="vc", waits=[tv, P.get("vr_users")])
                        P[f"pa{tt % 2}"] = t_vr_last
                    ttb_l = None
                    for i, nmt in enumerate(("xq_c", "xq_s", "xk_c", "xk_s")):
                        ttb_l = b.dma("gpsimd", TBL[:, i * 512:(i + 1) * 512],
                                      getattr(io, nmt)[:, w * 512:
                                                       (w + 1) * 512], "tbl",
                                      waits=[P.get("tbl_users")]
                                      if i == 0 else [])
                    t_oall = {}
                    tqk = None
                    for h in range(8):
                        p = pa[h % 2]
                        waits = [th, t_wqk, P.get(f"pa{h % 2}")]
                        def mmqk(e, p=p, h=h):
                            ins = None
                            for c in range(8):
                                ins = e.matmul(
                                    p[:, :],
                                    WQK[:, c * 1024 + h * 128:
                                        c * 1024 + (h + 1) * 128],
                                    HS[:, c * 512:(c + 1) * 512],
                                    start=(c == 0), stop=(c == 7))
                            return ins
                        tqk = b.op("tensor", mmqk, inc="tn", waits=waits)
                        tevq = b.op("scalar", lambda e, p=p: e.activation(
                            QF[:, 0:512], p[0:64, :], AF.Copy),
                            inc="ac", waits=[tqk, P.get("qf_users")])
                        tevk = b.op("scalar", lambda e, p=p: e.activation(
                            QF[:, 512:1024], p[64:128, :], AF.Copy),
                            inc="ac", waits=[tqk])
                        P[f"pa{h % 2}"] = tevk
                        tsq = b.op("scalar", lambda e: e.activation(
                            QF[:, 1024:2048], QF[:, 0:1024], AF.Square),
                            inc="ac", waits=[tevk])
                        tnm = b.op("tensor", lambda e: (
                            e.matmul(pm[0:1, 0:512], ONCF[0:64, :],
                                     QF[:, 1024:1536], start=True, stop=True),
                            e.matmul(pv[0:1, 0:512], ONCF[0:64, :],
                                     QF[:, 1536:2048], start=True,
                                     stop=True))[-1],
                            inc="tn", waits=[tsq, t_ones, P.get("pm_users"),
                                             P.get("pv_users"),
                                             P.get("pv_users2")])
                        tq1 = b.op("vector", lambda e: (
                            e.tensor_scalar(RW[0:1, 0:512], pm[0:1, 0:512],
                                            1e-24, None, op0=OP.max),
                            e.tensor_scalar(RW[0:1, 512:1024], pv[0:1, 0:512],
                                            1e-24, None, op0=OP.max))[-1],
                            inc="vc", waits=[tnm])
                        tq2 = b.op("scalar", lambda e: e.activation(
                            RW[0:1, 0:1024], RW[0:1, 0:1024], AF.Ln),
                            inc="ac", waits=[tq1])
                        tq3 = b.op("scalar", lambda e: e.activation(
                            RW[0:1, 0:1024], RW[0:1, 0:1024], AF.Exp,
                            scale=-0.5), inc="ac", waits=[tq2])
                        P["pm_users"] = tq1
                        tbc2 = b.op("tensor", lambda e: (
                            e.matmul(pb[0:64, 0:512], ONEF[0:1, 0:64],
                                     RW[0:1, 0:512], start=True, stop=True),
                            e.matmul(pb[0:64, 512:1024], ONEF[0:1, 0:64],
                                     RW[0:1, 512:1024], start=True,
                                     stop=True))[-1],
                            inc="tn", waits=[tq3, P.get("pb_users")])
                        tqh = b.op("vector", lambda e: (
                            e.scalar_tensor_tensor(
                                QF[:, 2048:2560], QF[:, 0:512], QKC[:, 0:1],
                                pb[0:64, 0:512], op0=OP.mult, op1=OP.mult),
                            e.scalar_tensor_tensor(
                                QF[:, 2560:3072], QF[:, 512:1024],
                                QKC[:, 1:2], pb[0:64, 512:1024],
                                op0=OP.mult, op1=OP.mult))[-1],
                            inc="vc", waits=[tbc2, t_qkc, tsq])
                        P["pb_users"] = tqh
                        trt = b.op("vector", lambda e: (
                            e.tensor_scalar(QF[0:32, 3072:3584],
                                            QF[32:64, 2048:2560], -1.0, None,
                                            op0=OP.mult),
                            e.tensor_copy(QF[32:64, 3072:3584],
                                          QF[0:32, 2048:2560]),
                            e.tensor_scalar(QF[0:32, 3584:4096],
                                            QF[32:64, 2560:3072], -1.0, None,
                                            op0=OP.mult),
                            e.tensor_copy(QF[32:64, 3584:4096],
                                          QF[0:32, 2560:3072]))[-1],
                            inc="vc", waits=[tqh])
                        trr = b.op("vector", lambda e: (
                            e.tensor_mul(QF[:, 0:512], QF[:, 2048:2560],
                                         TBL[:, 0:512]),
                            e.tensor_mul(QF[:, 512:1024], QF[:, 3072:3584],
                                         TBL[:, 512:1024]),
                            e.tensor_add(QF[:, 0:512], QF[:, 0:512],
                                         QF[:, 512:1024]),
                            e.tensor_mul(QF[:, 1024:1536], QF[:, 2560:3072],
                                         TBL[:, 1024:1536]),
                            e.tensor_mul(QF[:, 1536:2048], QF[:, 3584:4096],
                                         TBL[:, 1536:2048]),
                            e.tensor_add(QF[:, 1024:1536], QF[:, 1024:1536],
                                         QF[:, 1536:2048]))[-1],
                            inc="vc", waits=[trt, ttb_l, tevq])
                        kslot = w % 2
                        tcst = b.op("vector", lambda e, h=h, w=w, kslot=kslot: (
                            e.tensor_scalar(QS[:, :], QF[:, 0:512],
                                            GAM[:, w:w + 1], None,
                                            op0=OP.mult),
                            e.tensor_scalar(
                                KR[:, h * 1024 + kslot * 512:
                                   h * 1024 + (kslot + 1) * 512],
                                QF[:, 1024:1536], GAM[:, 8 + w:8 + w + 1],
                                None, op0=OP.mult))[-1],
                            inc="vc", waits=[trr, t_gam, P.get("qs_users")])
                        P["qf_users"] = tcst
                        trsc = None
                        if w >= 1:
                            pslot = (w - 1) % 2
                            trsc = b.op("vector",
                                lambda e, h=h, pslot=pslot: e.tensor_scalar(
                                    KR[:, h * 1024 + pslot * 512:
                                       h * 1024 + (pslot + 1) * 512],
                                    KR[:, h * 1024 + pslot * 512:
                                       h * 1024 + (pslot + 1) * 512],
                                    GAM[:, 16:17], None, op0=OP.mult),
                                inc="vc", waits=[tcst])
                        t_pv = None
                        groups = range(2) if w >= 1 else range(1, 2)
                        for g in groups:
                            waits = [tcst, trsc, P.get("pbsim_users"),
                                     t_vr_last]
                            def mmsim(e, g=g, h=h, w=w):
                                ins = None
                                for j in range(4):
                                    kt = 4 * g + j
                                    wk = w - 1 if kt < 4 else w
                                    ks2 = wk % 2
                                    ins = e.matmul(
                                        pb[:, j * 512:(j + 1) * 512],
                                        KR[:, h * 1024 + ks2 * 512 +
                                           (kt % 4) * 128:
                                           h * 1024 + ks2 * 512 +
                                           (kt % 4 + 1) * 128],
                                        QS[:, :], start=True, stop=True)
                                return ins
                            tsm = b.op("tensor", mmsim, inc="tn", waits=waits)
                            tmk = b.op("vector", lambda e, g=g:
                                e.scalar_tensor_tensor(
                                    pb[:, :], pb[:, :], 1.0,
                                    FMK[:, g * 2048:(g + 1) * 2048],
                                    op0=OP.mult, op1=OP.add),
                                inc="vc", waits=[tsm, t_fmk])
                            tex = b.op("scalar", lambda e: e.activation(
                                PT4[:, :], pb[:, :], AF.Exp,
                                bias=NC0F[:, 0:1]), inc="ac",
                                waits=[tmk, t_nc0])
                            P["pbsim_users"] = tex
                            def mmpv(e, g=g, h=h, w=w):
                                ins = None
                                for j in range(4):
                                    kt = 4 * g + j
                                    gslot = (w * 4 - 4 + kt) % 8
                                    first = (kt == 0) or (w == 0 and kt == 4)
                                    ins = e.matmul(
                                        pv[:, :],
                                        VR[:, gslot * 520 + h * 65:
                                           gslot * 520 + (h + 1) * 65],
                                        PT4[:, j * 512:(j + 1) * 512],
                                        start=first, stop=(kt == 7))
                                return ins
                            t_pv = b.op("tensor", mmpv, inc="tn",
                                        waits=[tex])
                        tre = b.op("vector", lambda e: e.reciprocal(
                            RW[0:1, 1024:1536], pv[64:65, :]),
                            inc="vc", waits=[t_pv])
                        tbc3 = b.op("tensor", lambda e: e.matmul(
                            pm[0:64, 0:512], ONEF[0:1, 0:64],
                            RW[0:1, 1024:1536], start=True, stop=True),
                            inc="tn", waits=[tre, P.get("pm_users")])
                        tcpo = b.op("scalar", lambda e: e.activation(
                            QF[:, 0:512], pv[0:64, :], AF.Copy),
                            inc="ac", waits=[t_pv, tcst])
                        P["pv_users"] = tcpo
                        P["pv_users2"] = tre
                        r0 = (h % 2) * 64
                        ch = h // 2
                        t_oall[h] = b.op("vector",
                            lambda e, r0=r0, ch=ch: e.tensor_tensor(
                                OALL[r0:r0 + 64, ch * 512:(ch + 1) * 512],
                                QF[0:64, 0:512], pm[0:64, 0:512],
                                op=OP.mult),
                            inc="vc", waits=[tbc3, tcpo,
                                             P.get("oall_users")])
                        P["pm_users"] = t_oall[h]
                        P["qs_users"] = t_pv
                        P["qf_users"] = tcpo
                    P["vr_users"] = t_pv
                    P["tbl_users"] = trr
                    P["hs_users"] = tqk
                    t_y1 = {}
                    tao = None
                    for tt in range(4):
                        yw = [t_y1.get(tt - 2) or P.get(f"yt_users{tt % 2}")]
                        if lyr == 1 and w == 0:
                            yw += [P.get("ost0"), P.get("ost1")]
                        t_ytl[tt] = b.dma(
                            "sync", YT[:, (tt % 2) * 1024:(tt % 2 + 1) * 1024],
                            YinF[(w * 4 + tt) * 128:
                                 (w * 4 + tt + 1) * 128, :],
                            f"ytl{tt % 2}", waits=yw)
                        waits = [t_oall[6], t_oall[7], t_wo, t_ytl[tt],
                                 P.get("pbsim_users")]
                        def mmo(e, tt=tt):
                            ins = None
                            for half in range(2):
                                for ch in range(4):
                                    ins = e.matmul(
                                        pb[:, half * 512:(half + 1) * 512],
                                        OALL[:, ch * 512 + tt * 128:
                                             ch * 512 + (tt + 1) * 128],
                                        WO[:, ch * 1024 + half * 512:
                                           ch * 1024 + (half + 1) * 512],
                                        start=(ch == 0), stop=(ch == 3))
                            return ins
                        tao = b.op("tensor", mmo, inc="tn", waits=waits)
                        t_y1[tt] = b.op("vector", lambda e, tt=tt:
                            e.tensor_add(Y1T[:, tt * 1024:(tt + 1) * 1024],
                                         YT[:, (tt % 2) * 1024:
                                            (tt % 2 + 1) * 1024],
                                         pb[:, 0:1024]),
                            inc="vc", waits=[tao, P.get(f"y2ev{tt}")])
                        P["pbsim_users"] = t_y1[tt]
                        P[f"yt_users{tt % 2}"] = t_y1[tt]
                        tymw = b.dma("gpsimd",
                              io.YMBP[1 + (w * 4 + tt) * 128:
                                      1 + (w * 4 + tt + 1) * 128, :],
                              Y1T[:, tt * 1024:(tt + 1) * 1024],
                              "ymb", waits=[t_y1[tt]])
                    t_ymb_all = tymw
                    P["oall_users"] = tao
                    th2 = None
                    for c in range(8):
                        r0 = (1 + w * 512) if c < 4 else (w * 512)
                        th2 = b.dma("sync", HS2[:, c * 512:(c + 1) * 512],
                                    io.YMBP[r0:r0 + 512,
                                            c * 128:(c + 1) * 128], "h2l",
                                    transpose=True,
                                    waits=[t_ymb_all, P.get("hs2_users")]
                                    if c == 0 else [])
                    tst = None
                    for c in range(8):
                        tqq = b.op("scalar", lambda e, c=c: e.activation(
                            SQ[:, :], HS2[:, c * 512:(c + 1) * 512],
                            AF.Square), inc="ac",
                            waits=[th2, P.get("sq_users")])
                        tst = b.op("tensor", lambda e, c=c: (
                            e.matmul(pv[0:1, 0:512], ONCB[:, :],
                                     HS2[:, c * 512:(c + 1) * 512],
                                     start=(c == 0), stop=(c == 7)),
                            e.matmul(pm[0:1, 0:512], ONCF[:, :], SQ[:, :],
                                     start=(c == 0), stop=(c == 7)))[-1],
                            inc="tn",
                            waits=[tqq, P.get("pv_users"),
                                   P.get("pv_users2"), P.get("pm_users")]
                            if c == 0 else [tqq])
                        P["sq_users"] = tst
                    tr1 = b.op("vector", lambda e: (
                        e.tensor_scalar(RW[0:1, 0:512], pv[0:1, 0:512],
                                        1.0 / D, None, op0=OP.mult),
                        e.tensor_scalar(RW[0:1, 512:1024], pm[0:1, 0:512],
                                        1.0 / D, None, op0=OP.mult))[-1],
                        inc="vc", waits=[tst])
                    tr2 = b.op("vector", lambda e: (
                        e.tensor_mul(RW[0:1, 1024:1536], RW[0:1, 0:512],
                                     RW[0:1, 0:512]),
                        e.tensor_sub(RW[0:1, 512:1024], RW[0:1, 512:1024],
                                     RW[0:1, 1024:1536]))[-1],
                        inc="vc", waits=[tr1])
                    tr3 = b.op("scalar", lambda e: e.activation(
                        RW[0:1, 512:1024], RW[0:1, 512:1024], AF.Ln,
                        bias=EPSR[0:1, 0:1]), inc="ac", waits=[tr2])
                    tr4 = b.op("scalar", lambda e: e.activation(
                        RW[0:1, 512:1024], RW[0:1, 512:1024], AF.Exp,
                        scale=-0.5), inc="ac", waits=[tr3])
                    tbc4 = b.op("tensor", lambda e: (
                        e.matmul(pm[:, 0:512], ONEF[:, :], RW[0:1, 0:512],
                                 start=True, stop=True),
                        e.matmul(pb[:, 0:512], ONEF[:, :], RW[0:1, 512:1024],
                                 start=True, stop=True))[-1],
                        inc="tn", waits=[tr4, tr1, P.get("pbsim_users"),
                                         P.get("pm_users")])
                    t_xln = {}
                    for c in range(8):
                        t_xln[c] = b.op("vector", lambda e, c=c: (
                            e.tensor_sub(SQ[:, :],
                                         HS2[:, c * 512:(c + 1) * 512],
                                         pm[:, 0:512]),
                            e.tensor_mul(SQ[:, :], SQ[:, :], pb[:, 0:512]),
                            e.tensor_scalar(XLN[:, c * 512:(c + 1) * 512],
                                            SQ[:, :], LGC[:, c:c + 1],
                                            LGC[:, 8 + c:8 + c + 1],
                                            op0=OP.mult, op1=OP.add))[-1],
                            inc="vc", waits=[tbc4, tbc, P.get("sq_users"),
                                             P.get("xln_users")])
                        P["sq_users"] = t_xln[c]
                    P["pm_users"] = t_xln[7]
                    P["pbsim_users"] = t_xln[7]
                    P["hs2_users"] = t_xln[7]
                    P["pv_users"] = tr1
                    P["pv_users2"] = None
                    t_h3 = {}
                    for fc in range(22):
                        tw1 = load_w1(fc, [P.get(f"w1u{fc % 2}")])
                        wslc = HIDC[fc]
                        sl2 = fc % 2
                        waits = [t_xln[7], tw1, t_h3.get(fc - 1)]
                        if fc < 2:
                            waits += [P.get("pa0"), P.get("pa1")]
                        def mm1(e, fc=fc, sl2=sl2, wslc=wslc):
                            ins = None
                            for half in range(2):
                                p = pa[half]
                                for c in range(8):
                                    col = sl2 * 2048 + c * 256 + half * 128
                                    ins = e.matmul(
                                        p[0:wslc, :],
                                        W1P[:, col:col + wslc],
                                        XLN[:, c * 512:(c + 1) * 512],
                                        start=(c == 0), stop=(c == 7))
                            return ins
                        tm1 = b.op("tensor", mm1, inc="tn", waits=waits)
                        P[f"w1u{fc % 2}"] = tm1
                        tgl = b.op("scalar", lambda e, wslc=wslc, fc=fc:
                            e.activation(GELB[0:wslc, :], pa[1][0:wslc, :],
                                         AF.Gelu,
                                         bias=B1C[0:wslc, 22 + fc:23 + fc]),
                            inc="ac", waits=[tm1, tbc, P.get("gel_users")])
                        t_h3[fc] = b.op("vector", lambda e, fc=fc, wslc=wslc:
                            e.scalar_tensor_tensor(
                                H3[0:wslc, fc * 512:(fc + 1) * 512],
                                pa[0][0:wslc, :], B1C[0:wslc, fc:fc + 1],
                                GELB[0:wslc, :], op0=OP.add, op1=OP.mult),
                            inc="vc", waits=[tgl, P.get("h3_users")])
                        P["gel_users"] = t_h3[fc]
                        P[f"pa{0}"] = t_h3[fc]
                        P[f"pa{1}"] = t_h3[fc]
                    P["xln_users"] = tm1
                    tm2 = None
                    for tp in range(2):
                        for fc in range(22):
                            tw2 = load_w2(fc, [P.get(f"w2u{fc % 2}")])
                            waits = [t_h3.get(fc, t_h3[21]), tw2]
                            if fc == 0:
                                waits += [t_b2, t_ones, P.get("pby_users"),
                                          P.get("pbsim_users"), t_h3[21],
                                          t_h3[20]]
                            def mm2s(e, tp=tp, fc=fc):
                                ins = None
                                for tloc in range(2):
                                    tt = tp * 2 + tloc
                                    for half in range(2):
                                        ins = e.matmul(
                                            pb[:, tloc * 1024 + half * 512:
                                               tloc * 1024 +
                                               (half + 1) * 512],
                                            H3[0:HIDC[fc],
                                               fc * 512 + tt * 128:
                                               fc * 512 + (tt + 1) * 128],
                                            W2S[0:HIDC[fc],
                                                (fc % 2) * 1024 + half * 512:
                                                (fc % 2) * 1024 +
                                                (half + 1) * 512],
                                            start=(fc == 0), stop=False)
                                return ins
                            tm2f = b.op("tensor", mm2s, inc="tn", waits=waits)
                            P[f"w2u{fc % 2}"] = tm2f
                        def mm2b(e, tp=tp):
                            ins = None
                            for tloc in range(2):
                                for half in range(2):
                                    ins = e.matmul(
                                        pb[:, tloc * 1024 + half * 512:
                                           tloc * 1024 + (half + 1) * 512],
                                        ONEB[:, :],
                                        B2R[0:1, half * 512:(half + 1) * 512],
                                        start=False, stop=True)
                            return ins
                        tm2 = b.op("tensor", mm2b, inc="tn", waits=[])
                        for tloc in range(2):
                            tt = tp * 2 + tloc
                            ti = w * 4 + tt
                            ty2 = b.op("vector", lambda e, tt=tt, tloc=tloc:
                                e.tensor_add(
                                    O1[:, (tt % 2) * 1024:
                                       (tt % 2 + 1) * 1024],
                                    Y1T[:, tt * 1024:(tt + 1) * 1024],
                                    pb[:, tloc * 1024:(tloc + 1) * 1024]),
                                inc="vc", waits=[tm2, P.get(f"o1{tt % 2}")])
                            P["pby_users"] = ty2
                            P[f"y2ev{tt}"] = ty2
                            if lyr == 0:
                                b.dma("gpsimd",
                                      io.Y1F[ti * 128:(ti + 1) * 128, :],
                                      O1[:, (tt % 2) * 1024:
                                         (tt % 2 + 1) * 1024],
                                      f"o{tt % 2}", waits=[ty2])
                                ts2 = b.dma("gpsimd",
                                            io.Y1BP[1 + ti * 128:
                                                    1 + (ti + 1) * 128, :],
                                            O1[:, (tt % 2) * 1024:
                                               (tt % 2 + 1) * 1024],
                                            f"o{tt % 2}")
                                P[f"o1{tt % 2}"] = ts2
                                P[f"ost{tt % 2}"] = ts2
                            else:
                                ks = tt % 2
                                tlx = b.dma("sync", XT[:, :],
                                            io.x[ti * 128:(ti + 1) * 128, :],
                                            "cx", waits=[P.get("cx")])
                                tla = b.dma("sync", A0T[:, :],
                                            io.ATT0[ti * 128:
                                                    (ti + 1) * 128, :],
                                            "cx")
                                tx1 = b.op("vector", lambda e, ks=ks: (
                                    e.tensor_add(
                                        O1[:, ks * 1024:(ks + 1) * 1024],
                                        O1[:, ks * 1024:(ks + 1) * 1024],
                                        XT[:, :]),
                                    e.tensor_add(
                                        O1[:, ks * 1024:(ks + 1) * 1024],
                                        O1[:, ks * 1024:(ks + 1) * 1024],
                                        A0T[:, :]))[-1],
                                    inc="vc", waits=[tla, tlx, ty2])
                                b.dma("gpsimd",
                                      io.X1[ti * 128:(ti + 1) * 128, :],
                                      O1[:, ks * 1024:(ks + 1) * 1024],
                                      f"o{ks}", waits=[tx1])

                                def ln_pass(src_ap, dst_ap, base_tok):
                                    tna = b.op("vector",
                                        lambda e: e.tensor_reduce(
                                            ST[:, 0:1], src_ap, AX, OP.add,
                                            negate=True), inc="vc",
                                        waits=[base_tok])
                                    tnb = b.op("vector",
                                        lambda e: e.tensor_scalar(
                                            ST[:, 1:2], ST[:, 0:1],
                                            1.0 / D, None, op0=OP.mult),
                                        inc="vc", waits=[tna])
                                    tnc = b.op("vector",
                                        lambda e: e.tensor_scalar(
                                            dst_ap, src_ap, ST[:, 1:2],
                                            None, op0=OP.add),
                                        inc="vc", waits=[tnb])
                                    dl, dr = dst_ap[:, 0:512], dst_ap[:, 512:1024]
                                    tnd = b.op("scalar",
                                        lambda e, dl=dl, dr=dr: (
                                            e.activation(GELB[:, :], dl,
                                                         AF.Square),
                                            e.activation(SQ[:, :], dr,
                                                         AF.Square))[-1],
                                        inc="ac",
                                        waits=[tnc, P.get("xh2_users")])
                                    tne0 = b.op("vector",
                                        lambda e: (
                                            e.tensor_reduce(ST[:, 2:3],
                                                            GELB[:, :], AX,
                                                            OP.add),
                                            e.tensor_reduce(ST[:, 4:5],
                                                            SQ[:, :], AX,
                                                            OP.add))[-1],
                                        inc="vc", waits=[tnd])
                                    tne = b.op("vector",
                                        lambda e: e.tensor_add(
                                            ST[:, 2:3], ST[:, 2:3],
                                            ST[:, 4:5]),
                                        inc="vc", waits=[tne0])
                                    P["xh2_users"] = tne
                                    tnf = b.op("scalar",
                                        lambda e: e.activation(
                                            ST[:, 3:4], ST[:, 2:3], AF.Ln,
                                            bias=EPSR[:, 0:1],
                                            scale=1.0 / D),
                                        inc="ac", waits=[tne])
                                    tng = b.op("scalar",
                                        lambda e: e.activation(
                                            ST[:, 3:4], ST[:, 3:4], AF.Exp,
                                            scale=-0.5), inc="ac",
                                        waits=[tnf])
                                    return b.op("vector",
                                        lambda e: e.tensor_scalar(
                                            dst_ap, dst_ap, ST[:, 3:4],
                                            None, op0=OP.mult),
                                        inc="vc", waits=[tng])

                                xn1_ap = XT[:, :]
                                t_xn1 = ln_pass(
                                    O1[:, ks * 1024:(ks + 1) * 1024],
                                    xn1_ap, tx1)
                                tsn = b.dma("gpsimd",
                                            io.XN1F[ti * 128:
                                                    (ti + 1) * 128, :],
                                            xn1_ap, f"o{ks}", waits=[t_xn1])
                                b.dma("gpsimd",
                                      io.XN1BP[1 + ti * 128:
                                               1 + (ti + 1) * 128, :],
                                      xn1_ap, f"o{ks}")
                                xh_ap = A0T[:, :]
                                t_xh = ln_pass(xn1_ap, xh_ap, t_xn1)
                                tsh = b.dma("gpsimd",
                                            io.XHATB[ti * 128:
                                                     (ti + 1) * 128, :],
                                            xh_ap, f"o{ks}", waits=[t_xh])
                                P[f"o1{ks}"] = tsn
                                P["cx"] = t_xh
                                P[f"ost{ks}"] = tsh
                    P["h3_users"] = tm2
                if DEBUG_A0 and lyr == 0:
                    td = b.dma("gpsimd", io.DBG3[:, 0:4096],
                               H3[:, 0:4096], "dbg", waits=[P.get("h3_users")])
                    td = b.dma("gpsimd", io.DBG1[0:64, :],
                               H3[:, 4096:6144], "dbg")
                    td = b.dma("gpsimd", io.DBG2[0:64, :],
                               XLN[:, 0:2048], "dbg")
                    td = b.dma("gpsimd", io.DBG0[:, 0:2048], OALL[:, :], "dbg",
                               waits=[P.get("oall_users")])
                    td = b.dma("gpsimd", io.DBG0[0:64, 2048:2560], QS[:, :],
                               "dbg", waits=[P.get("qs_users")])
                    td = b.dma("gpsimd", io.DBG0[0:64, 4096:5120],
                               KR[:, 7 * 1024:8 * 1024], "dbg")
                    td = b.dma("gpsimd", io.DBG0[:, 6144:8192],
                               VR[:, 0:2048], "dbg")
                    for kk in ("h3_users", "xln_users", "oall_users",
                               "gel_users", "hs2_users", "sq_users"):
                        P[kk] = td
                    b.barrier_wait("gpsimd", [td])
                P["wfree"] = tm2
            b.barrier_wait("gpsimd", [P.get("ost0"), P.get("ost1")])
            b.finish(block)


PHASE_FN["l0_ffn"] = ph_l0_ffn



GROUPS = [["wprep", "ln0", "l0_attn", "l0_ffn", "l1_attn", "moe", "comb"]]


def kernel(**inputs):
    pool, _ = kernel_run(inputs, GROUPS, core_ids=list(range(8)), fused=True)
    out = np.stack([pool[ci]["OUT"] for ci in range(8)], 0)
    return out.astype(np.float32)


# version 10
# speedup vs baseline: 1.1459x; 1.1459x over previous
import numpy as np
import ml_dtypes
from contextlib import ExitStack
import concourse.bass as bass
import concourse.mybir as mybir
from concourse.bass_utils import run_bass_kernel_spmd

F32 = mybir.dt.float32
BF16 = mybir.dt.bfloat16
AF = mybir.ActivationFunctionType
OP = mybir.AluOpType
AX = mybir.AxisListType.X

T = 4096
D = 1024
NT = T // 128
NSL = T // 512
NE = 8
FD = 4096
HID = 2730
NEG = -1.0e30


def host_constants():
    c = {}
    c["identbf"] = np.eye(128, dtype=ml_dtypes.bfloat16)
    c["identf"] = np.eye(128, dtype=np.float32)
    m = np.zeros((4, 128, 512), np.float32)
    for i in range(4):
        p = np.arange(128)[:, None]
        cc = np.arange(512)[None, :]
        m[i] = np.where(cc >= p + 128 * i, 0.0, NEG)
    c["causal4"] = m
    fm = np.zeros((2, 128, 2048), np.float32)
    for g in range(2):
        for kk in range(4):
            kt = 4 * g + kk
            jp = kt * 128 + np.arange(128)[:, None]
            tp = np.arange(512)[None, :]
            valid = (tp <= jp) & (jp <= tp + 512)
            fm[g][:, kk * 512:(kk + 1) * 512] = np.where(valid, 0.0, NEG)
    c["ffnmask"] = fm
    half = np.arange(0, 64, 2, dtype=np.float64)
    invf = 1.0 / 10000.0 ** (half / 64)
    t = np.arange(T, dtype=np.float64)
    fr = t[:, None] * invf[None, :]
    base = (half + 0.4 * 64) / (1.4 * 64)
    power = (t - T // 2) / 256.0
    sc = base[None, :] ** power[:, None]
    cos, sin = np.cos(fr), np.sin(fr)
    def two(a):
        return np.concatenate([a, a], 1).T.astype(np.float32).copy()
    c["xq_c"] = two(cos * sc)
    c["xq_s"] = two(sin * sc)
    c["xk_c"] = two(cos / sc)
    c["xk_s"] = two(sin / sc)
    gam = np.stack([base ** (-(w * 512 + 256 - T // 2) / 256.0)
                    for w in range(NSL)], 1)
    gam = np.concatenate([gam, gam], 0)
    c["gamq8"] = (gam * 8.0).astype(np.float32)
    c["gamki"] = (1.0 / gam).astype(np.float32)
    b2 = np.concatenate([base ** 2, base ** 2], 0)[:, None]
    c["gamrr"] = b2.astype(np.float32)
    return c

CONSTS = None
DEBUG_A0 = False
OB_INTERNAL = True



SCRATCH = {
    "XNF":   ([T, D], F32, "ln0", False),
    "XNBP":  ([T + 8, D], BF16, "ln0", False),
    "ATT0":  ([T, D], F32, "l0_attn", False),
    "X1":    ([T, D], F32, "l0_ffn", False),
    "XN1F":  ([T, D], F32, "l0_ffn", False),
    "XN1BP": ([T + 8, D], BF16, "l0_ffn", False),
    "XHATB": ([T, D], BF16, "l0_ffn", False),
    "Y1F":   ([T, D], F32, "l0_ffn", True),
    "Y1BP":  ([T + 8, D], BF16, "l0_ffn", True),
    "YMBP":  ([T + 8, D], BF16, "l0_ffn", True),
    "ATT1":  ([T, D], F32, "l1_attn", False),
    "MGATE": ([T, 8], F32, "l1_attn", False),
    "OB":    ([D, T], BF16, "l1_attn", True),
    "W1B":   ([NE, FD, D], BF16, "wprep", False),
    "W2B":   ([NE, D, FD], BF16, "wprep", False),
    "CENB":  ([16, 1024], BF16, "wprep", False),
    "FQKB":  ([2, 1024, 1536], BF16, "wprep", False),
    "FWOB":  ([2, 512, 1024], BF16, "wprep", False),
    "FW1B":  ([2, 1024, 5460], BF16, "wprep", False),
    "FW2B":  ([2, 2730, 1024], BF16, "wprep", False),
    "ACCD":  ([T, D], F32, "moe", False),
    "DBG0":  ([128, 8192], F32, "DBGPH", False),
    "DBG1":  ([64, T], F32, "DBGPH", False),
    "DBG2":  ([64, T], F32, "DBGPH", False),
    "DBG3":  ([128, 4096], F32, "DBGPH", False),
}
CONSUMES = {
    "wprep":   [],
    "ln0":     [],
    "l0_attn": ["XNF", "XNBP"],
    "l0_ffn":  ["XNF", "XNBP", "ATT0", "FQKB", "FWOB", "FW1B", "FW2B"],
    "l1_attn": ["XN1BP", "CENB"],
    "moe":     ["XHATB", "W1B", "W2B", "MGATE"],
    "comb":    ["ACCD", "X1", "ATT1", "XN1F"],
}
INPUT_SPECS = {
    "x":             ([T, D], F32),
    "L0_attn_WQ":    ([16, 1024, 64], F32),
    "L0_attn_WK":    ([16, 1024, 64], F32),
    "L0_attn_WV_down": ([16, 1024, 64], F32),
    "L0_attn_WV_up": ([16, 64, 1024], F32),
    "L0_ffn_wqkv":   ([2, 1024, 1536], F32),
    "L0_ffn_q_scale": ([2, 64], F32),
    "L0_ffn_k_scale": ([2, 64], F32),
    "L0_ffn_wo":     ([2, 512, 1024], F32),
    "L0_ffn_ln_g":   ([2, 1024], F32),
    "L0_ffn_ln_b":   ([2, 1024], F32),
    "L0_ffn_w1":     ([2, 1024, 5460], F32),
    "L0_ffn_b1":     ([2, 5460], F32),
    "L0_ffn_w2":     ([2, 2730, 1024], F32),
    "L0_ffn_b2":     ([2, 1024], F32),
    "L1_attn_w_qkv": ([1024, 3072], F32),
    "L1_attn_b_qkv": ([3072], F32),
    "L1_attn_w_out": ([1024, 1024], F32),
    "L1_attn_b_out": ([1024], F32),
    "L1_ffn_centroids": ([8, 1024], F32),
    "L1_ffn_ln_g":   ([8, 1024], F32),
    "L1_ffn_ln_b":   ([8, 1024], F32),
    "L1_ffn_w1":     ([8, 4096, 1024], F32),
    "L1_ffn_b1":     ([8, 4096], F32),
    "L1_ffn_w2":     ([8, 1024, 4096], F32),
    "L1_ffn_b2":     ([8, 1024], F32),
}
CONST_SPECS = {
    "identbf": ([128, 128], BF16),
    "identf":  ([128, 128], F32),
    "causal4": ([4, 128, 512], BF16),
    "ffnmask": ([2, 128, 2048], BF16),
    "xq_c":    ([64, T], F32),
    "xq_s":    ([64, T], F32),
    "xk_c":    ([64, T], F32),
    "xk_s":    ([64, T], F32),
    "gamq8":   ([64, NSL], F32),
    "gamki":   ([64, NSL], F32),
    "gamrr":   ([64, 1], F32),
}
WANTS = {
    "wprep":   ["L1_ffn_w1", "L1_ffn_w2", "L1_ffn_centroids",
                "L0_ffn_wqkv", "L0_ffn_wo", "L0_ffn_w1", "L0_ffn_w2"],
    "ln0":     ["x"],
    "l0_attn": ["L0_attn_WQ", "L0_attn_WK", "L0_attn_WV_down", "L0_attn_WV_up",
                "causal4", "identf", "identbf"],
    "l0_ffn":  ["x", "L0_ffn_wqkv", "L0_ffn_q_scale", "L0_ffn_k_scale",
                "L0_ffn_wo", "L0_ffn_ln_g", "L0_ffn_ln_b", "L0_ffn_w1",
                "L0_ffn_b1", "L0_ffn_w2", "L0_ffn_b2", "identf",
                "ffnmask", "xq_c", "xq_s", "xk_c", "xk_s", "gamq8", "gamki",
                "gamrr"],
    "l1_attn": ["L1_attn_w_qkv", "L1_attn_b_qkv", "L1_attn_w_out",
                "L1_attn_b_out", "causal4", "identbf"],
    "moe":     ["L1_ffn_ln_g", "L1_ffn_ln_b", "L1_ffn_b1", "L1_ffn_b2",
                "identf"],
    "comb":    [],
}


class IO:
    def __init__(self, nc, phases, fused):
        self.t = {}
        wanted = set()
        for ph in phases:
            wanted |= set(WANTS[ph])
        self.wanted_inputs = set()
        for n, (shape, dt) in {**INPUT_SPECS, **CONST_SPECS}.items():
            if n in wanted:
                self.t[n] = nc.dram_tensor(n, shape, dt, kind="ExternalInput")
                self.wanted_inputs.add(n)
        produced = {n for n, (_, _, p, _) in SCRATCH.items()
                    if p in phases or (p == "DBGPH" and DEBUG_A0)}
        consumed = set()
        for ph in phases:
            consumed |= set(CONSUMES[ph])
        self.outputs = []
        for n, (shape, dt, prod, internal) in SCRATCH.items():
            if n == "OB":
                internal = OB_INTERNAL
            if n.startswith("DBG") and not DEBUG_A0:
                continue
            if n in produced:
                if fused or internal:
                    kind = "Internal"
                else:
                    kind = "ExternalOutput"
                    self.outputs.append(n)
            elif n in consumed:
                kind = "ExternalInput"
                self.wanted_inputs.add(n)
            else:
                continue
            self.t[n] = nc.dram_tensor(n, shape, dt, kind=kind)
        if "comb" in phases:
            self.t["OUT"] = nc.dram_tensor("OUT", [T, D], F32, kind="ExternalOutput")
            self.outputs.append("OUT")

    def __getattr__(self, n):
        try:
            return self.t[n].ap()
        except KeyError:
            raise AttributeError(n)


class Sems:
    def __init__(self, nc, stack, tag):
        self.nc, self.stack, self.tag = nc, stack, tag
        self.h = {}
        self.c = {}
    def hdl(self, name):
        if name not in self.h:
            self.h[name] = self.stack.enter_context(
                self.nc.semaphore(f"{self.tag}_{name}"))
            self.c[name] = 0
        return self.h[name]


ENGINES = ["sync", "gpsimd", "tensor", "vector", "scalar"]


class Bld:
    def __init__(self, nc, s):
        self.nc, self.s = nc, s
        self.q = {e: [] for e in ENGINES}
        self.seen = {e: {} for e in ENGINES}

    def op(self, eng, fn, inc=None, incby=1, waits=()):
        s = self.s
        tok = None
        if inc is not None:
            s.hdl(inc)
            s.c[inc] += incby
            tok = (inc, s.c[inc])
        wl = [w for w in waits if w is not None]
        seen = self.seen[eng]
        need = []
        for wn, wv in wl:
            if seen.get(wn, 0) < wv:
                need.append((wn, wv))
                seen[wn] = wv
        def emit(e, need=need, fn=fn, inc=inc, incby=incby):
            for wn, wv in need:
                e.wait_ge(s.hdl(wn), wv)
            ins = fn(e)
            if inc is not None:
                assert ins is not None
                ins.then_inc(s.hdl(inc), incby)
        self.q[eng].append(emit)
        return tok

    def dma(self, eng, out, in_, stream, waits=(), **kw):
        return self.op(eng, lambda e: e.dma_start(out=out, in_=in_, **kw),
                       inc=stream, incby=16, waits=waits)

    def barrier_wait(self, eng, toks):
        self.op(eng, lambda e: None, waits=toks)

    def finish(self, block):
        for name in ENGINES:
            ops = self.q[name]
            if not ops:
                continue
            def body(eng, ops=ops):
                for f in ops:
                    f(eng)
            getattr(block, name)(body)



def ph_wprep(nc, io, stack):
    s = Sems(nc, stack, "wp")
    NW1 = NE * (FD // 128)
    NW2 = NE * (D // 128)
    JOBS = []
    for e in range(NE):
        for r in range(FD // 128):
            JOBS.append((lambda io, e=e, r=r: (
                io.L1_ffn_w1[e, r * 128:(r + 1) * 128, :],
                io.W1B[e, r * 128:(r + 1) * 128, :], 128, D)))
    for e in range(NE):
        for r in range(D // 128):
            JOBS.append((lambda io, e=e, r=r: (
                io.L1_ffn_w2[e, r * 128:(r + 1) * 128, :],
                io.W2B[e, r * 128:(r + 1) * 128, :], 128, FD)))
    JOBS.append(lambda io: (io.L1_ffn_centroids, io.CENB[0:8, :], 8, 1024))
    for l in range(2):
        for r in range(8):
            JOBS.append((lambda io, l=l, r=r: (
                io.L0_ffn_wqkv[l, r * 128:(r + 1) * 128, :],
                io.FQKB[l, r * 128:(r + 1) * 128, :], 128, 1536)))
        for r in range(4):
            JOBS.append((lambda io, l=l, r=r: (
                io.L0_ffn_wo[l, r * 128:(r + 1) * 128, :],
                io.FWOB[l, r * 128:(r + 1) * 128, :], 128, 1024)))
        for r in range(8):
            JOBS.append((lambda io, l=l, r=r: (
                io.L0_ffn_w1[l, r * 128:(r + 1) * 128, :],
                io.FW1B[l, r * 128:(r + 1) * 128, :], 128, 5460)))
        for r in range(22):
            rows = 128 if r < 21 else 42
            JOBS.append((lambda io, l=l, r=r, rows=rows: (
                io.L0_ffn_w2[l, r * 128:r * 128 + rows, :],
                io.FW2B[l, r * 128:r * 128 + rows, :], rows, 1024)))
    NLD = len(JOBS)
    NBW = 4
    with (
        nc.sbuf_tensor("wp_a", [128, NBW * 5460], BF16) as wa,
        nc.Block() as block,
    ):
        @block.gpsimd
        def _(g):
            for i in range(NLD):
                k = i % NBW
                a, _, rows, w = JOBS[i](io)
                if i >= NBW:
                    g.wait_ge(s.hdl(f"st{k}"), 16 * (i // NBW))
                g.dma_start(out=wa[0:rows, k * 5460:k * 5460 + w], in_=a) \
                    .then_inc(s.hdl(f"ld{k}"), 16)

        @block.sync
        def _(sy):
            for i in range(NLD):
                k = i % NBW
                _, d, rows, w = JOBS[i](io)
                sy.wait_ge(s.hdl(f"ld{k}"), 16 * (i // NBW + 1))
                sy.dma_start(out=d, in_=wa[0:rows, k * 5460:k * 5460 + w]) \
                    .then_inc(s.hdl(f"st{k}"), 16)
            for k in range(NBW):
                sy.wait_ge(s.hdl(f"st{k}"),
                           16 * ((NLD - k + NBW - 1) // NBW))


def ph_ln0(nc, io, stack):
    s = Sems(nc, stack, "l0")
    nb = 2
    with (
        nc.sbuf_tensor("ln_x", [128, nb * D], F32) as xb,
        nc.sbuf_tensor("ln_o", [128, nb * D], F32) as ob,
        nc.sbuf_tensor("ln_sq", [128, nb * D], F32) as sqb,
        nc.sbuf_tensor("ln_st", [128, nb * 8], F32) as st,
        nc.sbuf_tensor("ln_eps", [128, 1], F32) as eps,
        nc.sbuf_tensor("ln_z", [1, D], BF16) as zrow,
        nc.Block() as block,
    ):
        VC = 5
        AC = 3
        @block.vector
        def _(v):
            v.memset(eps[:, :], 1e-5)
            v.memset(zrow[:, :], 0.0).then_inc(s.hdl("aux"), 1)
            for i in range(NT):
                k = i % nb
                X = xb[:, k * D:(k + 1) * D]
                O = ob[:, k * D:(k + 1) * D]
                SQ = sqb[:, k * D:(k + 1) * D]
                ns = st[:, k * 8 + 0:k * 8 + 1]
                mn = st[:, k * 8 + 1:k * 8 + 2]
                rstd = st[:, k * 8 + 3:k * 8 + 4]
                c = VC * i
                v.wait_ge(s.hdl(f"ld{k}"), 16 * (i // nb + 1))
                if i >= nb:
                    v.wait_ge(s.hdl(f"wr{k}"), 32 * (i // nb))
                v.tensor_reduce(ns, X, AX, OP.add, negate=True) \
                    .then_inc(s.hdl("vc"), 1)
                v.wait_ge(s.hdl("vc"), c + 1)
                v.tensor_scalar(mn, ns, 1.0 / D, None, op0=OP.mult) \
                    .then_inc(s.hdl("vc"), 1)
                v.wait_ge(s.hdl("vc"), c + 2)
                v.tensor_scalar(O, X, mn, None, op0=OP.add) \
                    .then_inc(s.hdl("vc"), 1)
                v.wait_ge(s.hdl("ac"), AC * i + 1)
                v.tensor_reduce(rstd, SQ, AX, OP.add) \
                    .then_inc(s.hdl("vc"), 1)
                v.wait_ge(s.hdl("ac"), AC * i + 3)
                v.tensor_scalar(O, O, rstd, None, op0=OP.mult) \
                    .then_inc(s.hdl("vc"), 1)

        @block.scalar
        def _(sc):
            for i in range(NT):
                k = i % nb
                O = ob[:, k * D:(k + 1) * D]
                SQ = sqb[:, k * D:(k + 1) * D]
                rstd = st[:, k * 8 + 3:k * 8 + 4]
                c = VC * i
                sc.wait_ge(s.hdl("vc"), c + 3)
                sc.activation(SQ, O, AF.Square).then_inc(s.hdl("ac"), 1)
                sc.wait_ge(s.hdl("vc"), c + 4)
                sc.activation(rstd, rstd, AF.Ln, bias=eps[:, 0:1], scale=1.0 / D) \
                    .then_inc(s.hdl("ac"), 1)
                sc.wait_ge(s.hdl("ac"), AC * i + 2)
                sc.activation(rstd, rstd, AF.Exp, scale=-0.5) \
                    .then_inc(s.hdl("ac"), 1)

        @block.sync
        def _(sy):
            for i in range(NT):
                k = i % nb
                X = xb[:, k * D:(k + 1) * D]
                if i >= nb:
                    sy.wait_ge(s.hdl("vc"), VC * (i - nb) + 3)
                sy.dma_start(out=X, in_=io.x[i * 128:(i + 1) * 128, :]) \
                    .then_inc(s.hdl(f"ld{k}"), 16)

        @block.gpsimd
        def _(g):
            g.wait_ge(s.hdl("aux"), 1)
            g.dma_start(out=io.XNBP[0:1, :], in_=zrow[:, :]).then_inc(s.hdl("zr"), 16)
            for i in range(NT):
                k = i % nb
                O = ob[:, k * D:(k + 1) * D]
                g.wait_ge(s.hdl("vc"), VC * i + 5)
                g.dma_start(out=io.XNF[i * 128:(i + 1) * 128, :], in_=O) \
                    .then_inc(s.hdl(f"wr{k}"), 16)
                g.dma_start(out=io.XNBP[1 + i * 128:1 + (i + 1) * 128, :], in_=O) \
                    .then_inc(s.hdl(f"wr{k}"), 16)
            g.wait_ge(s.hdl("wr0"), 32 * ((NT + 1) // 2))
            g.wait_ge(s.hdl("wr1"), 32 * (NT // 2))
            g.wait_ge(s.hdl("zr"), 16)


PHASE_FN = {
    "wprep": ph_wprep,
    "ln0": ph_ln0,
}


def build_program(phases, fused):
    nc = bass.Bass("TRN2", target_bir_lowering=False)
    with ExitStack() as stack:
        io = IO(nc, phases, fused)
        for ph in phases:
            PHASE_FN[ph](nc, io, stack)
    return nc, io



def kernel_run(inputs, groups, core_ids=None, trace=False, fused=None):
    global CONSTS
    if CONSTS is None:
        CONSTS = host_constants()
    if core_ids is None:
        core_ids = list(range(8))
    ncore = len(core_ids)
    if fused is None:
        fused = len(groups) == 1 and len(groups[0]) == len(PHASE_FN)
    x = np.ascontiguousarray(np.asarray(inputs["L0_attn_inputs"], np.float32))
    base = {}
    for n in INPUT_SPECS:
        if n == "x":
            continue
        base[n] = np.ascontiguousarray(np.asarray(inputs[n], np.float32))
    np_dt = {F32: np.float32, BF16: ml_dtypes.bfloat16}
    for n, (shape, dt) in CONST_SPECS.items():
        base[n] = np.ascontiguousarray(CONSTS[n].astype(np_dt[dt]))
    pool = [dict() for _ in range(ncore)]
    res = None
    for phases in groups:
        nc, io = build_program(phases, fused)
        in_maps = []
        for ci in range(ncore):
            m = {}
            for k in io.wanted_inputs:
                if k == "x":
                    m["x"] = x[ci]
                elif k in base:
                    m[k] = base[k]
                else:
                    m[k] = pool[ci][k]
            in_maps.append(m)
        res = run_bass_kernel_spmd(nc, in_maps, core_ids=core_ids, trace=trace)
        for ci in range(ncore):
            for k, v in res.results[ci].items():
                pool[ci][k] = v
    return pool, res


def ph_l0_attn(nc, io, stack):
    s = Sems(nc, stack, "a0")
    NH = 16
    with ExitStack() as ls:
        b = Bld(nc, s)
        sb = lambda name, shape, dt: ls.enter_context(nc.sbuf_tensor(name, shape, dt))
        psa = lambda name, shape, dt: ls.enter_context(nc.psum_tensor(name, shape, dt))

        XN = sb("a0_xn", [128, 8 * T], BF16)
        WQK = sb("a0_wqk", [128, 2 * 1024], BF16)
        Q = sb("a0_q", [64, T], BF16)
        K = sb("a0_k", [64, T], BF16)
        DT = sb("a0_dt", [64, T], BF16)
        ONE64B = sb("a0_one64b", [64, 1], BF16)
        DG = sb("a0_dg", [1, T], F32)
        PS = sb("a0_pscr", [128, 2048], BF16)
        LP = sb("a0_lp", [128, 2], F32)
        LC = sb("a0_lc", [128, 32], F32)
        DE = sb("a0_de", [128, 32], F32)
        RL = sb("a0_rl", [128, 32], F32)
        DCOL = sb("a0_dcol", [128, NH * 32], F32)
        NRM = sb("a0_nrm", [64, 4], F32)
        SC0 = sb("a0_sc0", [1, 8], F32)
        NC0 = sb("a0_nc0", [128, 2], F32)
        ONE64 = sb("a0_one64", [64, 1], F32)
        ONE128 = sb("a0_one128", [1, 128], F32)
        CAUS = sb("a0_caus", [128, 2048], BF16)
        IDF = sb("a0_idf", [128, 128], F32)
        IDB = sb("a0_idb", [128, 128], BF16)
        WVD = sb("a0_wvd", [128, 8 * 1024], BF16)
        WVU = sb("a0_wvu", [128, 8 * 1024], BF16)
        DS = sb("a0_ds", [128, 512], BF16)
        DP = sb("a0_dp", [128, 8 * 512], BF16)
        XT = sb("a0_xt", [128, 4 * 1024], F32)
        AT = sb("a0_at", [128, 2 * 1024], F32)

        pj = [psa("a0_pj0", [128, 512], F32), psa("a0_pj1", [128, 512], F32)]
        scp = psa("a0_scp", [128, 2048], F32)
        msc = psa("a0_msc", [128, 512], F32)
        tpp = psa("a0_tp", [64, 1024], BF16)

        with nc.Block() as block:
            t_xn = None
            for c in range(8):
                t_xn = b.dma("sync", XN[:, c * T:(c + 1) * T],
                             io.XNBP[1:T + 1, c * 128:(c + 1) * 128],
                             "xn", transpose=True)
            t_caus = None
            for i in range(4):
                t_caus = b.dma("sync", CAUS[:, i * 512:(i + 1) * 512],
                               io.causal4[i], "xn")
            t_idf = b.dma("sync", IDF[:, :], io.identf, "xn")
            t_idb = b.dma("sync", IDB[:, :], io.identbf, "xn")
            t_ones = b.op("vector", lambda e: (e.memset(ONE64[:, :], 1.0),
                                               e.memset(ONE64B[:, :], 1.0),
                                               e.memset(ONE128[:, :], 1.0))[-1],
                          inc="vc")
            t_wvd = None
            for n in range(NH):
                t_wvd = b.dma(
                    "gpsimd",
                    WVD.ap().rearrange("p (c n d) -> p c n d", c=8, n=NH)[:, :, n, :],
                    io.L0_attn_WV_down[n].rearrange("(c p) d -> p c d", p=128), "wv")
            t_wvu = b.dma(
                "gpsimd",
                WVU.ap().rearrange("p (c e) -> p c e", c=8),
                io.L0_attn_WV_up.rearrange("n dd e -> (n dd) e")
                                .rearrange("(c p) e -> p c e", p=128), "wv")

            headA_done = {}
            t_wqk = {}

            def load_wqk(n):
                k2 = n % 2
                waits = []
                if n >= 2:
                    waits.append(headA_done[n - 2])
                base = WQK.ap()[:, k2 * 1024:(k2 + 1) * 1024] \
                    .rearrange("p (c w) -> p c w", c=8)
                b.dma("gpsimd", base[:, :, 0:64],
                      io.L0_attn_WQ[n].rearrange("(c p) d -> p c d", p=128),
                      f"wq{k2}", waits=waits)
                return b.dma("gpsimd", base[:, :, 64:128],
                             io.L0_attn_WK[n].rearrange("(c p) d -> p c d", p=128),
                             f"wq{k2}")

            t_prev_expfree = None
            t_prev_mscfree = None
            t_prev_diag_evac = None
            t_qk_free = []
            t_prev_t17 = None
            t_prev_t18 = None
            t_wqk[0] = load_wqk(0)
            for n in range(NH):
                k2 = n % 2
                t_evac = {}
                t_mm = None
                for sl in range(NSL):
                    p = pj[sl % 2]
                    waits = [t_wqk[n], t_xn]
                    if sl >= 2:
                        waits.append(t_evac[sl - 2])
                    if t_prev_diag_evac is not None:
                        waits.append(t_prev_diag_evac)
                    def mmA(e, p=p, sl=sl, k2=k2):
                        ins = None
                        for c in range(8):
                            ins = e.matmul(
                                p[:, :],
                                WQK[:, k2 * 1024 + c * 128:k2 * 1024 + (c + 1) * 128],
                                XN[:, c * T + sl * 512:c * T + (sl + 1) * 512],
                                start=(c == 0), stop=(c == 7))
                        return ins
                    t_mm = b.op("tensor", mmA, inc="tn", waits=waits)
                    ew = [t_mm] + (t_qk_free if sl == 0 else [])
                    b.op("scalar", lambda e, p=p, sl=sl: e.activation(
                        Q[:, sl * 512:(sl + 1) * 512], p[0:64, :],
                        AF.Copy, scale=0.125), waits=ew)
                    t_evac[sl] = b.op("scalar",
                        lambda e, p=p, sl=sl: e.activation(
                            K[:, sl * 512:(sl + 1) * 512],
                            p[64:128, :], AF.Copy),
                        inc="ac", waits=[t_mm])
                headA_done[n] = t_mm
                t_qk = t_evac[NSL - 1]
                if n + 1 < NH:
                    t_wqk[n + 1] = load_wqk(n + 1)

                t1 = b.op("vector", lambda e: e.tensor_reduce(
                    NRM[:, 0:1], Q[:, :], AX, OP.max,
                    apply_absolute_value=True),
                    waits=[t_qk])
                t2 = b.op("vector", lambda e: e.tensor_reduce(
                    NRM[:, 1:2], K[:, :], AX, OP.max,
                    apply_absolute_value=True),
                    inc="vc", waits=[t_qk])
                t3 = b.op("scalar", lambda e: e.activation(
                    NRM[:, 2:4], NRM[:, 0:2], AF.Square), inc="ac", waits=[t2])
                waits = [t3, t_ones]
                if t_prev_mscfree is not None:
                    waits.append(t_prev_mscfree)
                t4 = b.op("tensor", lambda e: e.matmul(
                    msc[0:1, 0:2], ONE64[:, :], NRM[:, 2:4], start=True, stop=True),
                    inc="tn", waits=waits)
                t5 = b.op("vector", lambda e: e.tensor_copy(SC0[0:1, 0:2],
                                                            msc[0:1, 0:2]),
                          inc="vc", waits=[t4])
                t6 = b.op("vector", lambda e: e.tensor_mul(
                    SC0[0:1, 2:3], SC0[0:1, 0:1], SC0[0:1, 1:2]),
                    inc="vc", waits=[t5])
                t7 = b.op("scalar", lambda e: e.activation(
                    SC0[0:1, 3:4], SC0[0:1, 2:3], AF.Ln), inc="ac", waits=[t6])
                t8 = b.op("scalar", lambda e: e.activation(
                    SC0[0:1, 4:5], SC0[0:1, 3:4], AF.Exp, scale=0.5),
                    inc="ac", waits=[t7])
                t9 = b.op("vector", lambda e: e.tensor_scalar(
                    SC0[0:1, 5:6], SC0[0:1, 4:5], -1.0, None, op0=OP.mult),
                    inc="vc", waits=[t8])
                t9b = b.op("vector", lambda e: e.tensor_scalar(
                    SC0[0:1, 6:7], SC0[0:1, 4:5], -1.0, -2.772588722239781,
                    op0=OP.mult, op1=OP.add), inc="vc", waits=[t8])
                t10 = b.op("tensor", lambda e: (
                    e.matmul(msc[:, 4:5], ONE128[:, :], SC0[0:1, 5:6],
                             start=True, stop=True),
                    e.matmul(msc[:, 5:6], ONE128[:, :], SC0[0:1, 6:7],
                             start=True, stop=True))[-1],
                    inc="tn", waits=[t9, t9b, t5])
                t11 = b.op("vector", lambda e: e.tensor_copy(NC0[:, 0:2],
                                                             msc[:, 4:6]),
                           inc="vc", waits=[t10])

                t12 = b.op("vector", lambda e: e.tensor_tensor(
                    DT[:, :], Q[:, :], K[:, :], op=OP.mult),
                    inc="vc", waits=[t_qk])

                t_lc = {}
                tmm = None
                for j in range(NT):
                    g0 = j // 16
                    t_exp = {}
                    for g in range(g0, 2):
                        b0 = max(0, j // 4 - 4 * g)
                        waits = [t_qk]
                        if t_prev_expfree is not None:
                            waits.append(t_prev_expfree)
                        def mmS(e, j=j, g=g, b0=b0):
                            ins = None
                            for bb in range(b0, 4):
                                ins = e.matmul(
                                    scp[:, bb * 512:(bb + 1) * 512],
                                    K[:, j * 128:(j + 1) * 128],
                                    Q[:, (4 * g + bb) * 512:
                                      (4 * g + bb + 1) * 512],
                                    start=True, stop=True)
                            return ins
                        tmm = b.op("tensor", mmS, inc="tn", waits=waits)
                        tmask = None
                        if 4 * g <= j // 4 <= 4 * g + 3:
                            bb = j // 4 - 4 * g
                            tmask = b.op("vector",
                                lambda e, bb=bb, j=j: e.scalar_tensor_tensor(
                                    scp[:, bb * 512:(bb + 1) * 512],
                                    scp[:, bb * 512:(bb + 1) * 512], 1.0,
                                    CAUS[:, (j % 4) * 512:(j % 4 + 1) * 512],
                                    op0=OP.mult, op1=OP.add),
                                inc="vc", waits=[tmm, t_caus])
                        ew = [tmm, t11]
                        if tmask is not None:
                            ew.append(tmask)
                        if j >= 1:
                            ew.append(t_lc.get(j - 1))
                        t_exp[g] = b.op("scalar", lambda e, g=g, b0=b0: e.activation(
                            PS[:, 0:(4 - b0) * 512], scp[:, b0 * 512:2048], AF.Exp,
                            bias=NC0[:, 0:1], accum_out=LP[:, g:g + 1]),
                            inc="ac", waits=ew)
                        t_prev_expfree = t_exp[g]
                    if g0 == 0:
                        t_lc[j] = b.op("vector", lambda e, j=j: e.tensor_add(
                            LC[:, j:j + 1], LP[:, 0:1], LP[:, 1:2]),
                            inc="vc", waits=[t_exp[0], t_exp[1], t_prev_t17])
                    else:
                        t_lc[j] = b.op("vector", lambda e, j=j: e.tensor_copy(
                            LC[:, j:j + 1], LP[:, 1:2]),
                            inc="vc", waits=[t_exp[1], t_prev_t17])
                t_scores_last = tmm

                t_dgc = {}
                for sl in range(NSL):
                    p = pj[sl % 2]
                    waits = [t12, t_ones]
                    if sl >= 2:
                        waits.append(t_dgc[sl - 2])
                    tdm = b.op("tensor", lambda e, p=p, sl=sl: e.matmul(
                        p[0:1, :], ONE64B[:, :], DT[:, sl * 512:(sl + 1) * 512],
                        start=True, stop=True), inc="tn", waits=waits)
                    t_dgc[sl] = b.op("vector", lambda e, p=p, sl=sl: e.tensor_copy(
                        DG[0:1, sl * 512:(sl + 1) * 512], p[0:1, :]),
                        inc="vc", waits=[tdm])
                t_prev_diag_evac = t_dgc[NSL - 1]
                def trd(e):
                    ins = None
                    for j in range(NT):
                        ins = e.matmul(msc[:, j:j + 1],
                                       DG[0:1, j * 128:(j + 1) * 128],
                                       ONE128[0:1, 0:1], start=True, stop=True)
                    return ins
                t15 = b.op("tensor", trd, inc="tn",
                           waits=[t_dgc[NSL - 1], t11, t_idf])
                t16 = b.op("scalar", lambda e: e.activation(
                    DE[:, 0:32], msc[:, 0:32], AF.Exp, bias=NC0[:, 1:2]),
                    inc="ac", waits=[t15, t_prev_t18])
                t17 = b.op("vector", lambda e: e.reciprocal(RL[:, :], LC[:, :]),
                           inc="vc", waits=[t_lc[NT - 1], t16, t_prev_t18])
                t18 = b.op("vector", lambda e, n=n: e.tensor_tensor(
                    DCOL[:, n * 32:(n + 1) * 32], DE[:, 0:32], RL[:, :],
                    op=OP.mult), inc="vc", waits=[t17])
                t_prev_mscfree = t16
                t_qk_free = [t_scores_last, t12]
                t_prev_t17 = t17
                t_prev_t18 = t18

            t_ld = {}
            t_st = {}
            t_ds_free = None
            t_tp_free = None
            t_dp_free = {}
            t_comb = {}
            dcol_ready = t18
            for sl in range(NSL):
                for tt in range(4):
                    ti = sl * 4 + tt
                    slot = ti % 4
                    waits = []
                    if ti >= 4:
                        waits.append(t_comb.get(ti - 4))
                    t_ld[ti] = b.dma("sync", XT[:, slot * 1024:(slot + 1) * 1024],
                                     io.XNF[ti * 128:(ti + 1) * 128, :],
                                     f"ld{slot}", waits=waits)
                for tt in range(4):
                    ti = sl * 4 + tt
                    slot = ti % 2
                    tcp = None
                    for hg in range(2):
                        p = pj[(tt * 2 + hg) % 2]
                        waits = [t_wvd, t_xn]
                        if t_ds_free is not None:
                            waits.append(t_ds_free)
                        else:
                            waits.append(t_prev_diag_evac)
                        def mmD(e, p=p, sl=sl, tt=tt, hg=hg):
                            ins = None
                            for c in range(8):
                                ins = e.matmul(
                                    p[:, :],
                                    XN[:, c * T + sl * 512 + tt * 128:
                                       c * T + sl * 512 + (tt + 1) * 128],
                                    WVD[:, c * 1024 + hg * 512:
                                        c * 1024 + (hg + 1) * 512],
                                    start=(c == 0), stop=(c == 7))
                            return ins
                        tdt = b.op("tensor", mmD, inc="tn", waits=waits)
                        def dsg(e, p=p, sl=sl, tt=tt, hg=hg):
                            ins = None
                            for h in range(8):
                                nh = hg * 8 + h
                                ins = e.tensor_scalar(
                                    DS[:, h * 64:(h + 1) * 64],
                                    p[:, h * 64:(h + 1) * 64],
                                    DCOL[:, nh * 32 + sl * 4 + tt:
                                         nh * 32 + sl * 4 + tt + 1],
                                    None, op0=OP.mult)
                            return ins
                        wds = [tdt, dcol_ready]
                        if t_ds_free is not None:
                            wds.append(t_ds_free)
                        tds = b.op("vector", dsg, inc="vc", waits=wds)
                        def trp(e):
                            ins = None
                            for h in range(8):
                                ins = e.transpose(tpp[:, h * 128:(h + 1) * 128],
                                                  DS[:, h * 64:(h + 1) * 64],
                                                  IDB[:, :])
                            return ins
                        wtr = [tds, t_idb]
                        if t_tp_free is not None:
                            wtr.append(t_tp_free)
                        ttr = b.op("tensor", trp, inc="tn", waits=wtr)
                        t_ds_free = ttr
                        def cpy(e, tt=tt, hg=hg):
                            ins = None
                            for h in range(8):
                                ch = hg * 4 + h // 2
                                r0 = (h % 2) * 64
                                ins = e.tensor_copy(
                                    DP[r0:r0 + 64,
                                       ch * 512 + tt * 128:ch * 512 + (tt + 1) * 128],
                                    tpp[:, h * 128:(h + 1) * 128])
                            return ins
                        waits = [ttr]
                        if sl >= 1 and hg == 0 and tt == 0:
                            waits.append(t_dp_free.get(sl - 1))
                        tcp = b.op("vector", cpy, inc="vc", waits=waits)
                        t_tp_free = tcp
                    def mmE(e, tt=tt):
                        ins = None
                        for eh in range(2):
                            for c in range(8):
                                ins = e.matmul(
                                    scp[:, eh * 512:(eh + 1) * 512],
                                    DP[:, c * 512 + tt * 128:c * 512 + (tt + 1) * 128],
                                    WVU[:, c * 1024 + eh * 512:
                                        c * 1024 + (eh + 1) * 512],
                                    start=(c == 0), stop=(c == 7))
                        return ins
                    waits = [tcp, t_wvu]
                    if t_prev_expfree is not None:
                        waits.append(t_prev_expfree)
                        t_prev_expfree = None
                    if ti >= 1:
                        waits.append(t_comb.get(ti - 1))
                    tdel = b.op("tensor", mmE, inc="tn", waits=waits)
                    t_dp_free[sl] = tdel
                    t_comb[ti] = b.op("vector",
                        lambda e, slot=slot, ls=ti % 4: e.tensor_add(
                            AT[:, slot * 1024:(slot + 1) * 1024],
                            XT[:, ls * 1024:(ls + 1) * 1024], scp[:, 0:1024]),
                        inc="vc", waits=[tdel, t_ld[ti], t_st.get(ti - 2)])
                    t_st[ti] = b.dma("gpsimd",
                                     io.ATT0[ti * 128:(ti + 1) * 128, :],
                                     AT[:, slot * 1024:(slot + 1) * 1024],
                                     f"st{slot}", waits=[t_comb[ti]])
            if DEBUG_A0:
                tb = b.dma("gpsimd", io.DBG3[:, :], DP[:, :], "dbg",
                           waits=[t_comb[NT - 1]])
                tb = b.dma("sync", io.DBG0[:, 0:512], DCOL[:, :], "dbg",
                           waits=[dcol_ready])
                tb = b.dma("sync", io.DBG0[:, 512:544], LC[:, :], "dbg")
                tb = b.dma("sync", io.DBG0[:, 544:546], NC0[:, :], "dbg")
                tb = b.dma("sync", io.DBG0[0:1, 4096:8192], DG[0:1, :], "dbg")
                tb = b.dma("sync", io.DBG0[:, 546:578], DE[:, :], "dbg")
                tb = b.dma("sync", io.DBG0[:, 578:610], RL[:, :], "dbg")
                tb = b.dma("gpsimd", io.DBG1[:, :], Q[:, :], "dbg")
                tb = b.dma("gpsimd", io.DBG2[:, :], K[:, :], "dbg")
                b.barrier_wait("sync", [tb])
            b.barrier_wait("gpsimd", [t_st[NT - 2], t_st[NT - 1]])
            b.finish(block)


PHASE_FN["l0_attn"] = ph_l0_attn

def ph_l1_attn(nc, io, stack):
    s = Sems(nc, stack, "a1")
    NH = 16
    with ExitStack() as ls:
        b = Bld(nc, s)
        sb = lambda name, shape, dt: ls.enter_context(nc.sbuf_tensor(name, shape, dt))
        psa = lambda name, shape, dt: ls.enter_context(nc.psum_tensor(name, shape, dt))

        XN = sb("a1_xn", [128, 8 * T], BF16)
        VH = sb("a1_vh", [128, 32 * 1040], BF16)
        WX = sb("a1_wx", [128, 8 * 1024], BF16)
        WQK = sb("a1_wqk", [128, 2 * 1024], BF16)
        Qb = sb("a1_q", [64, T], BF16)
        Kb = sb("a1_k", [64, T], BF16)
        PT = sb("a1_pt", [128, 2048], BF16)
        OBF = sb("a1_obf", [64, 512], BF16)
        OSB = sb("a1_osb", [64, 512], F32)
        RLR = sb("a1_rlr", [1, 512], F32)
        NRM = sb("a1_nrm", [64, 4], F32)
        SC0 = sb("a1_sc0", [1, 8], F32)
        NC0 = sb("a1_nc0", [128, 1], F32)
        BQR = sb("a1_bqr", [1, 2 * 128], BF16)
        BVO = sb("a1_bvo", [1, 2 * 1024], BF16)
        ONE64 = sb("a1_one64", [64, 1], F32)
        ONE128 = sb("a1_one128", [1, 128], F32)
        ONEB = sb("a1_oneb", [1, 512], BF16)
        CAUS = sb("a1_caus", [128, 2048], BF16)
        IDB = sb("a1_idb", [128, 128], BF16)
        CENT = sb("a1_cent", [128, 128], BF16)
        ALPH = sb("a1_alph", [128, 8], F32)
        RMAX = sb("a1_rmax", [128, 1], F32)
        MSK = sb("a1_msk", [128, 8], F32)
        MGT = sb("a1_mgt", [128, 2 * 8], F32)
        OSL = sb("a1_osl", [128, 2 * 8 * 512], BF16)
        AT = sb("a1_at", [128, 1024], F32)

        pj = [psa("a1_pj0", [128, 512], F32), psa("a1_pj1", [128, 512], F32)]
        big4 = psa("a1_big4", [128, 2048], F32)
        pvp = psa("a1_pv", [65, 512], F32)
        msc = psa("a1_msc", [128, 512], F32)

        with nc.Block() as block:
            t_xn = None
            for c in range(8):
                t_xn = b.dma("sync", XN[:, c * T:(c + 1) * T],
                             io.XN1BP[1:T + 1, c * 128:(c + 1) * 128],
                             "xn", transpose=True)
            t_caus = None
            for i in range(4):
                t_caus = b.dma("sync", CAUS[:, i * 512:(i + 1) * 512],
                               io.causal4[i], "xn")
            t_idb = b.dma("sync", IDB[:, :], io.identbf, "xn")
            t_wv = b.dma("gpsimd",
                         WX.ap().rearrange("p (c v) -> p c v", c=8),
                         io.L1_attn_w_qkv[:, 2048:3072]
                             .rearrange("(c p) v -> p c v", p=128), "wv")
            t_bv = b.dma("gpsimd", BVO[0:1, 0:1024],
                         io.L1_attn_b_qkv[2048:3072].rearrange("(o v) -> o v", o=1),
                         "wv")
            t_bo = b.dma("gpsimd", BVO[0:1, 1024:2048],
                         io.L1_attn_b_out.rearrange("(o v) -> o v", o=1), "wv")
            t_ones = b.op("vector", lambda e: (
                e.memset(ONE64[:, :], 1.0), e.memset(ONE128[:, :], 1.0),
                e.memset(ONEB[:, :], 1.0),
                e.memset(VH.ap().rearrange("p (k h w) -> p k h w", k=32, h=16)
                         [:, :, :, 64:65], 1.0))[-1], inc="vc")

            t_vh = {}
            for ti in range(NT):
                pa = pj[ti % 2]
                pb = big4.ap()[:, (ti % 2) * 512:(ti % 2 + 1) * 512]
                waits = [t_xn, t_wv, t_bv, t_ones]
                if ti >= 2:
                    waits.append(t_vh[ti - 2])
                def mmV(e, pa=pa, pb=pb, ti=ti):
                    ins = None
                    for half, pp in ((0, pa[:, :]), (1, pb)):
                        for c in range(8):
                            e.matmul(pp,
                                     XN[:, c * T + ti * 128:c * T + (ti + 1) * 128],
                                     WX[:, c * 1024 + half * 512:
                                        c * 1024 + (half + 1) * 512],
                                     start=(c == 0), stop=False)
                        ins = e.matmul(pp, ONEB[0:1, 0:128],
                                       BVO[0:1, half * 512:(half + 1) * 512],
                                       start=False, stop=True)
                    return ins
                tmv = b.op("tensor", mmV, inc="tn", waits=waits)
                base = VH.ap()[:, ti * 1040:(ti + 1) * 1040] \
                    .rearrange("p (h w) -> p h w", h=16)
                def evV(e, pa=pa, pb=pb, base=base):
                    e.tensor_copy(base[:, 0:8, 0:64],
                                  pa.ap().rearrange("p (h w) -> p h w", h=8))
                    return e.tensor_copy(base[:, 8:16, 0:64],
                                         pb.rearrange("p (h w) -> p h w", h=8))
                t_vh[ti] = b.op("vector", evV, inc="vc", waits=[tmv, t_ones])
            t_vh_all = t_vh[NT - 1]

            t_wqk = {}
            headA_done = {}
            t_ob_store = {}

            def load_wqk(h):
                k2 = h % 2
                waits = []
                if h >= 2:
                    waits.append(headQK_done.get(h - 2))
                base = WQK.ap()[:, k2 * 1024:(k2 + 1) * 1024] \
                    .rearrange("p (c w) -> p c w", c=8)
                b.dma("gpsimd", base[:, :, 0:64],
                      io.L1_attn_w_qkv[:, h * 64:(h + 1) * 64]
                          .rearrange("(c p) d -> p c d", p=128),
                      f"wq{k2}", waits=waits)
                b.dma("gpsimd", BQR[0:1, k2 * 128:k2 * 128 + 64],
                      io.L1_attn_b_qkv[h * 64:(h + 1) * 64]
                          .rearrange("(o p) -> o p", o=1),
                      f"wq{k2}", waits=waits)
                b.dma("gpsimd", BQR[0:1, k2 * 128 + 64:k2 * 128 + 128],
                      io.L1_attn_b_qkv[1024 + h * 64:1024 + (h + 1) * 64]
                          .rearrange("(o p) -> o p", o=1), f"wq{k2}")
                return b.dma("gpsimd", base[:, :, 64:128],
                             io.L1_attn_w_qkv[:, 1024 + h * 64:1024 + (h + 1) * 64]
                                 .rearrange("(c p) d -> p c d", p=128), f"wq{k2}")

            t_qk_free = []
            t_prev_pt = {}
            headQK_done = {}
            t_onorm_prev = None
            t_ob_last = None
            t_wqk[0] = load_wqk(0)
            t_bq8 = None
            for h in range(NH):
                k2 = h % 2
                t_evac = {}
                t_mm = None
                for sl in range(NSL):
                    p = pj[sl % 2]
                    waits = [t_wqk[h], t_xn, t_vh_all]
                    if sl >= 2:
                        waits.append(t_evac[sl - 2])
                    def mmA(e, p=p, sl=sl, k2=k2):
                        for c in range(8):
                            e.matmul(
                                p[:, :],
                                WQK[:, k2 * 1024 + c * 128:k2 * 1024 + (c + 1) * 128],
                                XN[:, c * T + sl * 512:c * T + (sl + 1) * 512],
                                start=(c == 0), stop=False)
                        return e.matmul(p[:, :],
                                        BQR[0:1, k2 * 128:(k2 + 1) * 128],
                                        ONEB[0:1, :],
                                        start=False, stop=True)
                    t_mm = b.op("tensor", mmA, inc="tn",
                                waits=waits + [t_ones])
                    ew = [t_mm] + (t_qk_free if sl == 0 else [])
                    b.op("scalar", lambda e, p=p, sl=sl: e.activation(
                        Qb[:, sl * 512:(sl + 1) * 512], p[0:64, :], AF.Copy,
                        scale=0.125), waits=ew)
                    t_evac[sl] = b.op("scalar",
                        lambda e, p=p, sl=sl: e.activation(
                            Kb[:, sl * 512:(sl + 1) * 512], p[64:128, :], AF.Copy),
                        inc="ac", waits=[t_mm])
                headA_done[h] = t_mm
                t_qk = t_evac[NSL - 1]
                headQK_done[h] = t_qk
                if h + 1 < NH:
                    t_wqk[h + 1] = load_wqk(h + 1)

                t1 = b.op("vector", lambda e: e.tensor_reduce(
                    NRM[:, 0:1], Qb[:, :], AX, OP.max, apply_absolute_value=True),
                    waits=[t_qk])
                t2 = b.op("vector", lambda e: e.tensor_reduce(
                    NRM[:, 1:2], Kb[:, :], AX, OP.max, apply_absolute_value=True),
                    inc="vc", waits=[t_qk])
                t3 = b.op("scalar", lambda e: e.activation(
                    NRM[:, 2:4], NRM[:, 0:2], AF.Square), inc="ac", waits=[t2])
                t4 = b.op("tensor", lambda e: e.matmul(
                    msc[0:1, 0:2], ONE64[:, :], NRM[:, 2:4], start=True, stop=True),
                    inc="tn", waits=[t3, t_ones, t_onorm_prev])
                t5 = b.op("vector", lambda e: e.tensor_copy(
                    SC0[0:1, 0:2], msc[0:1, 0:2]), inc="vc", waits=[t4])
                t6 = b.op("vector", lambda e: e.tensor_mul(
                    SC0[0:1, 2:3], SC0[0:1, 0:1], SC0[0:1, 1:2]),
                    inc="vc", waits=[t5])
                t7 = b.op("scalar", lambda e: e.activation(
                    SC0[0:1, 3:4], SC0[0:1, 2:3], AF.Ln), inc="ac", waits=[t6])
                t8 = b.op("scalar", lambda e: e.activation(
                    SC0[0:1, 4:5], SC0[0:1, 3:4], AF.Exp, scale=0.5),
                    inc="ac", waits=[t7])
                t9 = b.op("vector", lambda e: e.tensor_scalar(
                    SC0[0:1, 5:6], SC0[0:1, 4:5], -1.0, None, op0=OP.mult),
                    inc="vc", waits=[t8])
                t10 = b.op("tensor", lambda e: e.matmul(
                    msc[:, 4:5], ONE128[:, :], SC0[0:1, 5:6], start=True, stop=True),
                    inc="tn", waits=[t9, t5])
                t11 = b.op("vector", lambda e: e.tensor_copy(NC0[:, 0:1],
                                                             msc[:, 4:5]),
                           inc="vc", waits=[t10])

                t_onorm = {}
                t_pv_free = [None]
                for jq in range(NSL):
                    t_pv = None
                    texp = None
                    for kg in range(jq + 1):
                        waits = [t_qk, t11]
                        tprev = t_prev_pt.get(0)
                        if tprev is not None:
                            waits.append(tprev)
                        def mmS(e, kg=kg, jq=jq):
                            ins = None
                            for j4 in range(4):
                                kt = 4 * kg + j4
                                ins = e.matmul(
                                    big4[:, j4 * 512:(j4 + 1) * 512],
                                    Kb[:, kt * 128:(kt + 1) * 128],
                                    Qb[:, jq * 512:(jq + 1) * 512],
                                    start=True, stop=True)
                            return ins
                        tms = b.op("tensor", mmS, inc="tn", waits=waits)
                        tmask = None
                        if kg == jq:
                            tmask = b.op("vector",
                                lambda e: e.scalar_tensor_tensor(
                                    big4[:, :], big4[:, :], 1.0, CAUS[:, :],
                                    op0=OP.mult, op1=OP.add),
                                inc="vc", waits=[tms, t_caus])
                        ew = [tms]
                        if tmask is not None:
                            ew.append(tmask)
                        if t_pv is not None:
                            ew.append(t_pv)
                        elif t_pv_free[0] is not None:
                            ew.append(t_pv_free[0][0])
                        texp = b.op("scalar", lambda e: e.activation(
                            PT[:, :], big4[:, :], AF.Exp, bias=NC0[:, 0:1]),
                            inc="ac", waits=ew)
                        t_prev_pt[0] = texp
                        pw = [texp, t_vh_all]
                        if kg == 0 and t_pv_free[0] is not None:
                            pw.extend(t_pv_free[0])
                        def mmpv(e, kg=kg, jq=jq, h=h):
                            ins = None
                            for j4 in range(4):
                                kt = 4 * kg + j4
                                ins = e.matmul(
                                    pvp[:, :],
                                    VH[:, kt * 1040 + h * 65:
                                       kt * 1040 + (h + 1) * 65],
                                    PT[:, j4 * 512:(j4 + 1) * 512],
                                    start=(kt == 0),
                                    stop=(kg == jq and j4 == 3))
                            return ins
                        t_pv = b.op("tensor", mmpv, inc="tn", waits=pw)
                    tre2 = b.op("vector", lambda e: e.reciprocal(
                        RLR[0:1, :], pvp[64:65, :]),
                        inc="vc", waits=[t_pv])
                    tbc = b.op("tensor", lambda e: e.matmul(
                        msc[0:64, 0:512], ONE128[0:1, 0:64],
                        RLR[0:1, :], start=True, stop=True),
                        inc="tn", waits=[tre2, t_onorm.get(jq - 1), t_onorm_prev])
                    tcp = b.op("scalar", lambda e: e.activation(
                        OSB[:, :], pvp[0:64, :], AF.Copy), inc="ac", waits=[t_pv])
                    wn = [tbc, tcp, t_ob_last]
                    t_onorm[jq] = b.op("vector", lambda e, jq=jq: e.tensor_tensor(
                        OBF[:, :], OSB[:, :],
                        msc[0:64, 0:512], op=OP.mult), inc="vc", waits=wn)
                    t_pv_free[0] = (tcp, tre2)
                    t_ob_last = b.dma("gpsimd",
                                      io.OB[h * 64:(h + 1) * 64,
                                            jq * 512:(jq + 1) * 512],
                                      OBF[:, :], "ob", waits=[t_onorm[jq]])
                t_ob_store[h] = t_ob_last
                t_onorm_prev = t_onorm[NSL - 1]
                t_qk_free = [t_pv, t2]

            t_wo = b.dma("gpsimd",
                         WX.ap().rearrange("p (c v) -> p c v", c=8),
                         io.L1_attn_w_out.rearrange("(c p) d -> p c d", p=128),
                         "wv", waits=[t_vh_all, headA_done[NH - 1]])
            t_osl = {}
            t_att = {}
            t_st = {}
            for sl in range(NSL):
                slot = sl % 2
                waits = [t_ob_store.get(h) for h in range(NH)]
                if sl >= 2:
                    waits.append(t_att.get(sl * 4 - 8 + 3))
                tld = None
                for c in range(8):
                    tld = b.dma("sync",
                                OSL[:, slot * 4096 + c * 512:
                                    slot * 4096 + (c + 1) * 512],
                                io.OB[c * 128:(c + 1) * 128,
                                      sl * 512:(sl + 1) * 512],
                                f"os{slot}", waits=waits if c == 0 else [])
                for tt in range(4):
                    ti = sl * 4 + tt
                    pa = pj[tt % 2]
                    pb = big4.ap()[:, (tt % 2) * 512:(tt % 2 + 1) * 512]
                    waits = [tld, t_wo, t_bo]
                    if ti >= 2:
                        waits.append(t_att.get(ti - 2))
                    def mmO(e, pa=pa, pb=pb, tt=tt, sl=sl):
                        ins = None
                        for half, pp in ((0, pa[:, :]), (1, pb)):
                            for c in range(8):
                                e.matmul(pp,
                                         OSL[:, sl % 2 * 4096 + c * 512 + tt * 128:
                                             sl % 2 * 4096 + c * 512 + (tt + 1) * 128],
                                         WX[:, c * 1024 + half * 512:
                                            c * 1024 + (half + 1) * 512],
                                         start=(c == 0), stop=False)
                            ins = e.matmul(pp, ONEB[0:1, 0:128],
                                           BVO[0:1, 1024 + half * 512:
                                               1024 + (half + 1) * 512],
                                           start=False, stop=True)
                        return ins
                    tmo = b.op("tensor", mmO, inc="tn", waits=waits)
                    t_att[ti] = b.op("vector", lambda e, pa=pa, pb=pb, tt=tt:
                        (e.tensor_copy(AT[:, 0:512], pa[:, :]),
                         e.tensor_copy(AT[:, 512:1024], pb))[-1],
                        inc="vc", waits=[tmo, t_st.get(ti - 1)])
                    t_st[ti] = b.dma("gpsimd", io.ATT1[ti * 128:(ti + 1) * 128, :],
                                     AT[:, 0:1024],
                                     "st0", waits=[t_att[ti]])

            tce = None
            for c in range(8):
                tce = b.dma("sync", CENT[:, c * 16:(c + 1) * 16],
                            io.CENB[:, c * 128:(c + 1) * 128], "ct",
                            transpose=True)
            t_mg = {}
            t_mgv_prev = None
            for ti in range(NT):
                ap = pj[ti % 2].ap()[:, 0:8]
                waits = [tce, t_xn]
                if ti >= 2:
                    waits.append(t_mg[ti - 2])
                else:
                    waits += [t_att.get(NT - 2), t_att.get(NT - 1)]
                def mmAf(e, ap=ap, ti=ti):
                    ins = None
                    for c in range(8):
                        ins = e.matmul(ap,
                                       XN[:, c * T + ti * 128:c * T + (ti + 1) * 128],
                                       CENT[:, c * 16:c * 16 + 8],
                                       start=(c == 0), stop=(c == 7))
                    return ins
                tma = b.op("tensor", mmAf, inc="tn", waits=waits)
                tsg = b.op("scalar", lambda e, ap=ap: e.activation(
                    ALPH[:, :], ap, AF.Sigmoid), inc="ac",
                    waits=[tma, t_mgv_prev])
                trm = b.op("vector", lambda e, ap=ap: e.tensor_reduce(
                    RMAX[:, 0:1], ap, AX, OP.max), inc="vc", waits=[tma])
                tmk = b.op("vector", lambda e, ap=ap: e.tensor_scalar(
                    MSK[:, :], ap, RMAX[:, 0:1], None, op0=OP.is_equal),
                    inc="vc", waits=[trm])
                tmg = b.op("vector", lambda e, ti=ti: e.tensor_tensor(
                    MGT[:, (ti % 2) * 8:(ti % 2 + 1) * 8], MSK[:, :], ALPH[:, :],
                    op=OP.mult), inc="vc",
                    waits=[tmk, tsg, t_mg.get(ti - 2)])
                t_mgv_prev = tmg
                t_mg[ti] = b.dma("gpsimd", io.MGATE[ti * 128:(ti + 1) * 128, :],
                                 MGT[:, (ti % 2) * 8:(ti % 2 + 1) * 8],
                                 f"mg{ti % 2}", waits=[tmg])
            b.barrier_wait("gpsimd", [t_st[NT - 2], t_st[NT - 1],
                                      t_mg[NT - 2], t_mg[NT - 1]])
            b.finish(block)


PHASE_FN["l1_attn"] = ph_l1_attn


def ph_moe(nc, io, stack):
    s = Sems(nc, stack, "mo")
    with ExitStack() as ls:
        b = Bld(nc, s)
        sb = lambda name, shape, dt: ls.enter_context(nc.sbuf_tensor(name, shape, dt))
        psa = lambda name, shape, dt: ls.enter_context(nc.psum_tensor(name, shape, dt))

        W1T = sb("mo_w1t", [128, 8 * FD], BF16)
        W2T = sb("mo_w2t", [128, 32 * 1024], BF16)
        XH = sb("mo_xh", [128, 2 * 8 * 512], BF16)
        XE = sb("mo_xe", [128, 8 * 512], BF16)
        HS = sb("mo_hs", [128, 32 * 512], BF16)
        MG = sb("mo_mg", [128, 32 * 8], F32)
        LNR = sb("mo_lnr", [32, 2 * 384], F32)
        GBC = sb("mo_gbc", [128, 2 * 48], F32)
        B2R = sb("mo_b2r", [1, 2 * 1024], BF16)
        ONEB = sb("mo_oneb", [1, 128], BF16)
        IDF = sb("mo_idf", [128, 128], F32)
        YS = sb("mo_ys", [128, 2 * 1024], F32)

        hp = [psa("mo_h0", [128, 512], F32), psa("mo_h1", [128, 512], F32)]
        yp = [psa("mo_y0", [128, 1024], F32), psa("mo_y1", [128, 1024], F32)]
        lnp = psa("mo_lnp", [128, 48], F32)

        with nc.Block() as block:
            t_idf = b.dma("sync", IDF[:, :], io.identf, "ix")
            t_ones = b.op("vector", lambda e: e.memset(ONEB[:, :], 1.0), inc="vc")
            t_mg = b.dma("gpsimd",
                         MG.ap().rearrange("p (ti e) -> p ti e", ti=NT),
                         io.MGATE.rearrange("(ti p) e -> p ti e", p=128), "mg")

            def load_wexp(e_, waits):
                tw = None
                for dc in range(8):
                    tw = b.dma("sync", W1T[:, dc * FD:(dc + 1) * FD],
                               io.W1B[e_, :, dc * 128:(dc + 1) * 128], "w1",
                               transpose=True, waits=waits if dc == 0 else [])
                tw2 = None
                for fc in range(32):
                    tw2 = b.dma("sync", W2T[:, fc * 1024:(fc + 1) * 1024],
                                io.W2B[e_, :, fc * 128:(fc + 1) * 128], "w2",
                                transpose=True, waits=waits if fc == 0 else [])
                return tw, tw2

            def load_lns(e_, sl2, waits):
                c0 = sl2 * 384
                b.dma("sync", LNR[0:8, c0:c0 + 128],
                      io.L1_ffn_ln_g[e_].rearrange("(c p) -> c p", p=128),
                      f"ln{sl2}", waits=waits)
                b.dma("sync", LNR[0:8, c0 + 128:c0 + 256],
                      io.L1_ffn_ln_b[e_].rearrange("(c p) -> c p", p=128),
                      f"ln{sl2}")
                b.dma("sync", LNR[0:32, c0 + 256:c0 + 384],
                      io.L1_ffn_b1[e_].rearrange("(c p) -> c p", p=128),
                      f"ln{sl2}")
                return b.dma("gpsimd", B2R[0:1, sl2 * 1024:(sl2 + 1) * 1024],
                             io.L1_ffn_b2[e_].rearrange("(o v) -> o v", o=1),
                             f"ln{sl2}")

            t_w = load_wexp(0, [])
            t_ln = {0: load_lns(0, 0, [])}
            t_acc = {}
            t_gb = {}
            prev_m1_last = None
            prev_m2_last = None
            t_ys_free = {}
            t_xe_prev = {}
            t_hev_prev = {}
            tt1_prev = None
            for e in range(NE):
                sl2 = e % 2
                c0 = sl2 * 384
                waits = [t_ln[e], t_idf]
                if e >= 1:
                    waits.append(t_gb[e - 1])
                def trn(en, c0=c0):
                    en.transpose(lnp[:, 0:8], LNR[0:8, c0:c0 + 128],
                                 IDF[0:8, 0:8])
                    en.transpose(lnp[:, 8:16], LNR[0:8, c0 + 128:c0 + 256],
                                 IDF[0:8, 0:8])
                    return en.transpose(lnp[:, 16:48],
                                        LNR[0:32, c0 + 256:c0 + 384],
                                        IDF[0:32, 0:32])
                tt1 = b.op("tensor", trn, inc="tn", waits=waits)
                t_gb[e] = b.op("vector", lambda e2, sl2=sl2: e2.tensor_copy(
                    GBC[:, sl2 * 48:(sl2 + 1) * 48], lnp[:, 0:48]),
                    inc="vc", waits=[tt1])
                if e + 1 < NE:
                    t_ln[e + 1] = load_lns(e + 1, 1 - sl2,
                                           [tt1_prev, prev_m2_last])
                tt1_prev = tt1

                tw1, tw2 = t_w
                for sl in range(NSL):
                    xsl = sl % 2
                    waits = [t_xe_prev.get(xsl)]
                    tx = None
                    for c in range(8):
                        tx = b.dma("sync",
                                   XH[:, xsl * 4096 + c * 512:
                                      xsl * 4096 + (c + 1) * 512],
                                   io.XHATB[sl * 512:(sl + 1) * 512,
                                            c * 128:(c + 1) * 128],
                                   f"xh{xsl}", transpose=True,
                                   waits=waits if c == 0 else [])
                    def xe_f(en, xsl=xsl, sl2=sl2):
                        ins = None
                        for c in range(8):
                            ins = en.tensor_scalar(
                                XE[:, c * 512:(c + 1) * 512],
                                XH[:, xsl * 4096 + c * 512:
                                   xsl * 4096 + (c + 1) * 512],
                                GBC[:, sl2 * 48 + c:sl2 * 48 + c + 1],
                                GBC[:, sl2 * 48 + 8 + c:sl2 * 48 + 8 + c + 1],
                                op0=OP.mult, op1=OP.add)
                        return ins
                    waits = [tx, t_gb[e], prev_m1_last]
                    t_xe = b.op("vector", xe_f, inc="vc", waits=waits)
                    t_xe_prev[xsl] = t_xe
                    t_hev = {}
                    for fc in range(32):
                        p = hp[fc % 2]
                        waits = [t_xe, tw1]
                        if fc >= 2:
                            waits.append(t_hev[fc - 2])
                        else:
                            waits.append(t_hev_prev.get(fc % 2))
                        def m1(en, p=p, fc=fc):
                            ins = None
                            for dc in range(8):
                                ins = en.matmul(
                                    p[:, :],
                                    W1T[:, dc * FD + fc * 128:
                                        dc * FD + (fc + 1) * 128],
                                    XE[:, dc * 512:(dc + 1) * 512],
                                    start=(dc == 0), stop=(dc == 7))
                            return ins
                        tm1 = b.op("tensor", m1, inc="tn", waits=waits)
                        hsw = [tm1, prev_m2_last, t_gb[e]]
                        if fc % 2 == 0:
                            t_hev[fc] = b.op("scalar",
                                lambda en, p=p, fc=fc, c0=c0: en.activation(
                                    HS[:, fc * 512:(fc + 1) * 512], p[:, :],
                                    AF.Relu,
                                    bias=GBC[:, (c0 // 8) + 16 + fc:
                                             (c0 // 8) + 17 + fc]),
                                inc="ac", waits=hsw)
                        else:
                            t_hev[fc] = b.op("vector",
                                lambda en, p=p, fc=fc, c0=c0: en.tensor_scalar(
                                    HS[:, fc * 512:(fc + 1) * 512], p[:, :],
                                    GBC[:, (c0 // 8) + 16 + fc:
                                        (c0 // 8) + 17 + fc], 0.0,
                                    op0=OP.add, op1=OP.max),
                                inc="vc", waits=hsw)
                    prev_m1_last = tm1
                    t_hev_prev = {0: t_hev[30], 1: t_hev[31]}
                    t_h_all = [t_hev[30], t_hev[31]]
                    for tt in range(4):
                        ti = sl * 4 + tt
                        p = yp[tt % 2]
                        waits = list(t_h_all) + [tw2, t_ones, t_ln[e],
                                                 t_ys_free.get(tt % 2)]
                        def m2(en, p=p, tt=tt, sl2=sl2):
                            ins = None
                            for half in range(2):
                                for fc in range(32):
                                    en.matmul(
                                        p[:, half * 512:(half + 1) * 512],
                                        HS[:, fc * 512 + tt * 128:
                                           fc * 512 + (tt + 1) * 128],
                                        W2T[:, fc * 1024 + half * 512:
                                            fc * 1024 + (half + 1) * 512],
                                        start=(fc == 0), stop=False)
                                ins = en.matmul(
                                    p[:, half * 512:(half + 1) * 512],
                                    ONEB[:, :],
                                    B2R[0:1, sl2 * 1024 + half * 512:
                                        sl2 * 1024 + (half + 1) * 512],
                                    start=False, stop=True)
                            return ins
                        tm2 = b.op("tensor", m2, inc="tn", waits=waits)
                        prev_m2_last = tm2
                        tys = b.op("vector", lambda en, p=p, tt=tt, ti=ti, e=e:
                            en.tensor_scalar(
                                YS[:, (tt % 2) * 1024:(tt % 2 + 1) * 1024],
                                p[:, :],
                                MG[:, ti * 8 + e:ti * 8 + e + 1], None,
                                op0=OP.mult),
                            inc="vc", waits=[tm2, t_mg, t_acc.get(ti)])
                        t_acc[ti] = b.dma(
                            "gpsimd", io.ACCD[ti * 128:(ti + 1) * 128, :],
                            YS[:, (tt % 2) * 1024:(tt % 2 + 1) * 1024],
                            f"ya{tt % 2}", waits=[tys],
                            accum_op=(OP.add if e > 0 else OP.bypass))
                        t_ys_free[tt % 2] = t_acc[ti]
                if e + 1 < NE:
                    t_w = load_wexp(e + 1, [prev_m1_last, prev_m2_last])
            b.barrier_wait("gpsimd", [t_acc[NT - 2], t_acc[NT - 1]])
            b.finish(block)


def ph_comb(nc, io, stack):
    s = Sems(nc, stack, "cb")
    with ExitStack() as ls:
        b = Bld(nc, s)
        sb = lambda name, shape, dt: ls.enter_context(nc.sbuf_tensor(name, shape, dt))
        CX = sb("cb_cx", [128, 4 * 1024], F32)
        CO = sb("cb_co", [128, 2 * 1024], F32)
        CA = sb("cb_ca", [128, 2 * 1024], F32)
        with nc.Block() as block:
            t_co = {}
            t_cst = {}
            for ti in range(NT):
                ks = ti % 2
                waits = [t_co.get(ti - 2)]
                tl1 = b.dma("sync", CX[:, (ks * 2) * 1024:(ks * 2 + 1) * 1024],
                            io.X1[ti * 128:(ti + 1) * 128, :], f"c{ks}",
                            waits=waits)
                tl2 = b.dma("sync", CX[:, (ks * 2 + 1) * 1024:(ks * 2 + 2) * 1024],
                            io.ATT1[ti * 128:(ti + 1) * 128, :], f"c{ks}")
                tl3 = b.dma("gpsimd", CO[:, ks * 1024:(ks + 1) * 1024],
                            io.XN1F[ti * 128:(ti + 1) * 128, :], f"d{ks}",
                            waits=waits)
                tl4 = b.dma("gpsimd", CA[:, ks * 1024:(ks + 1) * 1024],
                            io.ACCD[ti * 128:(ti + 1) * 128, :], f"d{ks}")
                tadd = b.op("vector", lambda en, ks=ks: (
                    en.tensor_add(CO[:, ks * 1024:(ks + 1) * 1024],
                                  CO[:, ks * 1024:(ks + 1) * 1024],
                                  CX[:, (ks * 2) * 1024:(ks * 2 + 1) * 1024]),
                    en.tensor_add(CO[:, ks * 1024:(ks + 1) * 1024],
                                  CO[:, ks * 1024:(ks + 1) * 1024],
                                  CX[:, (ks * 2 + 1) * 1024:(ks * 2 + 2) * 1024]),
                    en.tensor_add(CO[:, ks * 1024:(ks + 1) * 1024],
                                  CO[:, ks * 1024:(ks + 1) * 1024],
                                  CA[:, ks * 1024:(ks + 1) * 1024]))[-1],
                    inc="vc", waits=[tl1, tl2, tl3, tl4, t_cst.get(ti - 2)])
                t_co[ti] = tadd
                t_cst[ti] = b.dma("gpsimd", io.OUT[ti * 128:(ti + 1) * 128, :],
                                  CO[:, ks * 1024:(ks + 1) * 1024],
                                  f"o{ks}", waits=[tadd])
            b.barrier_wait("gpsimd", [t_cst[NT - 2], t_cst[NT - 1]])
            b.finish(block)


PHASE_FN["moe"] = ph_moe
PHASE_FN["comb"] = ph_comb



def ph_l0_ffn(nc, io, stack):
    s = Sems(nc, stack, "f0")
    with ExitStack() as ls:
        b = Bld(nc, s)
        sb = lambda name, shape, dt: ls.enter_context(nc.sbuf_tensor(name, shape, dt))
        psa = lambda name, shape, dt: ls.enter_context(nc.psum_tensor(name, shape, dt))

        HS = sb("f0_hs", [128, 8 * 512], BF16)
        HS2 = sb("f0_hs2", [128, 8 * 512], BF16)
        XLN = HS2
        WQK = sb("f0_wqk", [128, 8 * 1024], BF16)
        WV = sb("f0_wv", [128, 8 * 512], BF16)
        WO = sb("f0_wo", [128, 4 * 1024], BF16)
        W1P = sb("f0_w1p", [128, 2 * 2048], BF16)
        W2S = sb("f0_w2s", [128, 2 * 1024], BF16)
        YT = sb("f0_yt", [128, 2 * 1024], F32)
        Y1T = sb("f0_y1t", [128, 4 * 1024], F32)
        QS = sb("f0_qsb", [64, 512], BF16)
        KR = sb("f0_kr", [64, 8 * 1024], BF16)
        VR = sb("f0_vr", [128, 8 * 520], BF16)
        OALL = sb("f0_oall", [128, 4 * 512], BF16)
        H3 = sb("f0_h3", [128, 22 * 512], BF16)
        GELB = sb("f0_gel", [128, 512], F32)
        SQ = sb("f0_sq", [128, 512], F32)
        TBL = sb("f0_tbl", [64, 4 * 512], F32)
        QF = sb("f0_qf", [64, 8 * 512], F32)
        PT4 = sb("f0_pt4", [128, 2048], BF16)
        FMK = sb("f0_fmk", [128, 2 * 2048], BF16)
        RW = sb("f0_rw", [1, 1792], F32)
        B1C = sb("f0_b1c", [128, 44], F32)
        LGC = sb("f0_lgc", [128, 16], F32)
        B2R = sb("f0_b2r", [1, 1024], BF16)
        B1T = sb("f0_b1t", [21, 256], F32)
        B1TT = sb("f0_b1tt", [1, 128], F32)
        LNR = sb("f0_lnr", [8, 256], F32)
        QKC = sb("f0_qkc", [64, 2], F32)
        GAM = sb("f0_gam", [64, 17], F32)
        NC0F = sb("f0_nc0", [128, 1], F32)
        EPSR = sb("f0_eps", [128, 1], F32)
        ONEF = sb("f0_onef", [1, 128], F32)
        ONCB = sb("f0_oncb", [128, 1], BF16)
        ONCF = sb("f0_oncf", [128, 1], F32)
        ONEB = sb("f0_oneb", [1, 128], BF16)
        IDF = sb("f0_idf", [128, 128], F32)
        XT = sb("f0_xtt", [128, 1024], F32)
        A0T = sb("f0_a0t", [128, 1024], F32)
        ST = sb("f0_stt", [128, 16], F32)
        O1 = sb("f0_o1", [128, 2 * 1024], F32)

        pa = [psa("f0_pa0", [128, 512], F32), psa("f0_pa1", [128, 512], F32)]
        pb = psa("f0_pb", [128, 2048], F32)
        pv = psa("f0_pv", [65, 512], F32)
        pm = psa("f0_pm", [128, 512], F32)

        HIDC = [128] * 21 + [42]
        P = {}

        with nc.Block() as block:
            t_ones = b.op("vector", lambda e: (
                e.memset(ONEF[:, :], 1.0), e.memset(ONCB[:, :], 1.0),
                e.memset(ONCF[:, :], 1.0), e.memset(ONEB[:, :], 1.0),
                e.memset(EPSR[:, :], 1e-5), e.memset(B2R[:, :], 0.0),
                e.memset(VR.ap().rearrange("p (k h w) -> p k h w", k=8, h=8)
                         [:, :, :, 64:65], 1.0))[-1], inc="vc")
            t_fmk = None
            for g in range(2):
                t_fmk = b.dma("sync", FMK[:, g * 2048:(g + 1) * 2048],
                              io.ffnmask[g], "ms")
            t_idf = b.dma("sync", IDF[:, :], io.identf, "ms")
            t_gam = b.dma("sync", GAM[:, 0:8], io.gamq8, "ms")
            b.dma("sync", GAM[:, 8:16], io.gamki, "ms")
            t_gam = b.dma("sync", GAM[:, 16:17], io.gamrr, "ms")
            ZR = B2R
            t_zr = None
            for nm in ("YMBP", "Y1BP", "XN1BP"):
                t_zr = b.dma("gpsimd", getattr(io, nm)[0:1, :], ZR[0:1, :],
                             "zr", waits=[t_ones])

            for lyr in range(2):
                YinF = io.XNF if lyr == 0 else io.Y1F
                YinB = io.XNBP if lyr == 0 else io.Y1BP
                lw = [P.get("wfree")]
                wqv = io.FQKB[lyr]
                t_wqk = None
                for c in range(8):
                    wqc = WQK.ap()[:, c * 1024:(c + 1) * 1024] \
                        .rearrange("p (h w) -> p h w", h=8)
                    b.dma("sync", wqc[:, :, 0:64],
                          wqv[c * 128:(c + 1) * 128, 0:512]
                              .rearrange("p (h d) -> p h d", h=8), "wl",
                          waits=lw if c == 0 else [])
                    t_wqk = b.dma("sync", wqc[:, :, 64:128],
                                  wqv[c * 128:(c + 1) * 128, 512:1024]
                                      .rearrange("p (h d) -> p h d", h=8),
                                  "wl")
                t_wv = b.dma("sync",
                             WV.ap().rearrange("p (c v) -> p c v", c=8),
                             wqv[:, 1024:1536].rearrange("(c p) v -> p c v",
                                                         p=128), "wl")
                t_wo = b.dma("sync",
                             WO.ap().rearrange("p (c v) -> p c v", c=4),
                             io.FWOB[lyr].rearrange("(c p) v -> p c v",
                                                    p=128), "wl")
                t_b2 = b.dma("gpsimd", B2R[0:1, :],
                             io.L0_ffn_b2[lyr].rearrange("(o v) -> o v", o=1),
                             "wl", waits=[t_zr])
                b.dma("sync", B1T[0:21, 0:128],
                      io.L0_ffn_b1[lyr, 0:2688].rearrange("(c p) -> c p",
                                                          p=128), "bl",
                      waits=lw)
                b.dma("sync", B1T[0:21, 128:256],
                      io.L0_ffn_b1[lyr, 2730:2730 + 2688]
                          .rearrange("(c p) -> c p", p=128), "bl")
                b.dma("sync", B1TT[0:1, 0:42],
                      io.L0_ffn_b1[lyr, 2688:2730].rearrange("(o p) -> o p",
                                                             o=1), "bl")
                t_b1 = b.dma("sync", B1TT[0:1, 64:106],
                             io.L0_ffn_b1[lyr, 2730 + 2688:5460]
                                 .rearrange("(o p) -> o p", o=1), "bl")
                b.dma("sync", LNR[0:8, 0:128],
                      io.L0_ffn_ln_g[lyr].rearrange("(c p) -> c p", p=128),
                      "bl")
                t_lnr = b.dma("sync", LNR[0:8, 128:256],
                              io.L0_ffn_ln_b[lyr].rearrange("(c p) -> c p",
                                                            p=128), "bl")
                b.dma("sync", QKC[:, 0:1],
                      io.L0_ffn_q_scale[lyr].rearrange("(p o) -> p o", o=1),
                      "bl")
                t_qkc = b.dma("sync", QKC[:, 1:2],
                              io.L0_ffn_k_scale[lyr].rearrange("(p o) -> p o",
                                                               o=1), "bl")
                b.dma("sync", RW[0:1, 1536:1600],
                      io.L0_ffn_q_scale[lyr].rearrange("(o p) -> o p", o=1),
                      "bl")
                t_qkr = b.dma("sync", RW[0:1, 1600:1664],
                              io.L0_ffn_k_scale[lyr].rearrange("(o p) -> o p",
                                                               o=1), "bl")

                waits = [t_b1, t_lnr, t_idf, t_ones, P.get("pm_users"),
                         P.get("gel_users"), P.get("xln_users")]
                def trb(e):
                    e.transpose(pm[:, 0:21], B1T[0:21, 0:128], IDF[0:21, 0:21])
                    e.matmul(pm[0:42, 21:22], B1TT[0:1, 0:42], ONEF[0:1, 0:1],
                             start=True, stop=True)
                    e.transpose(pm[:, 22:43], B1T[0:21, 128:256],
                                IDF[0:21, 0:21])
                    e.matmul(pm[0:42, 43:44], B1TT[0:1, 64:106],
                             ONEF[0:1, 0:1], start=True, stop=True)
                    e.transpose(pm[:, 44:52], LNR[0:8, 0:128], IDF[0:8, 0:8])
                    return e.transpose(pm[:, 52:60], LNR[0:8, 128:256],
                                       IDF[0:8, 0:8])
                ttb = b.op("tensor", trb, inc="tn", waits=waits)
                tbc = b.op("vector", lambda e: (
                    e.tensor_copy(B1C[:, 0:44], pm[:, 0:44]),
                    e.tensor_copy(LGC[:, 0:16], pm[:, 44:60]))[-1],
                    inc="vc", waits=[ttb])
                t1 = b.op("vector", lambda e: e.tensor_mul(
                    RW[0:1, 1664:1728], RW[0:1, 1536:1600],
                    RW[0:1, 1600:1664]), inc="vc", waits=[t_qkr])
                t2 = b.op("vector", lambda e: e.tensor_reduce(
                    RW[0:1, 1728:1729], RW[0:1, 1664:1728], AX, OP.max,
                    apply_absolute_value=True), inc="vc", waits=[t1])
                t3 = b.op("vector", lambda e: e.tensor_scalar(
                    RW[0:1, 1729:1730], RW[0:1, 1728:1729], -8.0, None,
                    op0=OP.mult), inc="vc", waits=[t2])
                t4 = b.op("tensor", lambda e: e.matmul(
                    pm[:, 60:61], ONEF[:, :], RW[0:1, 1729:1730],
                    start=True, stop=True), inc="tn", waits=[t3, ttb])
                t_nc0 = b.op("vector", lambda e: e.tensor_copy(
                    NC0F[:, :], pm[:, 60:61]), inc="vc", waits=[t4])
                P["pm_users"] = t_nc0

                def load_w1(fc, waits):
                    sl2 = fc % 2
                    wslc = HIDC[fc]
                    base = W1P.ap()[:, sl2 * 2048:(sl2 + 1) * 2048] \
                        .rearrange("p (c f) -> p c f", c=8)
                    b.dma("sync", base[:, :, 0:wslc],
                          io.FW1B[lyr][:, fc * 128:fc * 128 + wslc]
                              .rearrange("(c p) f -> p c f", p=128),
                          f"w1{sl2}", waits=waits)
                    return b.dma("sync", base[:, :, 128:128 + wslc],
                                 io.FW1B[lyr][:, 2730 + fc * 128:
                                              2730 + fc * 128 + wslc]
                                     .rearrange("(c p) f -> p c f", p=128),
                                 f"w1{sl2}")

                def load_w2(fc, waits):
                    sl2 = fc % 2
                    return b.dma("sync",
                                 W2S[0:HIDC[fc], sl2 * 1024:(sl2 + 1) * 1024],
                                 io.FW2B[lyr][fc * 128:
                                              fc * 128 + HIDC[fc], :],
                                 f"w2{sl2}", waits=waits)

                for w in range(NSL):
                    th = None
                    for c in range(8):
                        r0 = (1 + w * 512) if c < 4 else (w * 512)
                        hw = [P.get("hs_users")]
                        if lyr == 1 and w == 0:
                            hw += [P.get("ost0"), P.get("ost1")]
                        th = b.dma("sync", HS[:, c * 512:(c + 1) * 512],
                                   YinB[r0:r0 + 512, c * 128:(c + 1) * 128],
                                   "hsl", transpose=True,
                                   waits=hw if c == 0 else [])
                    t_ytl = {}
                    t_vr_last = None
                    for tt in range(4):
                        p = pa[tt % 2]
                        waits = [th, t_wv, t_ones, P.get(f"pa{tt % 2}")]
                        def mmv(e, p=p, tt=tt):
                            ins = None
                            for c in range(8):
                                ins = e.matmul(
                                    p[:, :],
                                    HS[:, c * 512 + tt * 128:
                                       c * 512 + (tt + 1) * 128],
                                    WV[:, c * 512:(c + 1) * 512],
                                    start=(c == 0), stop=(c == 7))
                            return ins
                        tv = b.op("tensor", mmv, inc="tn", waits=waits)
                        slot = (w * 4 + tt) % 8
                        t_vr_last = b.op("vector", lambda e, p=p, slot=slot:
                            e.tensor_copy(
                                VR.ap().rearrange("p (k h z) -> p k h z",
                                                  k=8, h=8)[:, slot, :, 0:64],
                                p.ap().rearrange("p (h z) -> p h z", h=8)),
                            inc="vc", waits=[tv, P.get("vr_users")])
                        P[f"pa{tt % 2}"] = t_vr_last
                    ttb_l = None
                    for i, nmt in enumerate(("xq_c", "xq_s", "xk_c", "xk_s")):
                        ttb_l = b.dma("gpsimd", TBL[:, i * 512:(i + 1) * 512],
                                      getattr(io, nmt)[:, w * 512:
                                                       (w + 1) * 512], "tbl",
                                      waits=[P.get("tbl_users")]
                                      if i == 0 else [])
                    t_oall = {}
                    tqk = None
                    for h in range(8):
                        p = pa[h % 2]
                        waits = [th, t_wqk, P.get(f"pa{h % 2}")]
                        def mmqk(e, p=p, h=h):
                            ins = None
                            for c in range(8):
                                ins = e.matmul(
                                    p[:, :],
                                    WQK[:, c * 1024 + h * 128:
                                        c * 1024 + (h + 1) * 128],
                                    HS[:, c * 512:(c + 1) * 512],
                                    start=(c == 0), stop=(c == 7))
                            return ins
                        tqk = b.op("tensor", mmqk, inc="tn", waits=waits)
                        tevq = b.op("scalar", lambda e, p=p: e.activation(
                            QF[:, 0:512], p[0:64, :], AF.Copy),
                            inc="ac", waits=[tqk, P.get("qf_users")])
                        tevk = b.op("scalar", lambda e, p=p: e.activation(
                            QF[:, 512:1024], p[64:128, :], AF.Copy),
                            inc="ac", waits=[tqk])
                        P[f"pa{h % 2}"] = tevk
                        tsq = b.op("scalar", lambda e: e.activation(
                            QF[:, 1024:2048], QF[:, 0:1024], AF.Square),
                            inc="ac", waits=[tevk])
                        tnm = b.op("tensor", lambda e: (
                            e.matmul(pm[0:1, 0:512], ONCF[0:64, :],
                                     QF[:, 1024:1536], start=True, stop=True),
                            e.matmul(pv[0:1, 0:512], ONCF[0:64, :],
                                     QF[:, 1536:2048], start=True,
                                     stop=True))[-1],
                            inc="tn", waits=[tsq, t_ones, P.get("pm_users"),
                                             P.get("pv_users"),
                                             P.get("pv_users2")])
                        tq1 = b.op("vector", lambda e: (
                            e.tensor_scalar(RW[0:1, 0:512], pm[0:1, 0:512],
                                            1e-24, None, op0=OP.max),
                            e.tensor_scalar(RW[0:1, 512:1024], pv[0:1, 0:512],
                                            1e-24, None, op0=OP.max))[-1],
                            inc="vc", waits=[tnm])
                        tq2 = b.op("scalar", lambda e: e.activation(
                            RW[0:1, 0:1024], RW[0:1, 0:1024], AF.Ln),
                            inc="ac", waits=[tq1])
                        tq3 = b.op("scalar", lambda e: e.activation(
                            RW[0:1, 0:1024], RW[0:1, 0:1024], AF.Exp,
                            scale=-0.5), inc="ac", waits=[tq2])
                        P["pm_users"] = tq1
                        tbc2 = b.op("tensor", lambda e: (
                            e.matmul(pb[0:64, 0:512], ONEF[0:1, 0:64],
                                     RW[0:1, 0:512], start=True, stop=True),
                            e.matmul(pb[0:64, 512:1024], ONEF[0:1, 0:64],
                                     RW[0:1, 512:1024], start=True,
                                     stop=True))[-1],
                            inc="tn", waits=[tq3, P.get("pb_users")])
                        tqh = b.op("vector", lambda e: (
                            e.scalar_tensor_tensor(
                                QF[:, 2048:2560], QF[:, 0:512], QKC[:, 0:1],
                                pb[0:64, 0:512], op0=OP.mult, op1=OP.mult),
                            e.scalar_tensor_tensor(
                                QF[:, 2560:3072], QF[:, 512:1024],
                                QKC[:, 1:2], pb[0:64, 512:1024],
                                op0=OP.mult, op1=OP.mult))[-1],
                            inc="vc", waits=[tbc2, t_qkc, tsq])
                        P["pb_users"] = tqh
                        trt = b.op("vector", lambda e: (
                            e.tensor_scalar(QF[0:32, 3072:3584],
                                            QF[32:64, 2048:2560], -1.0, None,
                                            op0=OP.mult),
                            e.tensor_copy(QF[32:64, 3072:3584],
                                          QF[0:32, 2048:2560]),
                            e.tensor_scalar(QF[0:32, 3584:4096],
                                            QF[32:64, 2560:3072], -1.0, None,
                                            op0=OP.mult),
                            e.tensor_copy(QF[32:64, 3584:4096],
                                          QF[0:32, 2560:3072]))[-1],
                            inc="vc", waits=[tqh])
                        trr = b.op("vector", lambda e: (
                            e.tensor_mul(QF[:, 0:512], QF[:, 2048:2560],
                                         TBL[:, 0:512]),
                            e.tensor_mul(QF[:, 512:1024], QF[:, 3072:3584],
                                         TBL[:, 512:1024]),
                            e.tensor_add(QF[:, 0:512], QF[:, 0:512],
                                         QF[:, 512:1024]),
                            e.tensor_mul(QF[:, 1024:1536], QF[:, 2560:3072],
                                         TBL[:, 1024:1536]),
                            e.tensor_mul(QF[:, 1536:2048], QF[:, 3584:4096],
                                         TBL[:, 1536:2048]),
                            e.tensor_add(QF[:, 1024:1536], QF[:, 1024:1536],
                                         QF[:, 1536:2048]))[-1],
                            inc="vc", waits=[trt, ttb_l, tevq])
                        kslot = w % 2
                        tcst = b.op("vector", lambda e, h=h, w=w, kslot=kslot: (
                            e.tensor_scalar(QS[:, :], QF[:, 0:512],
                                            GAM[:, w:w + 1], None,
                                            op0=OP.mult),
                            e.tensor_scalar(
                                KR[:, h * 1024 + kslot * 512:
                                   h * 1024 + (kslot + 1) * 512],
                                QF[:, 1024:1536], GAM[:, 8 + w:8 + w + 1],
                                None, op0=OP.mult))[-1],
                            inc="vc", waits=[trr, t_gam, P.get("qs_users")])
                        P["qf_users"] = tcst
                        trsc = None
                        if w >= 1:
                            pslot = (w - 1) % 2
                            trsc = b.op("vector",
                                lambda e, h=h, pslot=pslot: e.tensor_scalar(
                                    KR[:, h * 1024 + pslot * 512:
                                       h * 1024 + (pslot + 1) * 512],
                                    KR[:, h * 1024 + pslot * 512:
                                       h * 1024 + (pslot + 1) * 512],
                                    GAM[:, 16:17], None, op0=OP.mult),
                                inc="vc", waits=[tcst])
                        t_pv = None
                        groups = range(2) if w >= 1 else range(1, 2)
                        for g in groups:
                            waits = [tcst, trsc, P.get("pbsim_users"),
                                     t_vr_last]
                            def mmsim(e, g=g, h=h, w=w):
                                ins = None
                                for j in range(4):
                                    kt = 4 * g + j
                                    wk = w - 1 if kt < 4 else w
                                    ks2 = wk % 2
                                    ins = e.matmul(
                                        pb[:, j * 512:(j + 1) * 512],
                                        KR[:, h * 1024 + ks2 * 512 +
                                           (kt % 4) * 128:
                                           h * 1024 + ks2 * 512 +
                                           (kt % 4 + 1) * 128],
                                        QS[:, :], start=True, stop=True)
                                return ins
                            tsm = b.op("tensor", mmsim, inc="tn", waits=waits)
                            tmk = b.op("vector", lambda e, g=g:
                                e.scalar_tensor_tensor(
                                    pb[:, :], pb[:, :], 1.0,
                                    FMK[:, g * 2048:(g + 1) * 2048],
                                    op0=OP.mult, op1=OP.add),
                                inc="vc", waits=[tsm, t_fmk])
                            tex = b.op("scalar", lambda e: e.activation(
                                PT4[:, :], pb[:, :], AF.Exp,
                                bias=NC0F[:, 0:1]), inc="ac",
                                waits=[tmk, t_nc0])
                            P["pbsim_users"] = tex
                            def mmpv(e, g=g, h=h, w=w):
                                ins = None
                                for j in range(4):
                                    kt = 4 * g + j
                                    gslot = (w * 4 - 4 + kt) % 8
                                    first = (kt == 0) or (w == 0 and kt == 4)
                                    ins = e.matmul(
                                        pv[:, :],
                                        VR[:, gslot * 520 + h * 65:
                                           gslot * 520 + (h + 1) * 65],
                                        PT4[:, j * 512:(j + 1) * 512],
                                        start=first, stop=(kt == 7))
                                return ins
                            t_pv = b.op("tensor", mmpv, inc="tn",
                                        waits=[tex])
                        tre = b.op("vector", lambda e: e.reciprocal(
                            RW[0:1, 1024:1536], pv[64:65, :]),
                            inc="vc", waits=[t_pv])
                        tbc3 = b.op("tensor", lambda e: e.matmul(
                            pm[0:64, 0:512], ONEF[0:1, 0:64],
                            RW[0:1, 1024:1536], start=True, stop=True),
                            inc="tn", waits=[tre, P.get("pm_users")])
                        tcpo = b.op("scalar", lambda e: e.activation(
                            QF[:, 0:512], pv[0:64, :], AF.Copy),
                            inc="ac", waits=[t_pv, tcst])
                        P["pv_users"] = tcpo
                        P["pv_users2"] = tre
                        r0 = (h % 2) * 64
                        ch = h // 2
                        t_oall[h] = b.op("vector",
                            lambda e, r0=r0, ch=ch: e.tensor_tensor(
                                OALL[r0:r0 + 64, ch * 512:(ch + 1) * 512],
                                QF[0:64, 0:512], pm[0:64, 0:512],
                                op=OP.mult),
                            inc="vc", waits=[tbc3, tcpo,
                                             P.get("oall_users")])
                        P["pm_users"] = t_oall[h]
                        P["qs_users"] = t_pv
                        P["qf_users"] = tcpo
                    P["vr_users"] = t_pv
                    P["tbl_users"] = trr
                    P["hs_users"] = tqk
                    t_y1 = {}
                    tao = None
                    for tt in range(4):
                        yw = [t_y1.get(tt - 2) or P.get(f"yt_users{tt % 2}")]
                        if lyr == 1 and w == 0:
                            yw += [P.get("ost0"), P.get("ost1")]
                        t_ytl[tt] = b.dma(
                            "sync", YT[:, (tt % 2) * 1024:(tt % 2 + 1) * 1024],
                            YinF[(w * 4 + tt) * 128:
                                 (w * 4 + tt + 1) * 128, :],
                            f"ytl{tt % 2}", waits=yw)
                        waits = [t_oall[6], t_oall[7], t_wo, t_ytl[tt],
                                 P.get("pbsim_users")]
                        def mmo(e, tt=tt):
                            ins = None
                            for half in range(2):
                                for ch in range(4):
                                    ins = e.matmul(
                                        pb[:, half * 512:(half + 1) * 512],
                                        OALL[:, ch * 512 + tt * 128:
                                             ch * 512 + (tt + 1) * 128],
                                        WO[:, ch * 1024 + half * 512:
                                           ch * 1024 + (half + 1) * 512],
                                        start=(ch == 0), stop=(ch == 3))
                            return ins
                        tao = b.op("tensor", mmo, inc="tn", waits=waits)
                        t_y1[tt] = b.op("vector", lambda e, tt=tt:
                            e.tensor_add(Y1T[:, tt * 1024:(tt + 1) * 1024],
                                         YT[:, (tt % 2) * 1024:
                                            (tt % 2 + 1) * 1024],
                                         pb[:, 0:1024]),
                            inc="vc", waits=[tao, P.get(f"y2ev{tt}")])
                        P["pbsim_users"] = t_y1[tt]
                        P[f"yt_users{tt % 2}"] = t_y1[tt]
                        tymw = b.dma("gpsimd",
                              io.YMBP[1 + (w * 4 + tt) * 128:
                                      1 + (w * 4 + tt + 1) * 128, :],
                              Y1T[:, tt * 1024:(tt + 1) * 1024],
                              "ymb", waits=[t_y1[tt]])
                    t_ymb_all = tymw
                    P["oall_users"] = tao
                    th2 = None
                    for c in range(8):
                        r0 = (1 + w * 512) if c < 4 else (w * 512)
                        th2 = b.dma("sync", HS2[:, c * 512:(c + 1) * 512],
                                    io.YMBP[r0:r0 + 512,
                                            c * 128:(c + 1) * 128], "h2l",
                                    transpose=True,
                                    waits=[t_ymb_all, P.get("hs2_users")]
                                    if c == 0 else [])
                    tst = None
                    for c in range(8):
                        tqq = b.op("scalar", lambda e, c=c: e.activation(
                            SQ[:, :], HS2[:, c * 512:(c + 1) * 512],
                            AF.Square), inc="ac",
                            waits=[th2, P.get("sq_users")])
                        tst = b.op("tensor", lambda e, c=c: (
                            e.matmul(pv[0:1, 0:512], ONCB[:, :],
                                     HS2[:, c * 512:(c + 1) * 512],
                                     start=(c == 0), stop=(c == 7)),
                            e.matmul(pm[0:1, 0:512], ONCF[:, :], SQ[:, :],
                                     start=(c == 0), stop=(c == 7)))[-1],
                            inc="tn",
                            waits=[tqq, P.get("pv_users"),
                                   P.get("pv_users2"), P.get("pm_users")]
                            if c == 0 else [tqq])
                        P["sq_users"] = tst
                    tr1 = b.op("vector", lambda e: (
                        e.tensor_scalar(RW[0:1, 0:512], pv[0:1, 0:512],
                                        1.0 / D, None, op0=OP.mult),
                        e.tensor_scalar(RW[0:1, 512:1024], pm[0:1, 0:512],
                                        1.0 / D, None, op0=OP.mult))[-1],
                        inc="vc", waits=[tst])
                    tr2 = b.op("vector", lambda e: (
                        e.tensor_mul(RW[0:1, 1024:1536], RW[0:1, 0:512],
                                     RW[0:1, 0:512]),
                        e.tensor_sub(RW[0:1, 512:1024], RW[0:1, 512:1024],
                                     RW[0:1, 1024:1536]))[-1],
                        inc="vc", waits=[tr1])
                    tr3 = b.op("scalar", lambda e: e.activation(
                        RW[0:1, 512:1024], RW[0:1, 512:1024], AF.Ln,
                        bias=EPSR[0:1, 0:1]), inc="ac", waits=[tr2])
                    tr4 = b.op("scalar", lambda e: e.activation(
                        RW[0:1, 512:1024], RW[0:1, 512:1024], AF.Exp,
                        scale=-0.5), inc="ac", waits=[tr3])
                    tbc4 = b.op("tensor", lambda e: (
                        e.matmul(pm[:, 0:512], ONEF[:, :], RW[0:1, 0:512],
                                 start=True, stop=True),
                        e.matmul(pb[:, 0:512], ONEF[:, :], RW[0:1, 512:1024],
                                 start=True, stop=True))[-1],
                        inc="tn", waits=[tr4, tr1, P.get("pbsim_users"),
                                         P.get("pm_users")])
                    t_xln = {}
                    for c in range(8):
                        t_xln[c] = b.op("vector", lambda e, c=c: (
                            e.tensor_sub(SQ[:, :],
                                         HS2[:, c * 512:(c + 1) * 512],
                                         pm[:, 0:512]),
                            e.tensor_mul(SQ[:, :], SQ[:, :], pb[:, 0:512]),
                            e.tensor_scalar(XLN[:, c * 512:(c + 1) * 512],
                                            SQ[:, :], LGC[:, c:c + 1],
                                            LGC[:, 8 + c:8 + c + 1],
                                            op0=OP.mult, op1=OP.add))[-1],
                            inc="vc", waits=[tbc4, tbc, P.get("sq_users"),
                                             P.get("xln_users")])
                        P["sq_users"] = t_xln[c]
                    P["pm_users"] = t_xln[7]
                    P["pbsim_users"] = t_xln[7]
                    P["hs2_users"] = t_xln[7]
                    P["pv_users"] = tr1
                    P["pv_users2"] = None
                    t_h3 = {}
                    for fc in range(22):
                        tw1 = load_w1(fc, [P.get(f"w1u{fc % 2}")])
                        wslc = HIDC[fc]
                        sl2 = fc % 2
                        q0 = (fc % 2) * 1024
                        waits = [t_xln[7], tw1, t_h3.get(fc - 2)]
                        if fc < 2:
                            waits += [P.get("pby_users")]
                        def mm1(e, fc=fc, sl2=sl2, wslc=wslc, q0=q0):
                            ins = None
                            for half in range(2):
                                pq = pb[0:wslc, q0 + half * 512:
                                        q0 + half * 512 + 512]
                                for c in range(8):
                                    col = sl2 * 2048 + c * 256 + half * 128
                                    ins = e.matmul(
                                        pq,
                                        W1P[:, col:col + wslc],
                                        XLN[:, c * 512:(c + 1) * 512],
                                        start=(c == 0), stop=(c == 7))
                            return ins
                        tm1 = b.op("tensor", mm1, inc="tn", waits=waits)
                        P[f"w1u{fc % 2}"] = tm1
                        tgl = b.op("scalar", lambda e, wslc=wslc, fc=fc, q0=q0:
                            e.activation(GELB[0:wslc, :],
                                         pb[0:wslc, q0 + 512:q0 + 1024],
                                         AF.Gelu,
                                         bias=B1C[0:wslc, 22 + fc:23 + fc]),
                            inc="ac", waits=[tm1, tbc, P.get("gel_users")])
                        t_h3[fc] = b.op("vector",
                            lambda e, fc=fc, wslc=wslc, q0=q0:
                            e.scalar_tensor_tensor(
                                H3[0:wslc, fc * 512:(fc + 1) * 512],
                                pb[0:wslc, q0:q0 + 512],
                                B1C[0:wslc, fc:fc + 1],
                                GELB[0:wslc, :], op0=OP.add, op1=OP.mult),
                            inc="vc", waits=[tgl, P.get("h3_users")])
                        P["gel_users"] = t_h3[fc]
                    P["xln_users"] = tm1
                    P["pby_users"] = t_h3[21]
                    tm2 = None
                    for tp in range(2):
                        for fc in range(22):
                            tw2 = load_w2(fc, [P.get(f"w2u{fc % 2}")])
                            waits = [t_h3.get(fc, t_h3[21]), tw2]
                            if fc == 0:
                                waits += [t_b2, t_ones, P.get("pby_users"),
                                          P.get("pbsim_users"), t_h3[21],
                                          t_h3[20]]
                            def mm2s(e, tp=tp, fc=fc):
                                ins = None
                                for tloc in range(2):
                                    tt = tp * 2 + tloc
                                    for half in range(2):
                                        ins = e.matmul(
                                            pb[:, tloc * 1024 + half * 512:
                                               tloc * 1024 +
                                               (half + 1) * 512],
                                            H3[0:HIDC[fc],
                                               fc * 512 + tt * 128:
                                               fc * 512 + (tt + 1) * 128],
                                            W2S[0:HIDC[fc],
                                                (fc % 2) * 1024 + half * 512:
                                                (fc % 2) * 1024 +
                                                (half + 1) * 512],
                                            start=(fc == 0), stop=False)
                                return ins
                            tm2f = b.op("tensor", mm2s, inc="tn", waits=waits)
                            P[f"w2u{fc % 2}"] = tm2f
                        def mm2b(e, tp=tp):
                            ins = None
                            for tloc in range(2):
                                for half in range(2):
                                    ins = e.matmul(
                                        pb[:, tloc * 1024 + half * 512:
                                           tloc * 1024 + (half + 1) * 512],
                                        ONEB[:, :],
                                        B2R[0:1, half * 512:(half + 1) * 512],
                                        start=False, stop=True)
                            return ins
                        tm2 = b.op("tensor", mm2b, inc="tn", waits=[])
                        for tloc in range(2):
                            tt = tp * 2 + tloc
                            ti = w * 4 + tt
                            ty2 = b.op("vector", lambda e, tt=tt, tloc=tloc:
                                e.tensor_add(
                                    O1[:, (tt % 2) * 1024:
                                       (tt % 2 + 1) * 1024],
                                    Y1T[:, tt * 1024:(tt + 1) * 1024],
                                    pb[:, tloc * 1024:(tloc + 1) * 1024]),
                                inc="vc", waits=[tm2, P.get(f"o1{tt % 2}")])
                            P["pby_users"] = ty2
                            P[f"y2ev{tt}"] = ty2
                            if lyr == 0:
                                b.dma("gpsimd",
                                      io.Y1F[ti * 128:(ti + 1) * 128, :],
                                      O1[:, (tt % 2) * 1024:
                                         (tt % 2 + 1) * 1024],
                                      f"o{tt % 2}", waits=[ty2])
                                ts2 = b.dma("gpsimd",
                                            io.Y1BP[1 + ti * 128:
                                                    1 + (ti + 1) * 128, :],
                                            O1[:, (tt % 2) * 1024:
                                               (tt % 2 + 1) * 1024],
                                            f"o{tt % 2}")
                                P[f"o1{tt % 2}"] = ts2
                                P[f"ost{tt % 2}"] = ts2
                            else:
                                ks = tt % 2
                                tlx = b.dma("sync", XT[:, :],
                                            io.x[ti * 128:(ti + 1) * 128, :],
                                            "cx", waits=[P.get("cx")])
                                tla = b.dma("sync", A0T[:, :],
                                            io.ATT0[ti * 128:
                                                    (ti + 1) * 128, :],
                                            "cx")
                                tx1 = b.op("vector", lambda e, ks=ks: (
                                    e.tensor_add(
                                        O1[:, ks * 1024:(ks + 1) * 1024],
                                        O1[:, ks * 1024:(ks + 1) * 1024],
                                        XT[:, :]),
                                    e.tensor_add(
                                        O1[:, ks * 1024:(ks + 1) * 1024],
                                        O1[:, ks * 1024:(ks + 1) * 1024],
                                        A0T[:, :]))[-1],
                                    inc="vc", waits=[tla, tlx, ty2])
                                b.dma("gpsimd",
                                      io.X1[ti * 128:(ti + 1) * 128, :],
                                      O1[:, ks * 1024:(ks + 1) * 1024],
                                      f"o{ks}", waits=[tx1])

                                def ln_pass(src_ap, dst_ap, base_tok):
                                    tna = b.op("vector",
                                        lambda e: e.tensor_reduce(
                                            ST[:, 0:1], src_ap, AX, OP.add,
                                            negate=True), inc="vc",
                                        waits=[base_tok])
                                    tnb = b.op("vector",
                                        lambda e: e.tensor_scalar(
                                            ST[:, 1:2], ST[:, 0:1],
                                            1.0 / D, None, op0=OP.mult),
                                        inc="vc", waits=[tna])
                                    tnc = b.op("vector",
                                        lambda e: e.tensor_scalar(
                                            dst_ap, src_ap, ST[:, 1:2],
                                            None, op0=OP.add),
                                        inc="vc", waits=[tnb])
                                    dl, dr = dst_ap[:, 0:512], dst_ap[:, 512:1024]
                                    tnd = b.op("scalar",
                                        lambda e, dl=dl, dr=dr: (
                                            e.activation(GELB[:, :], dl,
                                                         AF.Square),
                                            e.activation(SQ[:, :], dr,
                                                         AF.Square))[-1],
                                        inc="ac",
                                        waits=[tnc, P.get("xh2_users")])
                                    tne0 = b.op("vector",
                                        lambda e: (
                                            e.tensor_reduce(ST[:, 2:3],
                                                            GELB[:, :], AX,
                                                            OP.add),
                                            e.tensor_reduce(ST[:, 4:5],
                                                            SQ[:, :], AX,
                                                            OP.add))[-1],
                                        inc="vc", waits=[tnd])
                                    tne = b.op("vector",
                                        lambda e: e.tensor_add(
                                            ST[:, 2:3], ST[:, 2:3],
                                            ST[:, 4:5]),
                                        inc="vc", waits=[tne0])
                                    P["xh2_users"] = tne
                                    tnf = b.op("scalar",
                                        lambda e: e.activation(
                                            ST[:, 3:4], ST[:, 2:3], AF.Ln,
                                            bias=EPSR[:, 0:1],
                                            scale=1.0 / D),
                                        inc="ac", waits=[tne])
                                    tng = b.op("scalar",
                                        lambda e: e.activation(
                                            ST[:, 3:4], ST[:, 3:4], AF.Exp,
                                            scale=-0.5), inc="ac",
                                        waits=[tnf])
                                    return b.op("vector",
                                        lambda e: e.tensor_scalar(
                                            dst_ap, dst_ap, ST[:, 3:4],
                                            None, op0=OP.mult),
                                        inc="vc", waits=[tng])

                                xn1_ap = XT[:, :]
                                t_xn1 = ln_pass(
                                    O1[:, ks * 1024:(ks + 1) * 1024],
                                    xn1_ap, tx1)
                                tsn = b.dma("gpsimd",
                                            io.XN1F[ti * 128:
                                                    (ti + 1) * 128, :],
                                            xn1_ap, f"o{ks}", waits=[t_xn1])
                                b.dma("gpsimd",
                                      io.XN1BP[1 + ti * 128:
                                               1 + (ti + 1) * 128, :],
                                      xn1_ap, f"o{ks}")
                                xh_ap = A0T[:, :]
                                t_xh = ln_pass(xn1_ap, xh_ap, t_xn1)
                                tsh = b.dma("gpsimd",
                                            io.XHATB[ti * 128:
                                                     (ti + 1) * 128, :],
                                            xh_ap, f"o{ks}", waits=[t_xh])
                                P[f"o1{ks}"] = tsn
                                P["cx"] = t_xh
                                P[f"ost{ks}"] = tsh
                    P["h3_users"] = tm2
                if DEBUG_A0 and lyr == 0:
                    td = b.dma("gpsimd", io.DBG3[:, 0:4096],
                               H3[:, 0:4096], "dbg", waits=[P.get("h3_users")])
                    td = b.dma("gpsimd", io.DBG1[0:64, :],
                               H3[:, 4096:6144], "dbg")
                    td = b.dma("gpsimd", io.DBG2[0:64, :],
                               XLN[:, 0:2048], "dbg")
                    td = b.dma("gpsimd", io.DBG0[:, 0:2048], OALL[:, :], "dbg",
                               waits=[P.get("oall_users")])
                    td = b.dma("gpsimd", io.DBG0[0:64, 2048:2560], QS[:, :],
                               "dbg", waits=[P.get("qs_users")])
                    td = b.dma("gpsimd", io.DBG0[0:64, 4096:5120],
                               KR[:, 7 * 1024:8 * 1024], "dbg")
                    td = b.dma("gpsimd", io.DBG0[:, 6144:8192],
                               VR[:, 0:2048], "dbg")
                    for kk in ("h3_users", "xln_users", "oall_users",
                               "gel_users", "hs2_users", "sq_users"):
                        P[kk] = td
                    b.barrier_wait("gpsimd", [td])
                P["wfree"] = tm2
            b.barrier_wait("gpsimd", [P.get("ost0"), P.get("ost1")])
            b.finish(block)


PHASE_FN["l0_ffn"] = ph_l0_ffn



GROUPS = [["wprep", "ln0", "l0_attn", "l0_ffn", "l1_attn", "moe", "comb"]]


def kernel(**inputs):
    pool, _ = kernel_run(inputs, GROUPS, core_ids=list(range(8)), fused=True)
    out = np.stack([pool[ci]["OUT"] for ci in range(8)], 0)
    return out.astype(np.float32)
